# version 05
# speedup vs baseline: 1.6578x; 1.1172x over previous
from contextlib import ExitStack
import numpy as np
import concourse.bass as bass
import concourse.mybir as mybir
from concourse.bass_utils import run_bass_kernel_spmd

ENGS = ("sync", "tensor", "scalar", "vector", "gpsimd")


class Ev:
    __slots__ = ("sem", "val", "eng", "used", "dma", "slot")

    def __init__(self, sem, val, eng, dma):
        self.sem = sem
        self.val = val
        self.eng = eng
        self.used = False
        self.dma = dma
        self.slot = None


class _Op:
    __slots__ = ("fn", "waits", "ev", "dma")

    def __init__(self, fn, waits, ev, dma):
        self.fn = fn
        self.waits = waits
        self.ev = ev
        self.dma = dma


class Prog:
    def __init__(self, nc):
        self.nc = nc
        self.q = {e: [] for e in ENGS}
        self.epoch = 0
        self._cur = {}
        self.n_sem_used = 0

    def _engine_sem(self, eng, dma):
        key = (eng, dma, self.epoch)
        if key not in self._cur:
            tag = f"s{dma[1]}" if isinstance(dma, tuple) else ("d" if dma else "")
            name = f"s_{eng}{tag}_{self.epoch}"
            h = self.nc.alloc_semaphore(name)
            self._cur[key] = [h, 0]
            self.n_sem_used += 1
        return self._cur[key]

    def new_epoch(self):
        self.epoch += 1

    def barrier(self):
        evs = []
        for eng in ENGS:
            if self.q[eng]:
                evs.append(self.q[eng][-1].ev)
                evs[-1].used = True
        return evs

    def do(self, eng, fn, after=(), dma=False, slot=None):
        if dma and slot is not None:
            s = self._engine_sem(eng, ("slot", slot))
        else:
            s = self._engine_sem(eng, dma)
        ev = Ev(s[0], 0, eng, dma)
        ev.slot = slot
        waits = []
        for a in after:
            if a is None:
                continue
            waits.append(a)
            a.used = True
        self.q[eng].append(_Op(fn, waits, ev, dma))
        return ev

    def dma(self, eng, out, in_, after=(), slot=None, **kw):
        return self.do(eng, lambda e: e.dma_start(out=out, in_=in_, **kw),
                       after=after, dma=True, slot=slot)

    def finalize(self):
        totals = {}
        for eng in ENGS:
            for op in self.q[eng]:
                if op.dma:
                    totals[op.ev.sem] = totals.get(op.ev.sem, 0) + 16
                    op.ev.val = totals[op.ev.sem]
                elif op.ev.used:
                    totals[op.ev.sem] = totals.get(op.ev.sem, 0) + 1
                    op.ev.val = totals[op.ev.sem]
        watermark = {e: {} for e in ENGS}
        nc = self.nc
        with nc.Block() as blk:
            def emit_engine(eng):
                def body(engine):
                    wm = watermark[eng]
                    for op in self.q[eng]:
                        for a in op.waits:
                            if a.eng == eng and a.dma and a.slot is None:
                                continue
                            if wm.get(a.sem, -1) >= a.val:
                                continue
                            engine.wait_ge(a.sem, a.val)
                            wm[a.sem] = a.val
                        instr = op.fn(engine)
                        if op.dma:
                            instr.then_inc(op.ev.sem, 16)
                            if op.ev.used and op.ev.slot is None:
                                engine.wait_ge(op.ev.sem, op.ev.val)
                                wm[op.ev.sem] = op.ev.val
                        elif op.ev.used:
                            if instr is None:
                                instr = engine.nop()
                            instr.then_inc(op.ev.sem, 1)
                return body
            for eng in ENGS:
                if self.q[eng]:
                    getattr(blk, eng)(emit_engine(eng))
        self.q = {e: [] for e in ENGS}


from concourse.bass_core_rust import IndirectOffsetOnAxis

F32 = mybir.dt.float32
AF = mybir.ActivationFunctionType
ALU = mybir.AluOpType
AX = mybir.AxisListType

T, D, DI, DS, NH, HD, E, FF = 2048, 1024, 2048, 64, 16, 64, 8, 4096
NTT, NKD, NKI, NFF = T // 128, D // 128, DI // 128, FF // 128
LNEPS = 1e-5


class Chain:
    def __init__(self, P, start=()):
        self.P = P
        self.last = [e for e in start if e is not None]

    def do(self, eng, fn, extra=(), dma=False):
        ev = self.P.do(eng, fn, after=self.last + [e for e in extra if e is not None], dma=dma)
        self.last = [ev]
        return ev

    def dma(self, eng, out, in_, extra=()):
        return self.do(eng, lambda e: e.dma_start(out=out, in_=in_), extra=extra, dma=True)


class L1:
    def __init__(self, upto=99, dbg=(), sub=9, moe_mode="sorted"):
        self.nc = bass.Bass("TRN2", target_bir_lowering=False)
        self.P = Prog(self.nc)
        self.outs = []
        self.upto = upto
        self.sub = sub
        self.moe_mode = moe_mode
        self.dbgnames = dbg
        self.build()

    def din(self, name, shape):
        return self.nc.dram_tensor(name, shape, F32, kind="ExternalInput").ap()

    def dout(self, name, shape):
        return self.nc.dram_tensor(name, shape, F32, kind="ExternalOutput").ap()

    def dint(self, name, shape):
        return self.nc.dram_tensor(name, shape, F32, kind="Internal").ap()

    def phase_end(self):
        b = self.P.barrier()
        self.P.new_epoch()
        return b

    def odma(self, out, in_, after=()):
        ev = self.P.dma("gpsimd", out, in_, after=after)
        self.outs.append(ev)
        return ev

    def dbg(self, name, out_slice_fn, in_, after=()):
        if name not in self.dbt:
            return None
        ev = self.P.dma("gpsimd", out_slice_fn(self.dbt[name]), in_, after=after)
        self.outs.append(ev)
        return ev

    def build(self):
        nc, P = self.nc, self.P
        d = self.din
        self.xw = d("xw", [T, D])
        self.lng, self.lnb = d("lng", [D]), d("lnb", [D])
        self.w_in = d("w_in", [D, 2 * DI])
        self.conv_w, self.conv_b = d("conv_w", [DI, 4]), d("conv_b", [DI])
        self.w_x = d("w_x", [DI, 2 * DS])
        self.w_dt, self.b_dt = d("w_dt", [DI, DI]), d("b_dt", [DI])
        self.a_log, self.dvec = d("a_log", [DS]), d("dvec", [DI])
        self.w_out = d("w_out", [DI, D])
        self.wq, self.wk, self.wv, self.wo = (d(n, [D, D]) for n in ("wq", "wk", "wv", "wo"))
        self.bq, self.bk, self.bv, self.bo = (d(n, [D]) for n in ("bq", "bk", "bv", "bo"))
        self.l1g, self.l1b, self.l2g, self.l2b = (d(n, [D]) for n in ("l1g", "l1b", "l2g", "l2b"))
        self.wr = d("wr", [D, E])
        self.w1 = d("w1", [E, D, FF])
        self.b1 = d("b1", [E, FF])
        self.w2 = d("w2", [E, FF, D])

        self.preout = self.dout("preout", [T, D])
        if self.moe_mode == "dense":
            self.y1o = self.dout("y1o", [T, D])
            self.y2o = self.dout("y2o", [T, D])
        else:
            self.y12o = self.dout("y12o", [2 * (T + 16) + 1, D])
        self.rowso = self.dout("rowso", [4, T])
        self.oh1o = self.dout("oh1o", [E, T])
        self.oh2o = self.dout("oh2o", [E, T])
        self.cnto = self.dout("cnto", [1, 2 * E])

        self.xn_d = self.dint("xn_d", [T, D])
        self.pre0_d = self.dint("pre0_d", [T, D])
        self.res_d = self.dint("res_d", [NKI, 128, T])
        self.xc_d = self.dint("xc_d", [NKI, 128, T])
        self.mam_d = self.dint("mam_d", [T, D])
        self.h1f_d = self.dint("h1f_d", [NKD, 128, T])
        self.attnf_d = self.dint("attnf_d", [NKD, 128, T])
        self.he_d = self.dint("he_d", [E, NFF, 128, T]) if self.moe_mode == "dense" else None
        self.h2tm_d = self.dint("h2tm_d", [T + 1, D])
        self.srctok_d = self.dint("srctok_d", [1, 5248])
        self.dsttok_d = self.dint("dsttok_d", [1, 5248])

        shp = {"xn": [T, D], "pre0": [T, D], "xmf": [NKD, 128, T], "h1f": [NKD, 128, T],
               "xc": [NKI, 128, T], "res": [NKI, 128, T], "bc": [128, T],
               "delta": [1, T], "ad": [DS, T], "ys": [1, T], "mam": [T, D],
               "attnf": [NKD, 128, T], "x2": [T, D], "h2f": [NKD, 128, T],
               "lgf": [E, T], "he0": [NFF, 128, T], "cum1": [E, T], "cum2": [E, T],
               "rws": [4, T]}
        self.dbt = {n: self.dout("dbg_" + n, shp[n]) for n in self.dbgnames}

        A = nc.alloc_sbuf_tensor
        self.ident = A("ident", [128, 128], F32)
        self.iota_r = A("iota_r", [128, 128], F32)
        self.iota_c = A("iota_c", [128, 1], F32)
        self.ones_row = A("ones_row", [1, 128], F32)
        self.ones_col = A("ones_col", [128, 1], F32)
        self.negmean_col = A("negmean_col", [128, 1], F32)
        self.vrow = A("vrow", [1, D], F32)
        for n in ("s_sum", "s_ssq", "s_mu", "s_var", "s_rstd", "s_msq"):
            setattr(self, n, A(n, [128, 1], F32))
        self.cwc = A("cwc", [128, 64], F32)
        self.cbc = A("cbc", [128, 16], F32)
        self.nbdt = A("nbdt", [128, 16], F32)
        self.dcol = A("dcol", [128, 16], F32)
        self.aneg = A("aneg", [1, DS], F32)
        self.bqc = A("bqc", [128, 8], F32)
        self.bkc = A("bkc", [128, 8], F32)
        self.bv_row = A("bv_row", [1, D], F32)
        self.wrc = A("wrc", [128, NKD * E], F32)
        self.b1c = A("b1c", [128, E * NFF], F32)
        self.epsc = A("epsc", [128, 1], F32)
        self.stg = A("stg", [128, 2048], F32)
        self.stg2 = A("stg2", [128, 2048], F32)
        self.oh1tm = A("oh1tm", [128, 128], F32)
        self.eidx_i = A("eidx_i", [128, 40], mybir.dt.int32)
        self.eidx_fs = A("eidx_fs", [128, 40], F32)
        self.basew1 = A("basew1", [128, 8], F32)
        self.basew2 = A("basew2", [128, 32], F32)
        self.oh2tm = A("oh2tm", [128, 128], F32)

        self.psA = nc.alloc_psum_tensor("psA", [128, 2048], F32)
        self.psB = nc.alloc_psum_tensor("psB", [128, 2048], F32)

        self.ph_init()
        with nc.sbuf_tensor("xmf", [128, NKD * T], F32) as xmf:
            self.xmf3 = xmf.ap().rearrange("p (c t) -> p c t", c=NKD)
            self.ph_p0()
            if self.upto >= 1:
                self.ph_m1()
        if self.upto >= 1 and self.sub >= 2:
            with nc.sbuf_tensor("xca", [128, NKI * 2048], F32) as xca, \
                 nc.sbuf_tensor("ysr", [128, T], F32) as ysr:
                self.xca3 = xca.ap().rearrange("p (c t) -> p c t", c=NKI)
                self.ysr = ysr
                self.ph_m34()
            self.bar = self.phase_end()
        if self.upto >= 2:
            self.ph_attn()
        if self.upto >= 3:
            with nc.sbuf_tensor("h2a", [128, NKD * T], F32) as h2a:
                self.h2a = h2a
                self.h2a3 = h2a.ap().rearrange("p (c t) -> p c t", c=NKD)
                self.ph_a3()
                if self.upto >= 4:
                    self.ph_routing()
                if self.moe_mode == "dense" and self.upto >= 5:
                    self.ph_moe1()
            if self.upto >= 5 and self.moe_mode == "sorted":
                self.ph_moes()
            if self.moe_mode == "dense" and self.upto >= 6:
                self.ph_moe2()
        P.do("sync", lambda e: e.nop(), after=self.outs)
        P.finalize()

    def ph_init(self):
        P, nc = self.P, self.nc
        ev_ir = P.do("gpsimd", lambda e: e.iota(self.iota_r.ap(), pattern=[[1, 128]], base=0,
                                                channel_multiplier=0, allow_small_or_imprecise_dtypes=True))
        ev_ic = P.do("gpsimd", lambda e: e.iota(self.iota_c.ap(), pattern=[[0, 1]], base=0,
                                                channel_multiplier=1, allow_small_or_imprecise_dtypes=True))
        ev_m1 = P.do("vector", lambda e: e.memset(self.ones_row[:1, :], 1.0))
        ev_m2 = P.do("vector", lambda e: e.memset(self.ones_col[:, :], 1.0))
        ev_m3 = P.do("vector", lambda e: e.memset(self.negmean_col[:, :], -1.0 / DI))
        ev_m4 = P.do("vector", lambda e: e.memset(self.epsc[:, :], LNEPS))
        self.ev_ident = P.do("vector", lambda e: e.tensor_single_scalar(
            self.ident[:, :], self.iota_r[:, :], self.iota_c[:, :1], ALU.is_equal),
            after=[ev_ir, ev_ic])
        self.ev_const = [ev_m1, ev_m2, ev_m3, ev_m4, self.ev_ident]

        ld = lambda out, in_, after=(): P.dma("sync", out, in_, after=after,
                                              allow_slow_non_contiguous=True)
        e_nb = ld(self.nbdt.ap().rearrange("p (c o) -> p c o", o=1),
                  self.b_dt.rearrange("(c p o) -> p c o", p=128, o=1))
        e_an = ld(self.aneg[:1, :], self.a_log.rearrange("(o d) -> o d", o=1))
        ld(self.cwc.ap().rearrange("p (c k) -> p c k", k=4),
           self.conv_w.rearrange("(c p) k -> p c k", p=128))
        ld(self.cbc.ap().rearrange("p (c o) -> p c o", o=1), self.conv_b.rearrange("(c p o) -> p c o", p=128, o=1))
        ld(self.dcol.ap().rearrange("p (c o) -> p c o", o=1), self.dvec.rearrange("(c p o) -> p c o", p=128, o=1))
        ld(self.bqc.ap().rearrange("p (c o) -> p c o", o=1), self.bq.rearrange("(c p o) -> p c o", p=128, o=1))
        ld(self.bkc.ap().rearrange("p (c o) -> p c o", o=1), self.bk.rearrange("(c p o) -> p c o", p=128, o=1))
        ld(self.bv_row[:1, :], self.bv.rearrange("(o d) -> o d", o=1))
        ld(self.wrc.ap().rearrange("p (c e) -> p c e", e=E),
           self.wr.rearrange("(c p) e -> p c e", p=128))
        ld(self.b1c.ap().rearrange("p (e m) -> p e m", e=E),
           self.b1.rearrange("e (m p) -> p e m", p=128))
        self.ev_small = P.barrier()
        self.ev_nbdt = P.do("vector", lambda e: e.tensor_scalar_mul(self.nbdt[:, :], self.nbdt[:, :], -1.0),
                            after=[e_nb])
        e1 = P.do("scalar", lambda e: e.activation(out=self.aneg[:1, :], in_=self.aneg[:1, :], func=AF.Exp),
                  after=[e_an])
        self.ev_aneg = P.do("scalar", lambda e: e.mul(self.aneg[:1, :], self.aneg[:1, :], -1.0), after=[e1])

    def replicate(self, dst, src_dram, ncols, after=()):
        P = self.P
        c = Chain(P, start=list(after) + self.ev_const)
        c.dma("sync", self.vrow[:1, 0:ncols], src_dram.rearrange("(o d) -> o d", o=1))

        def mm(e):
            r = None
            for s0 in range(0, ncols, 512):
                r = e.matmul(self.psB[:, s0:min(s0 + 512, ncols)], self.ones_row[:1, 0:128],
                             self.vrow[:1, s0:min(s0 + 512, ncols)], start=True, stop=True)
            return r
        c.do("tensor", mm)
        c.do("scalar", lambda e: e.copy(dst[:, 0:ncols], self.psB[:, 0:ncols]))
        return c.last[0]

    def ph_p0(self):
        P, nc = self.P, self.nc
        xmf3 = self.xmf3
        with nc.sbuf_tensor("repA", [128, D], F32) as repA, \
             nc.sbuf_tensor("repB", [128, D], F32) as repB, \
             nc.sbuf_tensor("repC", [128, D], F32) as repC, \
             nc.sbuf_tensor("repD", [128, D], F32) as repD, \
             nc.sbuf_tensor("xt0", [128, D], F32) as xt0, \
             nc.sbuf_tensor("xt1", [128, D], F32) as xt1, \
             nc.sbuf_tensor("t2t", [128, D], F32) as t2t, \
             nc.sbuf_tensor("t3t", [128, D], F32) as t3t, \
             nc.sbuf_tensor("xmt", [128, D], F32) as xmt, \
             nc.sbuf_tensor("h1t", [128, D], F32) as h1t, \
             nc.sbuf_tensor("sqd", [128, D], F32) as sqd:
            rA = self.replicate(repA, self.lng, D)
            rB = self.replicate(repB, self.lnb, D, after=[rA])
            rC = self.replicate(repC, self.l1g, D, after=[rB])
            rD = self.replicate(repD, self.l1b, D, after=[rC])
            c = Chain(P, start=[rD] + self.ev_const)
            h1f_pct = self.h1f_d.rearrange("c p t -> p c t")
            for tt in range(NTT):
                xb = xt0 if tt % 2 == 0 else xt1
                tile = slice(tt * 128, (tt + 1) * 128)
                e_x = P.dma("sync", xb[:, :], self.xw[tile, :],
                            after=c.last if tt >= 2 else [])
                S, V = "scalar", "vector"
                c.do(S, lambda e, xb=xb: e.activation(out=sqd[:, :], in_=xb[:, :], func=AF.Square,
                                                      accum_out=self.s_ssq[:, :1]), extra=[e_x])
                c.do(V, lambda e, xb=xb: e.reduce_sum(self.s_sum[:, :1], xb[:, :], axis=AX.X))
                c.do(V, lambda e: e.tensor_scalar_mul(self.s_mu[:, :1], self.s_sum[:, :1], 1.0 / D))
                c.do(V, lambda e: e.tensor_mul(self.s_msq[:, :1], self.s_mu[:, :1], self.s_mu[:, :1]))
                c.do(V, lambda e: e.scalar_tensor_tensor(self.s_var[:, :1], self.s_ssq[:, :1], 1.0 / D,
                                                         self.s_msq[:, :1], ALU.mult, ALU.subtract))
                c.do(S, lambda e: e.activation(out=self.s_var[:, :1], in_=self.s_var[:, :1],
                                               func=AF.Sqrt, bias=self.epsc[:, :1]))
                c.do(V, lambda e: e.reciprocal(self.s_rstd[:, :1], self.s_var[:, :1]))
                c.do(V, lambda e, xb=xb: e.tensor_scalar(t2t[:, :], xb[:, :], self.s_mu[:, :1],
                                                         self.s_rstd[:, :1], ALU.subtract, ALU.mult))
                c.do(V, lambda e, xb=xb: e.scalar_tensor_tensor(t3t[:, :], t2t[:, :], 2.0, xb[:, :],
                                                                ALU.mult, ALU.add))
                e_xnsp = P.dma("gpsimd", self.xn_d[tile, :], t2t[:, :], after=c.last)
                e_p0sp = P.dma("gpsimd", self.pre0_d[tile, :], t3t[:, :], after=c.last)
                self.dbg("xn", lambda t: t[tile, :], t2t[:, :], after=c.last)
                self.dbg("pre0", lambda t: t[tile, :], t3t[:, :], after=c.last)
                c.do(S, lambda e: e.activation(out=sqd[:, :], in_=t2t[:, :], func=AF.Square,
                                               accum_out=self.s_ssq[:, :1]))
                c.do(V, lambda e: e.reduce_sum(self.s_sum[:, :1], t2t[:, :], axis=AX.X))
                c.do(V, lambda e: e.tensor_scalar_mul(self.s_mu[:, :1], self.s_sum[:, :1], 1.0 / D))
                c.do(V, lambda e: e.tensor_mul(self.s_msq[:, :1], self.s_mu[:, :1], self.s_mu[:, :1]))
                c.do(V, lambda e: e.scalar_tensor_tensor(self.s_var[:, :1], self.s_ssq[:, :1], 1.0 / D,
                                                         self.s_msq[:, :1], ALU.mult, ALU.subtract))
                c.do(S, lambda e: e.activation(out=self.s_var[:, :1], in_=self.s_var[:, :1],
                                               func=AF.Sqrt, bias=self.epsc[:, :1]))
                c.do(V, lambda e: e.reciprocal(self.s_rstd[:, :1], self.s_var[:, :1]))
                c.do(V, lambda e: e.tensor_scalar(xmt[:, :], t2t[:, :], self.s_mu[:, :1],
                                                  self.s_rstd[:, :1], ALU.subtract, ALU.mult),
                     extra=[e_xnsp])
                c.do(V, lambda e: e.tensor_mul(h1t[:, :], xmt[:, :], repC[:, :]))
                c.do(V, lambda e: e.tensor_add(h1t[:, :], h1t[:, :], repD[:, :]))
                c.do(V, lambda e: e.tensor_mul(xmt[:, :], xmt[:, :], repA[:, :]))
                c.do(V, lambda e: e.tensor_add(xmt[:, :], xmt[:, :], repB[:, :]))

                def tps(src, ps):
                    def f(e):
                        r = None
                        for cc in range(NKD):
                            r = e.transpose(ps[:, cc * 128:(cc + 1) * 128],
                                            src[:, cc * 128:(cc + 1) * 128], self.ident[:, :])
                        return r
                    return f
                c.do("tensor", tps(xmt, self.psB))
                c.do(S, lambda e, tt=tt: e.copy(xmf3[:, :, tt * 128:(tt + 1) * 128], self.psB[:, 0:D]))
                self.dbg("xmf", lambda t, tt=tt: t.rearrange("c p t -> p c t")[:, :, tt * 128:(tt + 1) * 128],
                         xmf3[:, :, tt * 128:(tt + 1) * 128], after=c.last)
                c.do("tensor", tps(h1t, self.psA))
                c.do(S, lambda e: e.copy(self.stg2[:, 0:D], self.psA[:, 0:D]))
                c.do("gpsimd", lambda e, tt=tt: e.dma_start(
                    out=h1f_pct[:, :, tt * 128:(tt + 1) * 128],
                    in_=self.stg2[:, 0:1024].rearrange("p (c t) -> p c t", c=NKD)), dma=True)
                self.dbg("h1f", lambda t, tt=tt: t.rearrange("c p t -> p c t")[:, :, tt * 128:(tt + 1) * 128],
                         self.stg2[:, 0:1024].rearrange("p (c t) -> p c t", c=NKD), after=c.last)
                if tt == NTT - 1:
                    self.ev_xn_sp, self.ev_p0_sp = e_xnsp, e_p0sp
            self.ev_p0 = c.last[0]
            self.bar = self.phase_end()

    def ph_m1(self):
        P, nc = self.P, self.nc
        xmf3 = self.xmf3
        ev_xc_sp = self.ev_xc_sp = [None] * NKI
        ev_res_sp = self.ev_res_sp = [None] * NKI
        with nc.sbuf_tensor("winb", [128, NKD * 2048], F32) as winb, \
             nc.sbuf_tensor("xpb", [128, 2051], F32) as xpb, \
             nc.sbuf_tensor("cac", [128, 2048], F32) as cac, \
             nc.sbuf_tensor("xcb", [128, 2048], F32) as xcb, \
             nc.sbuf_tensor("resb", [128, 2048], F32) as resb:
            winb3 = winb.ap().rearrange("p (c m) -> p c m", c=NKD)
            e_pad = P.do("vector", lambda e: e.memset(xpb[:, 0:3], 0.0), after=self.bar)
            ev_psfree = {0: None, 1: None}
            e_evac = e_c4 = e_si = e_sp = e_rsi = e_rsp = None
            e_wl = None
            e_mm = None
            for half in range(2):
                e_wl_parts = []
                for cc in range(NKD):
                    e_wl_parts.append(P.dma(
                        "sync", winb3[:, cc, :],
                        self.w_in[cc * 128:(cc + 1) * 128, half * 2048:(half + 1) * 2048],
                        after=(self.bar if half == 0 else [e_wl, e_mm])))
                e_wl = e_wl_parts[-1]
                for ml in range(16):
                    m = half * 16 + ml
                    psX = self.psA if m % 2 == 0 else self.psB

                    def mmf(e, psX=psX, ml=ml):
                        r = None
                        for s in range(4):
                            for cc in range(NKD):
                                r = e.matmul(psX[:, s * 512:(s + 1) * 512],
                                             winb3[:, cc, ml * 128:(ml + 1) * 128],
                                             xmf3[:, cc, s * 512:(s + 1) * 512],
                                             start=(cc == 0), stop=(cc == NKD - 1))
                        return r
                    e_mm = P.do("tensor", mmf,
                                after=e_wl_parts + [ev_psfree[m % 2]] + (self.bar if m == 0 else []))
                    if half == 0:
                        e_evac = P.do("scalar", lambda e, psX=psX: e.copy(xpb[:, 3:2051], psX[:, :]),
                                      after=[e_mm, e_c4, e_pad])
                        e_c1 = P.do("vector", lambda e, m=m: e.tensor_single_scalar(
                            cac[:, :], xpb[:, 0:2048], self.cwc[:, m * 4:m * 4 + 1], ALU.mult),
                            after=[e_evac, e_si] + self.ev_small)
                        e_ck = e_c1
                        for k in (1, 2, 3):
                            e_ck = P.do("vector", lambda e, m=m, k=k: e.scalar_tensor_tensor(
                                cac[:, :], xpb[:, k:2048 + k], self.cwc[:, m * 4 + k:m * 4 + k + 1],
                                cac[:, :], ALU.mult, ALU.add), after=[e_ck])
                        e_c4 = e_ck
                        e_si = P.do("scalar", lambda e, m=m: e.activation(
                            out=xcb[:, :], in_=cac[:, :], func=AF.Silu,
                            bias=self.cbc[:, m:m + 1]), after=[e_c4, e_sp])
                        e_sp = P.dma("gpsimd", self.xc_d[m, :, :], xcb[:, :], after=[e_si])
                        self.dbg("xc", lambda t, m=m: t[m, :, :], xcb[:, :], after=[e_sp])
                        ev_xc_sp[m] = e_sp
                        ev_psfree[m % 2] = e_evac
                    else:
                        e_rsi = P.do("scalar", lambda e, psX=psX: e.activation(
                            out=resb[:, :], in_=psX[:, :], func=AF.Silu), after=[e_mm, e_rsp])
                        e_rsp = P.dma("gpsimd", self.res_d[m - 16, :, :], resb[:, :], after=[e_rsi])
                        self.dbg("res", lambda t, m=m: t[m - 16, :, :], resb[:, :], after=[e_rsp])
                        ev_res_sp[m - 16] = e_rsp
                        ev_psfree[m % 2] = e_rsi
            self.bar = self.phase_end()

    def ph_m34(self):
        P, nc = self.P, self.nc
        bar1 = self.bar
        ev_xc_sp = self.ev_xc_sp
        xca3 = self.xca3
        with nc.sbuf_tensor("wxb", [128, NKI * 128], F32) as wxb, \
             nc.sbuf_tensor("wdt0", [128, 2048], F32) as wdt0, \
             nc.sbuf_tensor("wdt1", [128, 2048], F32) as wdt1, \
             nc.sbuf_tensor("sgb", [128, 2048], F32) as sgb:
            bcsb = self.stg2
            adb = sgb
            hsb = wdt0
            dlr = wdt1
            ysw = wxb
            e_xca = [P.dma("sync", xca3[:, k, :], self.xc_d[k, :, :],
                           after=[ev_xc_sp[k]] + bar1) for k in range(NKI)]
            e_wx = [P.dma("sync", wxb[:, k * 128:(k + 1) * 128],
                          self.w_x[k * 128:(k + 1) * 128, :], after=bar1) for k in range(NKI)]

            def bcmm(e):
                r = None
                for s in range(4):
                    for k in range(NKI):
                        r = e.matmul(self.psA[:, s * 512:(s + 1) * 512], wxb[:, k * 128:(k + 1) * 128],
                                     xca3[:, k, s * 512:(s + 1) * 512],
                                     start=(k == 0), stop=(k == NKI - 1))
                return r
            e_bcmm = P.do("tensor", bcmm, after=e_xca + e_wx + bar1)
            e_bc = P.do("scalar", lambda e: e.copy(bcsb[:, :], self.psA[:, :]), after=[e_bcmm])
            self.dbg("bc", lambda t: t, bcsb[:, :], after=[e_bc])
            wdt_pct = self.w_dt.rearrange("(kc p) m -> p kc m", p=128)
            e_sg = e_ln = None
            e_dacc = None
            e_wdt = {0: None, 1: None}
            e_mm_prev = e_bc
            for m in range(NKI):
                wb = wdt0 if m % 2 == 0 else wdt1
                wb3 = wb.ap().rearrange("p (kc m) -> p kc m", kc=NKI)
                e_l0 = P.dma("sync", wb3[:, 0:8, :], wdt_pct[:, 0:8, m * 128:(m + 1) * 128],
                             after=[e_wdt[m % 2]])
                e_l1 = P.dma("sync", wb3[:, 8:16, :], wdt_pct[:, 8:16, m * 128:(m + 1) * 128],
                             after=[e_wdt[m % 2]])

                def dtmm(e, wb3=wb3):
                    r = None
                    for s in range(4):
                        for k in range(NKI):
                            r = e.matmul(self.psA[:, s * 512:(s + 1) * 512], wb3[:, k, :],
                                         xca3[:, k, s * 512:(s + 1) * 512],
                                         start=(k == 0), stop=(k == NKI - 1))
                    return r
                e_mm = P.do("tensor", dtmm, after=[e_l0, e_l1, e_sg, e_mm_prev])
                e_wdt[m % 2] = e_mm
                e_mm_prev = e_mm
                e_sg = P.do("scalar", lambda e, m=m: e.activation(
                    out=sgb[:, :], in_=self.psA[:, :], func=AF.Sigmoid, scale=-1.0,
                    bias=self.nbdt[:, m:m + 1]), after=[e_mm, e_ln, self.ev_nbdt])
                e_ln = P.do("scalar", lambda e: e.activation(out=self.stg[:, :], in_=sgb[:, :],
                                                             func=AF.Ln), after=[e_sg, e_dacc])

                def dmm(e, m=m):
                    r = None
                    for s in range(4):
                        r = e.matmul(self.psB[0:1, s * 512:(s + 1) * 512], self.negmean_col[:, :1],
                                     self.stg[:, s * 512:(s + 1) * 512],
                                     start=(m == 0), stop=(m == NKI - 1))
                    return r
                e_dacc = P.do("tensor", dmm, after=[e_ln])
            e_dl = P.do("scalar", lambda e: e.copy(dlr[0:1, :], self.psB[0:1, :]), after=[e_dacc, e_mm_prev])
            self.dbg("delta", lambda t: t, dlr[0:1, :], after=[e_dl])

            def admm(e):
                r = None
                for s in range(4):
                    r = e.matmul(self.psA[0:DS, s * 512:(s + 1) * 512], self.aneg[:1, :],
                                 dlr[0:1, s * 512:(s + 1) * 512], start=True, stop=True)
                return r
            e_admm = P.do("tensor", admm, after=[e_dl, e_sg, self.ev_aneg])
            e_ad = P.do("scalar", lambda e: e.activation(out=adb[0:DS, :], in_=self.psA[0:DS, :],
                                                         func=AF.Exp), after=[e_admm, e_ln])
            self.dbg("ad", lambda t: t, adb[0:DS, :], after=[e_ad])
            e_scan = P.do("vector", lambda e: e.tensor_tensor_scan(
                out=hsb[0:DS, :], data0=adb[0:DS, :], data1=bcsb[0:DS, :], initial=0.0,
                op0=ALU.mult, op1=ALU.add), after=[e_ad, e_bc, e_mm_prev])
            e_bcC = P.dma("gpsimd", wxb[0:DS, :], bcsb[DS:128, :], after=[e_bc, e_bcmm])
            e_hc = P.do("vector", lambda e: e.tensor_mul(adb[0:DS, :], hsb[0:DS, :], wxb[0:DS, :]),
                        after=[e_scan, e_ad, e_bcC])

            def ysmm(e):
                r = None
                for s in range(4):
                    r = e.matmul(self.psB[0:1, s * 512:(s + 1) * 512], self.ones_col[0:DS, :1],
                                 adb[0:DS, s * 512:(s + 1) * 512], start=True, stop=True)
                return r
            e_ysmm = P.do("tensor", ysmm, after=[e_hc, e_dl])
            e_ys = P.do("scalar", lambda e: e.copy(ysw[0:1, :], self.psB[0:1, :]), after=[e_ysmm, e_hc])
            self.dbg("ys", lambda t: t, ysw[0:1, :], after=[e_ys])

            def ysrepmm(e):
                r = None
                for s in range(4):
                    r = e.matmul(self.psA[:, s * 512:(s + 1) * 512], self.ones_row[:1, 0:128],
                                 ysw[0:1, s * 512:(s + 1) * 512], start=True, stop=True)
                return r
            e_ysrmm = P.do("tensor", ysrepmm, after=[e_ys, e_ad])
            e_ysr = P.do("scalar", lambda e: e.copy(self.ysr[:, :], self.psA[:, :]), after=[e_ysrmm])
            bar2 = P.barrier()
            self.e_wdt_done = e_mm_prev
            self.e_ysr = e_ysr
        if self.sub >= 3:
            with nc.sbuf_tensor("woutb", [128, 4 * 1024], F32) as woutb, \
                 nc.sbuf_tensor("resb2", [128, 2048], F32) as resb2:
                woutb3 = woutb.ap().rearrange("p (j n) -> p j n", j=4)
                e_y = [None] * NKI
                e_rld = None
                for m in range(NKI):
                    e_rld = P.dma("sync", resb2[:, :], self.res_d[m, :, :],
                                  after=[self.ev_res_sp[m], e_y[m - 1] if m else self.e_wdt_done] + bar2)
                    e_y1 = P.do("vector", lambda e, m=m: e.scalar_tensor_tensor(
                        xca3[:, m, :], xca3[:, m, :], self.dcol[:, m:m + 1], self.ysr[:, :],
                        ALU.mult, ALU.add), after=[self.e_ysr, self.e_wdt_done] + ([e_y[m - 1]] if m else []))
                    e_y[m] = P.do("vector", lambda e, m=m: e.tensor_mul(xca3[:, m, :], xca3[:, m, :],
                                                                       resb2[:, :]), after=[e_y1, e_rld])
                if self.sub >= 4:
                    ps_slot = [(self.psA, 0), (self.psA, 1024), (self.psB, 0), (self.psB, 1024)]
                    e_wld_last = None
                    e_evac = {}
                    prev_mms = []
                    for ttg in range(4):
                        e_mm_grp = {}
                        for g in range(4):
                            e_wld = []
                            for j in range(4):
                                e_wld.append(P.dma("sync", woutb3[:, j, :],
                                                   self.w_out[(g * 4 + j) * 128:(g * 4 + j + 1) * 128, :],
                                                   after=[e_wld_last] + prev_mms))
                            e_wld_last = e_wld[-1]
                            for tl in range(4):
                                tt = ttg * 4 + tl
                                ps, off = ps_slot[tl]

                                def wmm(e, g=g, tl=tl, tt=tt, ps=ps, off=off):
                                    r = None
                                    for j in range(4):
                                        kk = g * 4 + j
                                        for h2 in range(2):
                                            r = e.matmul(ps[:, off + h2 * 512: off + (h2 + 1) * 512],
                                                         xca3[:, kk, tt * 128:(tt + 1) * 128],
                                                         woutb3[:, j, h2 * 512:(h2 + 1) * 512],
                                                         start=(g == 0 and j == 0), stop=(g == 3 and j == 3))
                                    return r
                                e_mm_grp[tl] = P.do("tensor", wmm,
                                                    after=e_wld[-1:] + e_y + [e_evac.get((ttg - 1, tl))])
                            prev_mms = list(e_mm_grp.values())
                        e_prev_sp = e_evac.get((ttg - 1, 3), None)
                        for tl in range(4):
                            tt = ttg * 4 + tl
                            ps, off = ps_slot[tl]
                            e_ev = P.do("scalar", lambda e, ps=ps, off=off: e.copy(self.stg[:, 0:1024],
                                                                                  ps[:, off:off + 1024]),
                                        after=[e_mm_grp[tl], e_prev_sp])
                            e_spm = P.dma("gpsimd", self.mam_d[tt * 128:(tt + 1) * 128, :], self.stg[:, 0:1024],
                                          after=[e_ev])
                            e_dbgm = self.dbg("mam", lambda t, tt=tt: t[tt * 128:(tt + 1) * 128, :],
                                              self.stg[:, 0:1024], after=[e_spm])
                            e_prev_sp = e_dbgm or e_spm
                            e_evac[(ttg, tl)] = e_prev_sp
                    self.ev_mam_sp = list(e_evac.values())
                else:
                    self.ev_mam_sp = [e_y[-1]]
        self.bar = self.phase_end()
        self.ev_xc_done = None

    def ph_attn(self):
        P, nc = self.P, self.nc
        bar = self.bar
        attnf_pct = self.attnf_d.rearrange("c p t -> p c t")
        with nc.sbuf_tensor("h1a", [128, NKD * T], F32) as h1a, \
             nc.sbuf_tensor("qsb", [128, T], F32) as qsb, \
             nc.sbuf_tensor("ksb", [128, T], F32) as ksb, \
             nc.sbuf_tensor("vsb", [128, 8 * NTT * (HD + 1)], F32) as vsb, \
             nc.sbuf_tensor("pb0", [128, 2048], F32) as pb0, \
             nc.sbuf_tensor("pb1", [128, 2048], F32) as pb1, \
             nc.sbuf_tensor("whb0", [128, 2 * NKD * 128], F32) as whb0, \
             nc.sbuf_tensor("whb1", [128, 2 * NKD * 128], F32) as whb1, \
             nc.sbuf_tensor("wvb", [128, NKD * 512], F32) as wvb, \
             nc.sbuf_tensor("negch", [128, 1], F32) as negch, \
             nc.sbuf_tensor("rrow", [32, 512], F32) as rrow, \
             nc.sbuf_tensor("repsb", [64, 512], F32) as repsb, \
             nc.sbuf_tensor("ob", [64, 512], F32) as ob, \
             nc.sbuf_tensor("mx1", [32, 1], F32) as mx1, \
             nc.sbuf_tensor("mx2", [1, 1], F32) as mx2:
            h1a3 = h1a.ap().rearrange("p (c t) -> p c t", c=NKD)
            vsb4 = vsb.ap().rearrange("p (h tt f) -> p h tt f", h=8, tt=NTT)
            wvb3 = wvb.ap().rearrange("p (c n) -> p c n", c=NKD)
            e_h1l = [P.dma("sync", h1a3[:, c, :], self.h1f_d[c, :, :], after=bar, slot=0)
                     for c in range(NKD)]
            e_h1 = e_h1l[-1]
            e_vones = P.do("vector", lambda e: e.memset(vsb[:, :], 1.0), after=bar)
            P.do("vector", lambda e: e.memset(mx1[:, :], 0.0), after=bar)
            P.do("vector", lambda e: e.memset(rrow[:, :], 0.0), after=bar)
            pbs = (pb0, pb1)
            e_exp = {0: None, 1: None}
            e_ex_prev = None
            e_obv_prev = None
            e_pv_stop = None
            e_rep_ev = None
            e_rpc = None
            e_ob_dma = None
            e_obv_last = None
            e_km_prev = {0: None, 1: None}
            e_wvl_prev = None
            e_vm_last = None
            for g in range(2):
                e_wvl = P.dma("sync", wvb3[:, :, :],
                              self.wv.rearrange("(c p) m -> p c m", p=128)[:, :, g * 512:(g + 1) * 512],
                              after=[e_vm_last] + (bar if g == 0 else []), slot=2)
                e_wvl_prev = e_wvl
                e_ve = {0: None, 1: None}
                e_obv_gate = e_obv_last
                for tt in range(NTT):
                    pbk = (tt % 2) * 512

                    def vmm(e, tt=tt, pbk=pbk, g=g):
                        r = None
                        for cc in range(NKD):
                            r = e.matmul(self.psB[:, pbk:pbk + 512],
                                         h1a3[:, cc, tt * 128:(tt + 1) * 128],
                                         wvb3[:, cc, :], start=(cc == 0), stop=False)
                        return e.matmul(self.psB[:, pbk:pbk + 512], self.ones_row[:1, 0:128],
                                        self.bv_row[:1, g * 512:(g + 1) * 512],
                                        start=False, stop=True)
                    e_vm = P.do("tensor", vmm,
                                after=[e_wvl, e_h1, e_ve[tt % 2], e_obv_gate if tt <= 1 else None])
                    e_ve[tt % 2] = P.do("scalar", lambda e, tt=tt, pbk=pbk: e.copy(
                        vsb4[:, :, tt, 0:HD],
                        self.psB[:, pbk:pbk + 512].rearrange("p (h f) -> p h f", h=8)),
                        after=[e_vm, e_vones, e_pv_stop if tt <= 1 else None])
                e_vm_last = e_vm
                e_ve_last = e_ve[(NTT - 1) % 2]
                for j in range(4):
                    jj = g * 4 + j
                    whb = (whb0, whb1)[jj % 2]
                    whb3 = whb.ap().rearrange("p (w c n) -> p w c n", w=2, c=NKD)
                    e_whl = P.dma("sync", whb3[:, 0, :, :],
                                  self.wq.rearrange("(c p) m -> p c m", p=128)[:, :, jj * 128:(jj + 1) * 128],
                                  after=[e_km_prev[jj % 2]] + (bar if jj == 0 else []),
                                  slot=3 + jj % 2)
                    e_whl2 = P.dma("sync", whb3[:, 1, :, :],
                                   self.wk.rearrange("(c p) m -> p c m", p=128)[:, :, jj * 128:(jj + 1) * 128],
                                   after=[], slot=3 + jj % 2)

                    def qkmm(e, wi, whb3=whb3):
                        def f(ee):
                            r = None
                            for s4 in range(4):
                                for cc in range(NKD):
                                    r = ee.matmul(self.psA[:, s4 * 512:(s4 + 1) * 512],
                                                  whb3[:, wi, cc, :],
                                                  h1a3[:, cc, s4 * 512:(s4 + 1) * 512],
                                                  start=(cc == 0), stop=(cc == NKD - 1))
                            return r
                        return f
                    e_qm = P.do("tensor", qkmm(None, 0),
                                after=[e_whl2, e_h1, e_ex_prev, e_ve_last])
                    e_qe = P.do("vector", lambda e, jj=jj: e.tensor_scalar(
                        qsb[:, :], self.psA[:, :], self.bqc[:, jj:jj + 1], 0.125,
                        ALU.add, ALU.mult), after=[e_qm])
                    e_km = P.do("tensor", qkmm(None, 1), after=[e_qe])
                    e_km_prev[jj % 2] = e_km
                    e_ke = P.do("vector", lambda e, jj=jj: e.tensor_single_scalar(
                        ksb[:, :], self.psA[:, :], self.bkc[:, jj:jj + 1],
                        ALU.add), after=[e_km])
                    for s in range(2):
                        h = g * 8 + 2 * j + s
                        p0 = s * 64
                        e_sq = P.do("scalar", lambda e, p0=p0: e.activation(
                            out=self.stg[p0:p0 + 64, :], in_=qsb[p0:p0 + 64, :],
                            func=AF.Square), after=[e_qe, e_rep_ev])
                        e_sk = P.do("scalar", lambda e, p0=p0: e.activation(
                            out=self.stg2[p0:p0 + 64, :], in_=ksb[p0:p0 + 64, :],
                            func=AF.Square), after=[e_ke])

                        def nrm(e, src, row, p0=p0):
                            def f(ee):
                                r = None
                                for s4 in range(4):
                                    r = ee.matmul(self.psA[row:row + 1, s4 * 512:(s4 + 1) * 512],
                                                  self.ones_col[p0:p0 + 64, :1],
                                                  src[p0:p0 + 64, s4 * 512:(s4 + 1) * 512],
                                                  start=True, stop=True)
                                return r
                            return f
                        e_nq = P.do("tensor", nrm(None, self.stg, 0),
                                    after=[e_sq, e_ke, e_ex_prev])
                        e_nk = P.do("tensor", nrm(None, self.stg2, 32), after=[e_sk])
                        e_mq = P.do("vector", lambda e: e.reduce_max(
                            mx1[:1, :1], self.psA[0:1, :], axis=AX.X), after=[e_nq, e_nk])
                        e_mk = P.do("vector", lambda e: e.reduce_max(
                            mx2[:1, :1], self.psA[32:33, :], axis=AX.X), after=[e_mq])
                        e_s1 = P.do("scalar", lambda e: e.activation(out=mx1[:1, :1], in_=mx1[:1, :1],
                                                                     func=AF.Sqrt), after=[e_mq])
                        e_s2 = P.do("scalar", lambda e: e.activation(out=mx2[:1, :1], in_=mx2[:1, :1],
                                                                     func=AF.Sqrt), after=[e_mk])
                        e_ch = P.do("vector", lambda e: e.tensor_mul(mx1[:1, :1], mx1[:1, :1], mx2[:1, :1]),
                                    after=[e_s1, e_s2])
                        e_chn = P.do("vector", lambda e: e.tensor_scalar_mul(mx1[:1, :1], mx1[:1, :1], -1.0),
                                     after=[e_ch])

                        e_chc = e_chn
                        for qd in range(4):
                            e_chc = P.do("vector", lambda e, qd=qd: e.stream_shuffle(
                                negch[qd * 32:(qd + 1) * 32, 0:1], mx1[0:32, 0:1], [0] * 32),
                                after=[e_chc, e_ex_prev])
                        hh = 2 * j + s
                        for qs in range(4):
                            for sb in range(4):
                                reg = sb % 2

                                def smm4(e, sb=sb, qs=qs, p0=p0):
                                    r = None
                                    for i in range(4):
                                        kt = sb * 4 + i
                                        r = e.matmul(self.psA[:, i * 512:(i + 1) * 512],
                                                     ksb[p0:p0 + 64, kt * 128:(kt + 1) * 128],
                                                     qsb[p0:p0 + 64, qs * 512:(qs + 1) * 512],
                                                     start=True, stop=True)
                                    return r
                                e_sc = P.do("tensor", smm4,
                                            after=[e_chc if (qs == 0 and sb == 0) else None,
                                                   e_ex_prev])
                                e_ex = P.do("scalar", lambda e, reg=reg: e.activation(
                                    out=pbs[reg][:, :], in_=self.psA[:, :],
                                    func=AF.Exp, bias=negch[:, :1]),
                                    after=[e_sc, e_pv_stop if sb <= 1 else e_exp[reg]])
                                e_pv_prev = P.do("tensor", lambda e, sb=sb, qs=qs, reg=reg, hh=hh: [
                                    e.matmul(self.psB[0:65, qs * 512:(qs + 1) * 512],
                                             vsb4[:, hh, sb * 4 + i, 0:HD + 1],
                                             pbs[reg][:, i * 512:(i + 1) * 512],
                                             start=(sb == 0 and i == 0),
                                             stop=(sb == 3 and i == 3)) for i in range(4)][-1],
                                    after=[e_ex, e_ve_last] + ([e_rep_ev] if sb == 0 else []))
                                e_exp[reg] = e_pv_prev
                                e_ex_prev = e_ex
                            e_pv_stop = e_pv_prev
                            e_rc = P.do("vector", lambda e, qs=qs: e.reciprocal(
                                rrow[:1, :], self.psB[64:65, qs * 512:(qs + 1) * 512]), after=[e_pv_stop])
                            e_rp1 = P.do("vector", lambda e: e.stream_shuffle(
                                repsb[0:32, :], rrow[0:32, :], [0] * 32), after=[e_rc, e_obv_prev])
                            e_rpc = P.do("vector", lambda e: e.stream_shuffle(
                                repsb[32:64, :], rrow[0:32, :], [0] * 32), after=[e_rp1])
                            e_obv = P.do("vector", lambda e, qs=qs: e.tensor_mul(
                                ob[:, :], self.psB[0:64, qs * 512:(qs + 1) * 512], repsb[:, :]),
                                after=[e_rpc, e_ob_dma])
                            e_obv_prev = e_obv
                            e_ob_dma = P.dma("gpsimd",
                                             attnf_pct[(h % 2) * 64:(h % 2) * 64 + 64,
                                                       h // 2, qs * 512:(qs + 1) * 512],
                                             ob[:, :], after=[e_obv])
                            self.dbg("attnf", lambda t, h=h, qs=qs: t.rearrange("c p t -> p c t")[
                                (h % 2) * 64:(h % 2) * 64 + 64, h // 2, qs * 512:(qs + 1) * 512],
                                ob[:, :], after=[e_ob_dma])
                            e_rep_ev = e_obv
                        e_obv_last = e_obv
            self.ev_attn_sp = e_ob_dma
        self.bar = self.phase_end()

    def ph_a3(self):
        P, nc = self.P, self.nc
        bar = self.bar
        attnf_pct = self.attnf_d.rearrange("c p t -> p c t")
        h2a3 = self.h2a3
        with nc.sbuf_tensor("wob", [128, NKD * 1024], F32) as wob, \
             nc.sbuf_tensor("at0", [128, NKD * 128], F32) as at0, \
             nc.sbuf_tensor("at1", [128, NKD * 128], F32) as at1, \
             nc.sbuf_tensor("grep", [128, D], F32) as grep, \
             nc.sbuf_tensor("brep", [128, D], F32) as brep, \
             nc.sbuf_tensor("borep", [128, D], F32) as borep, \
             nc.sbuf_tensor("xnt", [128, D], F32) as xnt, \
             nc.sbuf_tensor("pret", [128, D], F32) as pret, \
             nc.sbuf_tensor("mamt", [128, D], F32) as mamt, \
             nc.sbuf_tensor("x2t", [128, D], F32) as x2t, \
             nc.sbuf_tensor("att", [128, D], F32) as att, \
             nc.sbuf_tensor("h2t", [128, D], F32) as h2t, \
             nc.sbuf_tensor("sqd2", [128, D], F32) as sqd2:
            wob3 = wob.ap().rearrange("p (c n) -> p c n", c=NKD)
            if self.moe_mode == "sorted":
                e_zr = P.do("vector", lambda e: e.memset(self.stg[:, 0:D], 0.0), after=bar)
                e_zsp = P.dma("gpsimd", self.h2tm_d[T:T + 1, :], self.stg[0:1, 0:D], after=[e_zr])
                self.ev_h2zero = e_zsp
            e_g = self.replicate(grep, self.l2g, D, after=bar)
            e_b = self.replicate(brep, self.l2b, D, after=[e_g])
            e_bo = self.replicate(borep, self.bo, D, after=[e_b])
            e_wol = [P.dma("sync", wob3[:, c, :], self.wo[c * 128:(c + 1) * 128, :], after=bar)
                     for c in range(NKD)]
            c = Chain(P, start=[e_bo] + e_wol + [self.ev_attn_sp])
            prev_epo = None
            for tt in range(NTT):
                ab = at0 if tt % 2 == 0 else at1
                ab3 = ab.ap().rearrange("p (c t) -> p c t", c=NKD)
                tile = slice(tt * 128, (tt + 1) * 128)
                e_at = P.dma("sync", ab3[:, :, :], attnf_pct[:, :, tile],
                             after=[self.ev_attn_sp] + (c.last if tt >= 2 else []),
                             slot=2 + tt % 2)
                e_xn = P.dma("sync", xnt[:, :], self.xn_d[tile, :],
                             after=[self.ev_xn_sp] + c.last, slot=1)
                e_pr = P.dma("sync", pret[:, :], self.pre0_d[tile, :],
                             after=[self.ev_p0_sp, prev_epo] + c.last, slot=9)
                e_mm2 = P.dma("sync", mamt[:, :], self.mam_d[tile, :],
                              after=self.ev_mam_sp[-1:] + c.last, slot=10)

                def womm(e, ab3=ab3):
                    r = None
                    for h2 in range(2):
                        for cc in range(NKD):
                            r = e.matmul(self.psA[:, h2 * 512:(h2 + 1) * 512],
                                         ab3[:, cc, :], wob3[:, cc, h2 * 512:(h2 + 1) * 512],
                                         start=(cc == 0), stop=(cc == NKD - 1))
                    return r
                c.do("tensor", womm, extra=[e_at])
                c.do("vector", lambda e: e.tensor_add(att[:, :], self.psA[:, 0:1024], borep[:, :]))
                c.do("vector", lambda e: e.tensor_add(x2t[:, :], att[:, :], xnt[:, :]), extra=[e_xn])
                self.dbg("x2", lambda t, tile=tile: t[tile, :], x2t[:, :], after=c.last)
                c.do("vector", lambda e: e.tensor_add(pret[:, :], pret[:, :], mamt[:, :]),
                     extra=[e_pr, e_mm2])
                c.do("vector", lambda e: e.tensor_add(pret[:, :], pret[:, :], att[:, :]))
                e_po = P.dma("gpsimd", self.preout[tile, :], pret[:, :], after=c.last)
                self.outs.append(e_po)
                prev_epo = e_po
                S, V = "scalar", "vector"
                c.do(S, lambda e: e.activation(out=sqd2[:, :], in_=x2t[:, :], func=AF.Square,
                                               accum_out=self.s_ssq[:, :1]))
                c.do(V, lambda e: e.reduce_sum(self.s_sum[:, :1], x2t[:, :], axis=AX.X))
                c.do(V, lambda e: e.tensor_scalar_mul(self.s_mu[:, :1], self.s_sum[:, :1], 1.0 / D))
                c.do(V, lambda e: e.tensor_mul(self.s_msq[:, :1], self.s_mu[:, :1], self.s_mu[:, :1]))
                c.do(V, lambda e: e.scalar_tensor_tensor(self.s_var[:, :1], self.s_ssq[:, :1], 1.0 / D,
                                                         self.s_msq[:, :1], ALU.mult, ALU.subtract))
                c.do(S, lambda e: e.activation(out=self.s_var[:, :1], in_=self.s_var[:, :1],
                                               func=AF.Sqrt, bias=self.epsc[:, :1]))
                c.do(V, lambda e: e.reciprocal(self.s_rstd[:, :1], self.s_var[:, :1]))
                c.do(V, lambda e: e.tensor_scalar(h2t[:, :], x2t[:, :], self.s_mu[:, :1],
                                                  self.s_rstd[:, :1], ALU.subtract, ALU.mult))
                c.do(V, lambda e: e.tensor_mul(h2t[:, :], h2t[:, :], grep[:, :]))
                c.do(V, lambda e: e.tensor_add(h2t[:, :], h2t[:, :], brep[:, :]))

                if self.moe_mode == "sorted":
                    e_h2sp = P.dma("gpsimd", self.h2tm_d[tile, :], h2t[:, :], after=c.last)
                    c.last = c.last + [e_h2sp]

                def tps(e):
                    r = None
                    for cc in range(NKD):
                        r = e.transpose(self.psB[:, cc * 128:(cc + 1) * 128],
                                        h2t[:, cc * 128:(cc + 1) * 128], self.ident[:, :])
                    return r
                c.do("tensor", tps)
                c.do(S, lambda e, tile=tile: e.copy(h2a3[:, :, tile], self.psB[:, 0:D]))
                self.dbg("h2f", lambda t, tile=tile: t.rearrange("c p t -> p c t")[:, :, tile],
                         h2a3[:, :, tile], after=c.last)
            self.ev_h2 = c.last[0]
        self.bar = self.phase_end()

    def ph_routing(self):
        P, nc = self.P, self.nc
        bar = self.bar
        h2a3 = self.h2a3
        with nc.sbuf_tensor("lgf", [E, T], F32) as lgf, \
             nc.sbuf_tensor("lgt", [128, 128], F32) as lgt, \
             nc.sbuf_tensor("lmx", [128, 8], F32) as lmx, \
             nc.sbuf_tensor("esr", [128, 8], F32) as esr, \
             nc.sbuf_tensor("pk", [128, 4], F32) as pk, \
             nc.sbuf_tensor("negl", [128, 1], F32) as negl, \
             nc.sbuf_tensor("sume", [128, 1], F32) as sume, \
             nc.sbuf_tensor("rws", [4, T], F32) as rws, \
             nc.sbuf_tensor("l1r", [1, T], F32) as l1r, \
             nc.sbuf_tensor("oh1", [E, T], F32) as oh1, \
             nc.sbuf_tensor("oh2", [E, T], F32) as oh2, \
             nc.sbuf_tensor("cum1", [E, T], F32) as cum1, \
             nc.sbuf_tensor("cum2", [E, T], F32) as cum2, \
             nc.sbuf_tensor("reprow", [E, T], F32) as reprow, \
             nc.sbuf_tensor("zro", [E, T], F32) as zro, \
             nc.sbuf_tensor("d1r", [1, T], F32) as d1r, \
             nc.sbuf_tensor("d2r", [1, T], F32) as d2r, \
             nc.sbuf_tensor("cnt", [1, 2 * E], F32) as cnt:
            def lgmm(e):
                r = None
                for s in range(4):
                    for cc in range(NKD):
                        r = e.matmul(self.psA[0:E, s * 512:(s + 1) * 512],
                                     self.wrc[:, cc * E:(cc + 1) * E],
                                     h2a3[:, cc, s * 512:(s + 1) * 512],
                                     start=(cc == 0), stop=(cc == NKD - 1))
                return r
            e_lgmm = P.do("tensor", lgmm, after=bar + [self.ev_h2])
            e_lgf = P.do("scalar", lambda e: e.copy(lgf[:, :], self.psA[0:E, :]), after=[e_lgmm])
            self.dbg("lgf", lambda t: t, lgf[:, :], after=[e_lgf])

            def tptm(e):
                r = None
                for tt in range(NTT):
                    r = e.transpose(self.psB[:, tt * 8:(tt + 1) * 8],
                                    lgf[0:E, tt * 128:(tt + 1) * 128], self.ident[0:E, 0:E])
                return r
            e_tp = P.do("tensor", tptm, after=[e_lgf])
            e_lgt = P.do("scalar", lambda e: e.copy(lgt[:, :], self.psB[:, 0:128]), after=[e_tp])
            c = Chain(P, start=[e_lgt])
            for tt in range(NTT):
                c.do("vector", lambda e, tt=tt: e.max(lmx[:, :], lgt[:, tt * 8:(tt + 1) * 8]))
                c.do("vector", lambda e: e.tensor_scalar_mul(negl[:, :1], lmx[:, 0:1], -1.0))
                c.do("scalar", lambda e: e.activation(out=esr[:, :], in_=lmx[:, :], func=AF.Exp,
                                                      bias=negl[:, :1], accum_out=sume[:, :1]))
                c.do("vector", lambda e: e.reciprocal(sume[:, :1], sume[:, :1]))
                c.do("vector", lambda e: e.tensor_single_scalar(pk[:, 2:4], esr[:, 0:2],
                                                                sume[:, :1], ALU.mult))
                c.do("vector", lambda e: e.tensor_copy(pk[:, 0:2], lmx[:, 0:2]))
                c.do("tensor", lambda e, tt=tt: e.transpose(self.psA[0:4, tt * 128:(tt + 1) * 128],
                                                            pk[:, 0:4], self.ident[:, :]))
            e_rws = P.do("scalar", lambda e: e.copy(rws[:, :], self.psA[0:4, :]), after=c.last)
            self.dbg("rws", lambda t: t, rws[:, :], after=[e_rws])
            e_l1r = P.dma("gpsimd", l1r[0:1, :], rws[1:2, :], after=[e_rws])

            def reprep(rowap):
                def f(e):
                    r = None
                    for s in range(4):
                        r = e.matmul(self.psB[0:E, s * 512:(s + 1) * 512], self.ones_row[:1, 0:E],
                                     rowap[0:1, s * 512:(s + 1) * 512], start=True, stop=True)
                    return r
                return f
            e_r1 = P.do("tensor", reprep(rws), after=[e_rws, e_tp])
            e_r1c = P.do("scalar", lambda e: e.copy(reprow[:, :], self.psB[0:E, :]), after=[e_r1])
            e_o1 = P.do("vector", lambda e: e.tensor_tensor(oh1[:, :], lgf[:, :], reprow[:, :],
                                                            ALU.is_equal), after=[e_r1c])
            e_r2 = P.do("tensor", reprep(l1r), after=[e_o1, e_l1r])
            e_r2c = P.do("scalar", lambda e: e.copy(reprow[:, :], self.psB[0:E, :]), after=[e_r2])
            e_o2 = P.do("vector", lambda e: e.tensor_tensor(oh2[:, :], lgf[:, :], reprow[:, :],
                                                            ALU.is_equal), after=[e_r2c])
            e_z = P.do("vector", lambda e: e.memset(zro[:, :], 0.0), after=bar)
            e_c1 = P.do("vector", lambda e: e.tensor_tensor_scan(
                out=cum1[:, :], data0=oh1[:, :], data1=zro[:, :], initial=0.0,
                op0=ALU.add, op1=ALU.add), after=[e_o1, e_z])
            e_c2 = P.do("vector", lambda e: e.tensor_tensor_scan(
                out=cum2[:, :], data0=oh2[:, :], data1=zro[:, :], initial=0.0,
                op0=ALU.add, op1=ALU.add), after=[e_o2, e_z])
            self.dbg("cum1", lambda t: t, cum1[:, :], after=[e_c1])
            self.dbg("cum2", lambda t: t, cum2[:, :], after=[e_c2])
            e_oh1sp = self.odma(self.oh1o, oh1[:, :], after=[e_o1])
            e_oh2sp = self.odma(self.oh2o, oh2[:, :], after=[e_o2])
            e_m1 = P.do("vector", lambda e: e.tensor_mul(self.stg[0:E, :], oh1[:, :], cum1[:, :]),
                        after=[e_c1])

            def dsum(e):
                r = None
                for s in range(4):
                    r = e.matmul(self.psB[0:1, s * 512:(s + 1) * 512], self.ones_col[0:E, :1],
                                 self.stg[0:E, s * 512:(s + 1) * 512], start=True, stop=True)
                return r
            e_d1m = P.do("tensor", dsum, after=[e_m1, e_r2c])
            e_d1 = P.do("scalar", lambda e: e.copy(d1r[:1, :], self.psB[0:1, :]), after=[e_d1m])
            e_m2 = P.do("vector", lambda e: e.tensor_mul(self.stg[0:E, :], oh2[:, :], cum2[:, :]),
                        after=[e_c2, e_d1m])
            e_d2m = P.do("tensor", dsum, after=[e_m2, e_d1])
            e_d2 = P.do("scalar", lambda e: e.copy(d2r[:1, :], self.psB[0:1, :]), after=[e_d2m])
            self.outs.append(P.dma("gpsimd", self.rowso[0:1, :], d1r[:1, :], after=[e_d1]))
            self.outs.append(P.dma("gpsimd", self.rowso[1:2, :], d2r[:1, :], after=[e_d2]))
            self.outs.append(P.dma("gpsimd", self.rowso[2:3, :], rws[2:3, :], after=[e_rws]))
            self.outs.append(P.dma("gpsimd", self.rowso[3:4, :], rws[3:4, :], after=[e_rws]))
            def cntt(e):
                e.transpose(self.psB[0:1, 0:E], cum1[0:E, T - 1:T], self.ident[0:E, 0:E])
                return e.transpose(self.psB[0:1, E:2 * E], cum2[0:E, T - 1:T], self.ident[0:E, 0:E])
            e_cm = P.do("tensor", cntt, after=[e_c1, e_c2, e_d2])
            e_cc = P.do("scalar", lambda e: e.copy(cnt[:1, :], self.psB[0:1, 0:2 * E]), after=[e_cm])
            self.outs.append(P.dma("gpsimd", self.cnto, cnt[:1, :], after=[e_cc]))
            def ohtm(src, dst):
                def f(e):
                    r = None
                    for tt in range(NTT):
                        r = e.transpose(self.psA[:, tt * 8:(tt + 1) * 8],
                                        src[0:E, tt * 128:(tt + 1) * 128], self.ident[0:E, 0:E])
                    return r
                return f
            e_t1 = P.do("tensor", ohtm(oh1, None), after=[e_o1, e_rws, e_d2m, e_cm])
            e_t1c = P.do("scalar", lambda e: e.copy(self.oh1tm[:, :], self.psA[:, 0:128]), after=[e_t1])
            e_t2 = P.do("tensor", ohtm(oh2, None), after=[e_o2, e_t1c])
            e_t2c = P.do("scalar", lambda e: e.copy(self.oh2tm[:, :], self.psA[:, 0:128]), after=[e_t2])
            self.ev_ohtm = [e_t1c, e_t2c]
            if self.moe_mode == "sorted":
                self._routing_sorted(locals())
        self.bar = self.phase_end()

    def _routing_sorted(self, L):
        P, nc = self.P, self.nc
        cum1, cum2, zro = L["cum1"], L["cum2"], L["zro"]
        oh1, oh2 = L["oh1"], L["oh2"]
        d1r, d2r = L["d1r"], L["d2r"]
        e_c1, e_c2 = L["e_c1"], L["e_c2"]
        e_d1, e_d2 = L["e_d1"], L["e_d2"]
        e_o1, e_o2 = L["e_o1"], L["e_o2"]
        with nc.sbuf_tensor("srt", [128, 96], F32) as srt, \
             nc.sbuf_tensor("vals", [128, 16], F32) as vals, \
             nc.sbuf_tensor("vals2", [128, 16], F32) as vals2, \
             nc.sbuf_tensor("dr1", [1, T], F32) as dr1, \
             nc.sbuf_tensor("dr2", [1, T], F32) as dr2, \
             nc.sbuf_tensor("dest1", [128, 16], F32) as dest1, \
             nc.sbuf_tensor("dest2", [128, 16], F32) as dest2, \
             nc.sbuf_tensor("dest1i", [128, 16], mybir.dt.int32) as dest1i, \
             nc.sbuf_tensor("dest2i", [128, 16], mybir.dt.int32) as dest2i, \
             nc.sbuf_tensor("ixc", [128, 1], mybir.dt.int32) as ixc, \
             nc.sbuf_tensor("row8", [1, 64], F32) as row8:
            c = Chain(P, start=[e_c1, e_c2, e_d1, e_d2, self.ev_h2zero] + self.ev_ohtm)
            V, S, G, Tn = "vector", "scalar", "gpsimd", "tensor"
            c.do(V, lambda e: e.tensor_add(srt[0:E, 0:1], cum1[0:E, T - 1:T], cum2[0:E, T - 1:T]))
            c.do(G, lambda e: e.iota(srt[0:E, 1:33], pattern=[[1, 32]], base=0,
                                     channel_multiplier=0, allow_small_or_imprecise_dtypes=True))
            c.do(V, lambda e: e.tensor_scalar_mul(srt[0:E, 1:33], srt[0:E, 1:33], 128.0))
            c.do(V, lambda e: e.tensor_single_scalar(srt[0:E, 1:33], srt[0:E, 1:33],
                                                     srt[0:E, 0:1], ALU.is_lt))
            c.do(V, lambda e: e.reduce_sum(srt[0:E, 33:34], srt[0:E, 1:33], axis=AX.X))
            c.do(V, lambda e: e.tensor_scalar_mul(srt[0:E, 33:34], srt[0:E, 33:34], 128.0))
            c.do(Tn, lambda e: e.transpose(self.psB[0:1, 0:E], srt[0:E, 33:34],
                                           self.ident[0:E, 0:E]), extra=[e_o2])
            c.do(S, lambda e: e.copy(row8[0:1, 0:E], self.psB[0:1, 0:E]))
            c.do(V, lambda e: e.tensor_tensor_scan(out=row8[0:1, 8:16], data0=row8[0:1, 0:E],
                                                   data1=zro[0:1, 0:E], initial=0.0,
                                                   op0=ALU.add, op1=ALU.add))
            c.do(V, lambda e: e.tensor_sub(row8[0:1, 16:24], row8[0:1, 8:16], row8[0:1, 0:E]))
            c.do(Tn, lambda e: e.transpose(self.psB[0:E, 32:33], row8[0:1, 16:24],
                                           self.ident[0:1, 0:1]))
            c.do(S, lambda e: e.copy(srt[0:E, 34:35], self.psB[0:E, 32:33]))
            c.do(V, lambda e: e.tensor_add(srt[0:E, 35:36], srt[0:E, 34:35], cum1[0:E, T - 1:T]))
            c.do(Tn, lambda e: e.transpose(self.psB[0:E, 33:34], row8[0:1, 8:16],
                                           self.ident[0:1, 0:1]))
            c.do(S, lambda e: e.copy(srt[0:E, 36:37], self.psB[0:E, 33:34]))

            def segmm(col, ohs):
                def f(e):
                    r = None
                    for s in range(4):
                        r = e.matmul(self.psB[0:1, s * 512:(s + 1) * 512], col,
                                     ohs[:, s * 512:(s + 1) * 512], start=True, stop=True)
                    return r
                return f
            c.do(Tn, segmm(srt[0:E, 34:35], oh1), extra=[e_o1])
            c.do(V, lambda e: e.scalar_tensor_tensor(dr1[0:1, :], d1r[0:1, :], -1.0,
                                                     self.psB[0:1, :], ALU.add, ALU.add))
            c.do(Tn, segmm(srt[0:E, 35:36], oh2))
            c.do(V, lambda e: e.scalar_tensor_tensor(dr2[0:1, :], d2r[0:1, :], -1.0,
                                                     self.psB[0:1, :], ALU.add, ALU.add))
            def dtp(row, coloff):
                def f(e):
                    r = None
                    for tt in range(NTT):
                        r = e.transpose(self.psA[:, coloff + tt:coloff + tt + 1],
                                        row[0:1, tt * 128:(tt + 1) * 128], self.ident[0:1, 0:1])
                    return r
                return f
            c.do(Tn, dtp(dr1, 0))
            c.do(S, lambda e: e.copy(dest1[:, :], self.psA[:, 0:16]))
            c.do(Tn, dtp(dr2, 16))
            c.do(S, lambda e: e.copy(dest2[:, :], self.psA[:, 16:32]))
            c.do(V, lambda e: e.tensor_copy(dest1i[:, :], dest1[:, :]))
            c.do(V, lambda e: e.tensor_copy(dest2i[:, :], dest2[:, :]))
            c.do(G, lambda e: e.iota(vals.ap(), pattern=[[128, 16]], base=0,
                                     channel_multiplier=1, allow_small_or_imprecise_dtypes=True))
            c.do(V, lambda e: e.tensor_scalar_add(vals2[:, :], vals[:, :], float(T + 16)))
            reprow = L["reprow"]
            c.do(V, lambda e: e.memset(reprow[0:1, :], float(T)))
            for s0 in (0, 2048, 4096):
                w = min(2048, 5248 - s0)
                c.do(G, lambda e, s0=s0, w=w: e.dma_start(
                    out=self.srctok_d[0:1, s0:s0 + w], in_=reprow[0:1, 0:w]), dma=True)
            c.do(V, lambda e: e.memset(reprow[0:1, :], float(2 * (T + 16))))
            for s0 in (0, 2048, 4096):
                w = min(2048, 5248 - s0)
                c.do(G, lambda e, s0=s0, w=w: e.dma_start(
                    out=self.dsttok_d[0:1, s0:s0 + w], in_=reprow[0:1, 0:w]), dma=True)
            src1d = self.srctok_d.rearrange("o (s u) -> (o s) u", u=1)
            dst1d = self.dsttok_d.rearrange("o (s u) -> (o s) u", u=1)
            for tt in range(NTT):
                for di, dsti in ((0, dest1i), (1, dest2i)):
                    c.do(V, lambda e, tt=tt, dsti=dsti: e.tensor_copy(ixc[:, :], dsti[:, tt:tt + 1]))
                    c.do(G, lambda e, tt=tt: e.indirect_dma_start(
                        src1d, IndirectOffsetOnAxis(ixc[:, 0:1], 0),
                        vals[:, tt:tt + 1], None), dma=True)
                    vsrc = vals if di == 0 else vals2
                    c.do(G, lambda e, vsrc=vsrc, tt=tt: e.indirect_dma_start(
                        dst1d, IndirectOffsetOnAxis(ixc[:, 0:1], 0),
                        vsrc[:, tt:tt + 1], None), dma=True)
            c.do(G, lambda e: e.iota(srt[0:E, 40:80], pattern=[[1, 40]], base=0,
                                     channel_multiplier=0, allow_small_or_imprecise_dtypes=True))
            c.do(V, lambda e: e.tensor_scalar_mul(srt[0:E, 40:80], srt[0:E, 40:80], 128.0))
            c.do(V, lambda e: e.tensor_single_scalar(srt[0:E, 40:80], srt[0:E, 40:80],
                                                     srt[0:E, 36:37], ALU.is_ge))

            def esum(e):
                return e.matmul(self.psB[0:1, 1024:1064], self.ones_col[0:E, 0:1],
                                srt[0:E, 40:80], start=True, stop=True)
            c.do(Tn, esum)
            c.do(S, lambda e: e.copy(row8[0:1, 24:64], self.psB[0:1, 1024:1064]))
            c.do(V, lambda e: e.tensor_scalar_min(row8[0:1, 24:64], row8[0:1, 24:64], 7.0))

            def erep(e):
                return e.matmul(self.psA[:, 32:72], self.ones_row[:1, 0:128],
                                row8[0:1, 24:64], start=True, stop=True)
            c.do(Tn, erep)
            c.do(S, lambda e: e.copy(self.eidx_fs[:, :], self.psA[:, 32:72]))
            c.do(V, lambda e: e.tensor_copy(self.eidx_i[:, :], self.eidx_fs[:, :]))
            c.do(G, lambda e: e.iota(self.basew1.ap(), pattern=[[128, 8]], base=0,
                                     channel_multiplier=1, allow_small_or_imprecise_dtypes=True))
            c.do(G, lambda e: e.iota(self.basew2.ap(), pattern=[[128, 32]], base=0,
                                     channel_multiplier=1, allow_small_or_imprecise_dtypes=True))
            self.ev_sortmeta = c.last[0]

    def ph_moes(self):
        P, nc = self.P, self.nc
        bar = self.bar
        NTILE = 40
        w1rows = self.w1.rearrange("e d f -> (e d) f")
        w2rows = self.w2.rearrange("e f d -> (e f) d")
        src1d = self.srctok_d.rearrange("o (s u) -> (o s) u", u=1)
        dst1d = self.dsttok_d.rearrange("o (s u) -> (o s) u", u=1)
        _st = ExitStack()
        gtm0 = _st.enter_context(nc.sbuf_tensor("gtm0", [128, D], F32))
        gtm1 = _st.enter_context(nc.sbuf_tensor("gtm1", [128, D], F32))
        hfm0 = _st.enter_context(nc.sbuf_tensor("hfm0", [128, NKD * 128], F32))
        hfm1 = _st.enter_context(nc.sbuf_tensor("hfm1", [128, NKD * 128], F32))
        w1g0 = _st.enter_context(nc.sbuf_tensor("w1g0", [128, FF], F32))
        w1g1 = _st.enter_context(nc.sbuf_tensor("w1g1", [128, FF], F32))
        b1g = _st.enter_context(nc.sbuf_tensor("b1g", [1, FF], F32))
        b1g2 = _st.enter_context(nc.sbuf_tensor("b1g2", [128, 32], F32))
        hetm = _st.enter_context(nc.sbuf_tensor("hetm", [128, FF], F32))
        hefm = _st.enter_context(nc.sbuf_tensor("hefm", [128, FF], F32))
        w2g0 = _st.enter_context(nc.sbuf_tensor("w2g0", [128, 4 * D], F32))
        w2g1 = _st.enter_context(nc.sbuf_tensor("w2g1", [128, 4 * D], F32))
        ysb = _st.enter_context(nc.sbuf_tensor("ysb", [128, D], F32))
        si0 = _st.enter_context(nc.sbuf_tensor("si0", [128, 1], mybir.dt.int32))
        si1 = _st.enter_context(nc.sbuf_tensor("si1", [128, 1], mybir.dt.int32))
        di0 = _st.enter_context(nc.sbuf_tensor("di0", [128, 1], mybir.dt.int32))
        di1 = _st.enter_context(nc.sbuf_tensor("di1", [128, 1], mybir.dt.int32))
        iw10 = _st.enter_context(nc.sbuf_tensor("iw10", [128, 8], mybir.dt.int32))
        iw11 = _st.enter_context(nc.sbuf_tensor("iw11", [128, 8], mybir.dt.int32))
        iw20 = _st.enter_context(nc.sbuf_tensor("iw20", [128, 32], mybir.dt.int32))
        iw21 = _st.enter_context(nc.sbuf_tensor("iw21", [128, 32], mybir.dt.int32))
        ib0 = _st.enter_context(nc.sbuf_tensor("ib0", [128, 1], mybir.dt.int32))
        ib1 = _st.enter_context(nc.sbuf_tensor("ib1", [128, 1], mybir.dt.int32))
        scf = _st.enter_context(nc.sbuf_tensor("scf", [128, 8], F32))
        ecol = _st.enter_context(nc.sbuf_tensor("ecol", [128, 1], F32))

        e_sc_prev = None
        e_y_ev = None
        e_b1prev = None
        e_prev_g = None
        e_prev_tp = None
        e_prev_hfuse = {0: None, 1: None}
        e_w1prev = {}
        e_w2prev = {0: None, 1: None}
        e_hetp = None
        e_mm2_last = None
        e_heev = None
        e_mm1_evacs = []
        for i in range(NTILE):
            gb = (gtm0, gtm1)[i % 2]
            hb = (hfm0, hfm1)[i % 2]
            sib = (si0, si1)[i % 2]
            dib = (di0, di1)[i % 2]
            iw1 = (iw10, iw11)[i % 2]
            iw2 = (iw20, iw21)[i % 2]
            ibb = (ib0, ib1)[i % 2]
            par = 4 * (i % 2)
            e_s1 = P.dma("sync", scf[:, par:par + 1], src1d[i * 128:(i + 1) * 128, :],
                         after=[e_sc_prev] + (bar if i == 0 else []))
            e_s2 = P.dma("sync", scf[:, par + 1:par + 2], dst1d[i * 128:(i + 1) * 128, :],
                         after=[e_s1])
            e_ci1 = P.do("vector", lambda e, par=par, sib=sib: e.tensor_copy(
                sib[:, :], scf[:, par:par + 1]),
                after=[e_s2, e_y_ev, self.ev_sortmeta if i == 0 else None])
            e_ci2 = P.do("vector", lambda e, par=par, dib=dib: e.tensor_copy(
                dib[:, :], scf[:, par + 1:par + 2]), after=[e_ci1])
            e_e1 = P.do("vector", lambda e, i=i: e.tensor_scalar_mul(
                ecol[:, :], self.eidx_fs[:, i:i + 1], 1024.0),
                after=[e_ci2, e_w1prev.get(("w", i % 2))])
            e_e2 = P.do("vector", lambda e, iw1=iw1, par=par: e.tensor_single_scalar(
                scf[:, par + 2:par + 3], self.basew1[:, 0:1], ecol[:, 0:1], ALU.add),
                after=[e_e1]) if False else None
            e_i1f = P.do("vector", lambda e, iw1=iw1: e.tensor_single_scalar(
                self.stg2[:, 0:8], self.basew1[:, :], ecol[:, 0:1], ALU.add), after=[e_e1])
            e_i1 = P.do("vector", lambda e, iw1=iw1: e.tensor_copy(iw1[:, :], self.stg2[:, 0:8]),
                        after=[e_i1f])
            e_e3 = P.do("vector", lambda e, i=i: e.tensor_scalar_mul(
                ecol[:, :], self.eidx_fs[:, i:i + 1], 4096.0), after=[e_i1f, e_w2prev[i % 2]])
            e_i2f = P.do("vector", lambda e: e.tensor_single_scalar(
                self.stg2[:, 8:40], self.basew2[:, :], ecol[:, 0:1], ALU.add), after=[e_e3])
            e_i2 = P.do("vector", lambda e, iw2=iw2: e.tensor_copy(iw2[:, :], self.stg2[:, 8:40]),
                        after=[e_i2f])
            e_ibf = P.do("vector", lambda e, i=i: e.tensor_scalar_mul(
                ecol[:, :], self.eidx_fs[:, i:i + 1], 128.0), after=[e_i2f, e_b1prev])
            e_ibf2 = P.do("vector", lambda e: e.tensor_add(ecol[:, :], ecol[:, :],
                                                           self.iota_c[:, 0:1]), after=[e_ibf])
            e_ib = P.do("vector", lambda e, ibb=ibb: e.tensor_copy(ibb[:, :], ecol[:, :]),
                        after=[e_ibf2, e_i2])
            e_ci = e_ib
            e_sc_prev = e_ci
            e_g = P.do("gpsimd", lambda e, gb=gb, sib=sib: e.indirect_dma_start(
                gb[:, :], None, self.h2tm_d, IndirectOffsetOnAxis(sib[:, 0:1], 0)),
                after=[e_ci1, e_prev_g], dma=True, slot=2)

            def gtp(e, gb=gb):
                r = None
                for cc in range(NKD):
                    r = e.transpose(self.psB[:, 1024 + cc * 128:1024 + (cc + 1) * 128],
                                    gb[:, cc * 128:(cc + 1) * 128], self.ident[:, :])
                return r
            e_tp = P.do("tensor", gtp, after=[e_g, e_prev_tp, self.ev_ident, e_mm2_last] +
                        e_mm1_evacs[-4:])
            e_hc = P.do("scalar", lambda e, hb=hb: e.copy(hb[:, :], self.psB[:, 1024:2048]),
                        after=[e_tp, e_prev_hfuse[i % 2]])
            e_prev_g = e_tp
            e_prev_tp = e_hc
            e_b1a = P.do("gpsimd", lambda e, ibb=ibb: e.indirect_dma_start(
                b1g2[:, :], None, self.b1.rearrange("e (p m) -> (e p) m", p=128),
                IndirectOffsetOnAxis(ibb[:, 0:1], 0)),
                after=[e_ci, e_heev, e_b1prev], dma=True, slot=3)
            e_b1 = P.do("gpsimd", lambda e: e.dma_start(out=b1g[0:1, :], in_=b1g2[:, :]),
                        after=[e_b1a, e_heev], dma=True, slot=3)
            e_b1prev = e_b1
            ps8 = [(self.psA, 0), (self.psA, 512), (self.psA, 1024), (self.psA, 1536),
                   (self.psB, 0), (self.psB, 512), (self.psB, 1024), (self.psB, 1536)]
            e_wl = {}
            e_mmk = None
            for cc in range(NKD):
                wb = (w1g0, w1g1)[cc % 2]
                e_w = P.do("gpsimd", lambda e, wb=wb, iw1=iw1, cc=cc: e.indirect_dma_start(
                    wb[:, :], None, w1rows, IndirectOffsetOnAxis(iw1[:, cc:cc + 1], 0)),
                    after=[e_i1, e_wl.get(cc - 2), e_mm2_last if cc <= 1 else None] +
                    (e_mm1_evacs if cc <= 1 else []), dma=True, slot=4 + cc % 2)

                def mm1(e, wb=wb, cc=cc, hb=hb):
                    r = None
                    for ms in range(8):
                        ps, off = ps8[ms]
                        r = e.matmul(ps[:, off:off + 512], hb[:, cc * 128:(cc + 1) * 128],
                                     wb[:, ms * 512:(ms + 1) * 512],
                                     start=(cc == 0), stop=False)
                    return r
                e_mmk = P.do("tensor", mm1, after=[e_w, e_hc, e_mmk] +
                             (e_mm1_evacs if cc <= 1 else []) + ([e_hetp] if cc == 0 else []))
                e_wl[cc] = e_mmk
                e_w_last = e_w

            def mm1b(e):
                r = None
                for ms in range(8):
                    ps, off = ps8[ms]
                    r = e.matmul(ps[:, off:off + 512], self.ones_row[:1, 0:128],
                                 b1g[0:1, ms * 512:(ms + 1) * 512], start=False, stop=True)
                return r
            e_mmb = P.do("tensor", mm1b, after=[e_mmk, e_b1])
            e_mm1_evacs = []
            e_he = {}
            for ms in range(8):
                ps, off = ps8[ms]
                e_h = P.do("scalar", lambda e, ps=ps, off=off, ms=ms: e.activation(
                    out=hetm[:, ms * 512:(ms + 1) * 512], in_=ps[:, off:off + 512],
                    func=AF.Relu), after=[e_mmb, e_hetp])
                e_he[ms] = e_h
                e_mm1_evacs.append(e_h)
            e_heev = e_he[7]
            e_prev_hfuse[i % 2] = e_mmk
            e_w1prev[("w", i % 2)] = e_w_last
            e_hetp = None
            for g8 in range(4):
                def htp(e, g8=g8):
                    r = None
                    for j in range(8):
                        mc = g8 * 8 + j
                        r = e.transpose(self.psA[:, 1024 + j * 128:1024 + (j + 1) * 128],
                                        hetm[:, mc * 128:(mc + 1) * 128], self.ident[:, :])
                    return r
                e_t = P.do("tensor", htp,
                           after=[e_he[2 * g8], e_he[2 * g8 + 1], e_he[2], e_he[3],
                                  e_hetp, e_mm2_last])
                e_hetp = P.do("scalar", lambda e, g8=g8: e.copy(
                    hefm[:, g8 * 1024:(g8 + 1) * 1024], self.psA[:, 1024:2048]),
                    after=[e_t, e_mm2_last])
            e_mm2_last = None
            for g in range(8):
                wb = (w2g0, w2g1)[g % 2]
                e_w2 = None
                for j in range(4):
                    e_w2 = P.do("gpsimd", lambda e, wb=wb, iw2=iw2, g=g, j=j: e.indirect_dma_start(
                        wb[:, j * D:(j + 1) * D], None, w2rows,
                        IndirectOffsetOnAxis(iw2[:, g * 4 + j:g * 4 + j + 1], 0)),
                        after=([e_i2, e_w2prev[g % 2]] if j == 0 else []),
                        dma=True, slot=6 + g % 2)

                def mm2(e, wb=wb, g=g):
                    r = None
                    for j in range(4):
                        mc = g * 4 + j
                        for dh in range(2):
                            r = e.matmul(self.psB[:, dh * 512:(dh + 1) * 512],
                                         hefm[:, mc * 128:(mc + 1) * 128],
                                         wb[:, j * D + dh * 512:j * D + (dh + 1) * 512],
                                         start=(g == 0 and j == 0), stop=(g == 7 and j == 3))
                    return r
                e_m2 = P.do("tensor", mm2,
                            after=[e_w2, e_hetp if g >= 1 else None, e_mm2_last] +
                            ([e_he[4], e_he[5], e_y_ev] if g == 0 else []) +
                            ([e_hetp] if g == 0 else []))
                e_w2prev[g % 2] = e_m2
                e_mm2_last = e_m2
            e_ye = P.do("scalar", lambda e: e.copy(ysb[:, :], self.psB[:, 0:1024]),
                        after=[e_mm2_last, e_y_ev])
            e_ysc = P.do("gpsimd", lambda e, dib=dib: e.indirect_dma_start(
                self.y12o, IndirectOffsetOnAxis(dib[:, 0:1], 0),
                ysb[:, :], None), after=[e_ye], dma=True, slot=8)
            self.outs.append(e_ysc)
            e_y_ev = e_ysc
        _st.close()
        self.bar = self.phase_end()


    def ph_moe1(self):
        P, nc = self.P, self.nc
        bar = self.bar
        h2a3 = self.h2a3
        with nc.sbuf_tensor("w1b0", [128, NKD * 128], F32) as w1b0, \
             nc.sbuf_tensor("w1b1", [128, NKD * 128], F32) as w1b1, \
             nc.sbuf_tensor("heb0", [128, 2048], F32) as heb0, \
             nc.sbuf_tensor("heb1", [128, 2048], F32) as heb1:
            w1bufs = (w1b0, w1b1)
            hebufs = (heb0, heb1)
            e_mm = {0: None, 1: None}
            e_sp = {0: None, 1: None}
            e_ld = {0: None, 1: None}
            for e8 in range(E):
                for m in range(NFF):
                    i = m % 2
                    wb = w1bufs[i]
                    wb3 = wb.ap().rearrange("p (c m) -> p c m", c=NKD)
                    e_l = []
                    for cc in range(NKD):
                        e_l.append(P.dma("sync", wb3[:, cc, :],
                                         self.w1[e8, cc * 128:(cc + 1) * 128, m * 128:(m + 1) * 128],
                                         after=[e_mm[i]] + (bar if e8 == 0 and m == 0 else [])))
                    e_ld[i] = e_l

                    def mmf(e, wb3=wb3, i=i):
                        ps = self.psA if i == 0 else self.psB
                        r = None
                        for s in range(4):
                            for cc in range(NKD):
                                r = e.matmul(ps[:, s * 512:(s + 1) * 512],
                                             wb3[:, cc, :], h2a3[:, cc, s * 512:(s + 1) * 512],
                                             start=(cc == 0), stop=(cc == NKD - 1))
                        return r
                    e_m = P.do("tensor", mmf, after=e_l + [e_sp[i]])
                    e_mm[i] = e_m
                    e_rl = P.do("scalar", lambda e, i=i, e8=e8, m=m: e.activation(
                        out=hebufs[i][:, :], in_=(self.psA if i == 0 else self.psB)[:, :],
                        func=AF.Relu, bias=self.b1c[:, e8 * NFF + m:e8 * NFF + m + 1]),
                        after=[e_m, e_sp[i]])
                    e_s = P.dma("gpsimd", self.he_d[e8, m, :, :], hebufs[i][:, :], after=[e_rl])
                    if e8 == 0:
                        e_s = self.dbg("he0", lambda t, m=m: t[m, :, :], hebufs[i][:, :],
                                       after=[e_s]) or e_s
                    e_sp[i] = e_s
            self.ev_he = [e_sp[0], e_sp[1]]
        self.bar = self.phase_end()

    def ph_moe2(self):
        P, nc = self.P, self.nc
        bar = self.bar
        with nc.sbuf_tensor("y1a", [128, NTT * 1024], F32) as y1a, \
             nc.sbuf_tensor("y2a", [128, NTT * 1024], F32) as y2a, \
             nc.sbuf_tensor("hec0", [128, 512], F32) as heb0, \
             nc.sbuf_tensor("hec1", [128, 512], F32) as heb1, \
             nc.sbuf_tensor("w2b0", [128, 1024], F32) as w2b0, \
             nc.sbuf_tensor("w2b1", [128, 1024], F32) as w2b1:
            y1a3 = y1a.ap().rearrange("p (tt n) -> p tt n", tt=NTT)
            y2a3 = y2a.ap().rearrange("p (tt n) -> p tt n", tt=NTT)
            hebufs, w2bufs = (heb0, heb1), (w2b0, w2b1)
            e_z1 = P.do("vector", lambda e: e.memset(y1a[:, :], 0.0), after=bar)
            e_z2 = P.do("vector", lambda e: e.memset(y2a[:, :], 0.0), after=bar)
            ps_slot = [(self.psA, 0), (self.psA, 1024), (self.psB, 0), (self.psB, 1024)]
            e_ev2 = {}
            e_mmk = {0: None, 1: None}
            for e8 in range(E):
                for tb in range(4):
                    e_mm_last = None
                    for k in range(NFF):
                        i = k % 2
                        e_hl = P.dma("sync", hebufs[i][:, :],
                                     self.he_d[e8, k, :, tb * 512:(tb + 1) * 512],
                                     after=[e_mmk[i], self.ev_he[i] if (e8 == 0 and tb == 0) else None])
                        e_wl = P.dma("sync", w2bufs[i][:, :],
                                     self.w2[e8, k * 128:(k + 1) * 128, :],
                                     after=[e_mmk[i]])

                        def mmf(e, i=i, k=k, tb=tb):
                            r = None
                            for tl in range(4):
                                ps, off = ps_slot[tl]
                                for h2 in range(2):
                                    r = e.matmul(ps[:, off + h2 * 512:off + (h2 + 1) * 512],
                                                 hebufs[i][:, tl * 128:(tl + 1) * 128],
                                                 w2bufs[i][:, h2 * 512:(h2 + 1) * 512],
                                                 start=(k == 0), stop=(k == NFF - 1))
                            return r
                        prev_tb = (e8, tb - 1) if tb else (e8 - 1, 3)
                        e_mm_last = P.do("tensor", mmf,
                                         after=[e_hl, e_wl] +
                                         ([e_ev2.get(prev_tb + (tl,)) for tl in range(4)]
                                          if k == 0 else []) +
                                         ([e_z1, e_z2] if (e8 == 0 and k == 0) else []))
                        e_mmk[i] = e_mm_last
                    for tl in range(4):
                        tt = tb * 4 + tl
                        ps, off = ps_slot[tl]
                        e_a1 = P.do("vector", lambda e, ps=ps, off=off, tt=tt, e8=e8:
                                    e.scalar_tensor_tensor(
                                        y1a3[:, tt, :], ps[:, off:off + 1024],
                                        self.oh1tm[:, tt * 8 + e8:tt * 8 + e8 + 1],
                                        y1a3[:, tt, :], ALU.mult, ALU.add),
                                    after=[e_mm_last, e_ev2.get((e8 - 1, tb, tl))] +
                                    (self.ev_ohtm if e8 == 0 else []) +
                                    ([e_z1] if e8 == 0 else []))
                        e_a2 = P.do("vector", lambda e, ps=ps, off=off, tt=tt, e8=e8:
                                    e.scalar_tensor_tensor(
                                        y2a3[:, tt, :], ps[:, off:off + 1024],
                                        self.oh2tm[:, tt * 8 + e8:tt * 8 + e8 + 1],
                                        y2a3[:, tt, :], ALU.mult, ALU.add),
                                    after=[e_a1] + ([e_z2] if e8 == 0 else []))
                        e_ev2[(e8, tb, tl)] = e_a2
            for tt in range(NTT):
                self.outs.append(P.dma(
                    "gpsimd", self.y1o[tt * 128:(tt + 1) * 128, :], y1a3[:, tt, :],
                    after=[e_ev2[(E - 1, tt // 4, tt % 4)]]))
                self.outs.append(P.dma(
                    "gpsimd", self.y2o[tt * 128:(tt + 1) * 128, :], y2a3[:, tt, :],
                    after=[e_ev2[(E - 1, tt // 4, tt % 4)]]))
        self.bar = self.phase_end()



F32 = mybir.dt.float32
AF = mybir.ActivationFunctionType
ALU = mybir.AluOpType
AX = mybir.AxisListType

T, D, E, C = 2048, 1024, 8, 4096
NTT = T // 128
EPS32 = float(np.finfo(np.float32).eps)


class L2:
    def __init__(self):
        self.nc = bass.Bass("TRN2", target_bir_lowering=False)
        self.P = Prog(self.nc)
        self.outs = []
        self.build()

    def build(self):
        nc, P = self.nc, self.P

        def din(name, shape):
            return nc.dram_tensor(name, shape, F32, kind="ExternalInput").ap()
        pre = din("pre", [T, D])
        y1 = din("y1", [T, D])
        y2 = din("y2", [T, D])
        rows = din("rows", [4, T])
        oh1 = din("oh1", [E, T])
        oh2 = din("oh2", [E, T])
        cnts = din("cnts", [E, 2 * E])
        rid = din("rid", [1, 1])
        b2 = din("b2", [E, D])
        outp = nc.dram_tensor("outp", [T, D], F32, kind="ExternalOutput").ap()

        A = nc.alloc_sbuf_tensor
        oh1s = A("oh1s", [E, T], F32)
        oh2s = A("oh2s", [E, T], F32)
        d1s = A("d1s", [1, T], F32)
        d2s = A("d2s", [1, T], F32)
        t1s = A("t1s", [1, T], F32)
        t2s = A("t2s", [1, T], F32)
        cnt = A("cnt", [E, 2 * E], F32)
        rid_s = A("rid_s", [1, 1], F32)
        b2s = A("b2s", [E, D], F32)
        iota8 = A("iota8", [128, 1], F32)
        ones_row = A("ones_row", [1, 128], F32)
        ones_col = A("ones_col", [128, 1], F32)
        ident = A("ident", [128, 128], F32)
        iota_r = A("iota_r", [128, 128], F32)
        prefm = A("prefm", [E, 1], F32)
        offc1 = A("offc1", [E, 1], F32)
        offc2 = A("offc2", [E, 1], F32)
        totc1 = A("totc1", [E, 1], F32)
        loc = A("loc", [1, T], F32)
        v1r = A("v1r", [1, T], F32)
        v2r = A("v2r", [1, T], F32)
        g1r = A("g1r", [1, T], F32)
        g2r = A("g2r", [1, T], F32)
        den = A("den", [1, T], F32)
        G1 = A("G1", [E, T], F32)
        GG = A("GG", [E, T], F32)
        gtm = A("gtm", [128, 2 * NTT], F32)
        pret = A("pret", [128, D], F32)
        y1t = A("y1t", [128, D], F32)
        y2t = A("y2t", [128, D], F32)
        ot = A("ot", [128, D], F32)

        psA = nc.alloc_psum_tensor("psA", [128, 2048], F32)
        psB = nc.alloc_psum_tensor("psB", [128, 2048], F32)

        ld = lambda out, in_, after=(): P.dma("sync", out, in_, after=after)
        e_in = [ld(oh1s[:, :], oh1), ld(oh2s[:, :], oh2), ld(d1s[0:1, :], rows[0:1, :]),
                ld(cnt[:, :], cnts), ld(rid_s[:1, :1], rid), ld(b2s[:, :], b2),
                ld(d2s[0:1, :], rows[1:2, :]), ld(t1s[0:1, :], rows[2:3, :]),
                ld(t2s[0:1, :], rows[3:4, :])]
        e_ir = P.do("gpsimd", lambda e: e.iota(iota8.ap(), pattern=[[0, 1]], base=0,
                                               channel_multiplier=1,
                                               allow_small_or_imprecise_dtypes=True))
        e_ir2 = P.do("gpsimd", lambda e: e.iota(iota_r.ap(), pattern=[[1, 128]], base=0,
                                                channel_multiplier=0,
                                                allow_small_or_imprecise_dtypes=True))
        e_c0 = P.do("vector", lambda e: e.memset(ones_row[:1, :], 1.0))
        e_c1 = P.do("vector", lambda e: e.memset(ones_col[:, :], 1.0))
        e_id = P.do("vector", lambda e: e.tensor_single_scalar(
            ident[:, :], iota_r[:, :], iota8[:, :1], ALU.is_equal), after=[e_ir, e_ir2])

        e_rb = P.do("tensor", lambda e: e.matmul(psB[0:E, 0:1], ones_row[:1, 0:E],
                                                 rid_s[:1, 0:1], start=True, stop=True),
                    after=[e_in[4], e_c0])
        e_pm = P.do("vector", lambda e: e.tensor_tensor(prefm[:, :1], iota8[0:E, :1],
                                                        psB[0:E, 0:1], ALU.is_lt),
                    after=[e_rb, e_ir])
        e_o1 = P.do("tensor", lambda e: e.matmul(psA[0:E, 0:1], cnt[:, 0:E], prefm[:, 0:1],
                                                 start=True, stop=True),
                    after=[e_pm, e_in[3]])
        e_o1c = P.do("scalar", lambda e: e.copy(offc1[:, :1], psA[0:E, 0:1]), after=[e_o1])
        e_o2 = P.do("tensor", lambda e: e.matmul(psA[0:E, 1:2], cnt[:, E:2 * E], prefm[:, 0:1],
                                                 start=True, stop=True), after=[e_o1c, e_pm])
        e_o2c = P.do("scalar", lambda e: e.copy(offc2[:, :1], psA[0:E, 1:2]), after=[e_o2])
        e_t1 = P.do("tensor", lambda e: e.matmul(psA[0:E, 2:3], cnt[:, 0:E], ones_col[0:E, 0:1],
                                                 start=True, stop=True), after=[e_o2c, e_c1])
        e_t1c = P.do("scalar", lambda e: e.copy(totc1[:, :1], psA[0:E, 2:3]), after=[e_t1])

        def rowmm(col_evs, cols, ohs, oh_ev):
            def f(e):
                r = None
                for s in range(4):
                    for ci, colap in enumerate(cols):
                        r = e.matmul(psB[0:1, s * 512:(s + 1) * 512], colap,
                                     ohs[:, s * 512:(s + 1) * 512],
                                     start=(ci == 0), stop=(ci == len(cols) - 1))
                return r
            return f
        e_m1 = P.do("tensor", rowmm(None, [offc1[:, 0:1]], oh1s, None),
                    after=[e_o1c, e_in[0], e_rb])
        e_l1 = P.do("vector", lambda e: e.scalar_tensor_tensor(
            loc[:1, :], d1s[0:1, :], -1.0, psB[0:1, :], ALU.add, ALU.add),
            after=[e_m1, e_in[2]])
        e_v1 = P.do("vector", lambda e: e.tensor_single_scalar(v1r[:1, :], loc[:1, :],
                                                               float(C), ALU.is_lt),
                    after=[e_l1])
        e_m2 = P.do("tensor", rowmm(None, [offc2[:, 0:1], totc1[:, 0:1]], oh2s, None),
                    after=[e_o2c, e_t1c, e_in[1], e_v1])
        e_l2 = P.do("vector", lambda e: e.scalar_tensor_tensor(
            loc[:1, :], d2s[0:1, :], -1.0, psB[0:1, :], ALU.add, ALU.add), after=[e_m2, e_in[6]])
        e_v2 = P.do("vector", lambda e: e.tensor_single_scalar(v2r[:1, :], loc[:1, :],
                                                               float(C), ALU.is_lt),
                    after=[e_l2])
        e_k1 = P.do("vector", lambda e: e.tensor_mul(v1r[:1, :], v1r[:1, :], t1s[0:1, :]),
                    after=[e_v1, e_in[7]])
        e_k2 = P.do("vector", lambda e: e.tensor_mul(v2r[:1, :], v2r[:1, :], t2s[0:1, :]),
                    after=[e_v2, e_in[8]])
        e_dn = P.do("vector", lambda e: e.tensor_add(den[:1, :], v1r[:1, :], v2r[:1, :]),
                    after=[e_k1, e_k2])
        e_dn2 = P.do("vector", lambda e: e.tensor_scalar_max(den[:1, :], den[:1, :], EPS32),
                     after=[e_dn])
        e_rc = P.do("vector", lambda e: e.reciprocal(den[:1, :], den[:1, :]), after=[e_dn2])
        e_g1 = P.do("vector", lambda e: e.tensor_mul(g1r[:1, :], v1r[:1, :], den[:1, :]),
                    after=[e_rc])
        e_g1b = P.do("vector", lambda e: e.tensor_scalar_mul(g1r[:1, :], g1r[:1, :], 0.8),
                     after=[e_g1])
        e_g2 = P.do("vector", lambda e: e.tensor_mul(g2r[:1, :], v2r[:1, :], den[:1, :]),
                    after=[e_g1b])
        e_g2b = P.do("vector", lambda e: e.tensor_scalar_mul(g2r[:1, :], g2r[:1, :], 0.8),
                     after=[e_g2])
        def grep(rowap):
            def f(e):
                r = None
                for s in range(4):
                    r = e.matmul(psB[0:E, s * 512:(s + 1) * 512], ones_row[:1, 0:E],
                                 rowap[0:1, s * 512:(s + 1) * 512], start=True, stop=True)
                return r
            return f
        e_gr1 = P.do("tensor", grep(g1r), after=[e_g1b, e_m2])
        e_G1 = P.do("vector", lambda e: e.tensor_mul(G1[:, :], oh1s[:, :], psB[0:E, :]),
                    after=[e_gr1])
        e_gr2 = P.do("tensor", grep(g2r), after=[e_g2b, e_G1])
        e_G2 = P.do("vector", lambda e: e.tensor_mul(GG[:, :], oh2s[:, :], psB[0:E, :]),
                    after=[e_gr2])
        e_G = P.do("vector", lambda e: e.tensor_add(GG[:, :], GG[:, :], G1[:, :]), after=[e_G2])
        def gtp(e):
            r = None
            for tt in range(NTT):
                r = e.transpose(psA[:, 512 + tt * 2:512 + tt * 2 + 1],
                                g1r[0:1, tt * 128:(tt + 1) * 128], ident[0:1, 0:1])
                r = e.transpose(psA[:, 512 + tt * 2 + 1:512 + tt * 2 + 2],
                                g2r[0:1, tt * 128:(tt + 1) * 128], ident[0:1, 0:1])
            return r
        e_gt = P.do("tensor", gtp, after=[e_g2b, e_id, e_t1c])
        e_gtc = P.do("scalar", lambda e: e.copy(gtm[:, :], psA[:, 512:512 + 2 * NTT]),
                     after=[e_gt])
        e_prev = {}
        for tt in range(NTT):
            tile = slice(tt * 128, (tt + 1) * 128)
            e_p = ld(pret[:, :], pre[tile, :], after=[e_prev.get("o")])
            e_y1 = ld(y1t[:, :], y1[tile, :], after=[e_prev.get("o")])
            e_y2 = ld(y2t[:, :], y2[tile, :], after=[e_prev.get("o")])

            def b2mm(e, tt=tt):
                r = None
                for h2 in range(2):
                    r = e.matmul(psA[:, h2 * 512:(h2 + 1) * 512],
                                 GG[0:E, tt * 128:(tt + 1) * 128],
                                 b2s[:, h2 * 512:(h2 + 1) * 512], start=True, stop=True)
                return r
            e_bm = P.do("tensor", b2mm, after=[e_G, e_in[5], e_gtc, e_prev.get("a3")])
            e_a1 = P.do("vector", lambda e, tt=tt: e.scalar_tensor_tensor(
                ot[:, :], y1t[:, :], gtm[:, tt * 2:tt * 2 + 1], pret[:, :],
                ALU.mult, ALU.add), after=[e_p, e_y1, e_gtc, e_prev.get("o")])
            e_a2 = P.do("vector", lambda e, tt=tt: e.scalar_tensor_tensor(
                ot[:, :], y2t[:, :], gtm[:, tt * 2 + 1:tt * 2 + 2], ot[:, :],
                ALU.mult, ALU.add), after=[e_a1, e_y2])
            e_a3 = P.do("vector", lambda e: e.tensor_add(ot[:, :], ot[:, :], psA[:, 0:1024]),
                        after=[e_a2, e_bm])
            e_o = P.dma("gpsimd", outp[tile, :], ot[:, :], after=[e_a3])
            self.outs.append(e_o)
            e_prev = {"o": e_o, "a3": e_a3}
        P.do("sync", lambda e: e.nop(), after=self.outs)
        P.finalize()


_CACHE = {}


def _get_progs():
    if "p" not in _CACHE:
        _CACHE["p"] = (L1(upto=6, dbg=()), L2())
    return _CACHE["p"]


def kernel(**inputs):
    b1, b2 = _get_progs()
    KEYMAP = {
        "lng": "L0_attn_ln_g", "lnb": "L0_attn_ln_b", "w_in": "L0_attn_W_in",
        "conv_w": "L0_attn_conv_w", "conv_b": "L0_attn_conv_b", "w_x": "L0_attn_W_x",
        "w_dt": "L0_attn_W_dt", "b_dt": "L0_attn_b_dt", "a_log": "L0_attn_A_log",
        "dvec": "L0_attn_D", "w_out": "L0_attn_W_out",
        "wq": "L0_ffn_wq", "wk": "L0_ffn_wk", "wv": "L0_ffn_wv", "wo": "L0_ffn_wo",
        "bq": "L0_ffn_bq", "bk": "L0_ffn_bk", "bv": "L0_ffn_bv", "bo": "L0_ffn_bo",
        "l1g": "L0_ffn_ln1_g", "l1b": "L0_ffn_ln1_b",
        "l2g": "L0_ffn_ln2_g", "l2b": "L0_ffn_ln2_b",
        "wr": "L0_ffn_wr", "w1": "L0_ffn_w1", "b1": "L0_ffn_b1", "w2": "L0_ffn_w2",
    }
    base = {k: np.ascontiguousarray(np.asarray(inputs[v], np.float32))
            for k, v in KEYMAP.items()}
    x = np.asarray(inputs["L0_attn_x"], np.float32)
    in_maps1 = []
    for c in range(8):
        m = dict(base)
        m["xw"] = np.ascontiguousarray(x[c])
        in_maps1.append(m)
    r1 = run_bass_kernel_spmd(b1.nc, in_maps1, core_ids=list(range(8)))
    cnts = np.stack([np.asarray(r["cnto"])[0] for r in r1.results])
    b2w = np.ascontiguousarray(np.asarray(inputs["L0_ffn_b2"], np.float32))
    in_maps2 = []
    for c in range(8):
        r = r1.results[c]
        y12 = np.asarray(r["y12o"])
        in_maps2.append({
            "pre": np.asarray(r["preout"]), "y1": np.ascontiguousarray(y12[0:2048]),
            "y2": np.ascontiguousarray(y12[2064:4112]), "rows": np.asarray(r["rowso"]),
            "oh1": np.asarray(r["oh1o"]), "oh2": np.asarray(r["oh2o"]),
            "cnts": cnts, "rid": np.full((1, 1), float(c), np.float32),
            "b2": b2w,
        })
    r2 = run_bass_kernel_spmd(b2.nc, in_maps2, core_ids=list(range(8)))
    return np.stack([np.asarray(r["outp"]) for r in r2.results])


# version 06
# speedup vs baseline: 1.9198x; 1.1580x over previous
from contextlib import ExitStack
import numpy as np
import concourse.bass as bass
import concourse.mybir as mybir
from concourse.bass_utils import run_bass_kernel_spmd

ENGS = ("sync", "tensor", "scalar", "vector", "gpsimd")


class Ev:
    __slots__ = ("sem", "val", "eng", "used", "dma", "slot")

    def __init__(self, sem, val, eng, dma):
        self.sem = sem
        self.val = val
        self.eng = eng
        self.used = False
        self.dma = dma
        self.slot = None


class _Op:
    __slots__ = ("fn", "waits", "ev", "dma")

    def __init__(self, fn, waits, ev, dma):
        self.fn = fn
        self.waits = waits
        self.ev = ev
        self.dma = dma


class Prog:
    def __init__(self, nc):
        self.nc = nc
        self.q = {e: [] for e in ENGS}
        self.epoch = 0
        self._cur = {}
        self.n_sem_used = 0

    def _engine_sem(self, eng, dma):
        key = (eng, dma, self.epoch)
        if key not in self._cur:
            tag = f"s{dma[1]}" if isinstance(dma, tuple) else ("d" if dma else "")
            name = f"s_{eng}{tag}_{self.epoch}"
            h = self.nc.alloc_semaphore(name)
            self._cur[key] = [h, 0]
            self.n_sem_used += 1
        return self._cur[key]

    def new_epoch(self):
        self.epoch += 1

    def barrier(self):
        evs = []
        for eng in ENGS:
            if self.q[eng]:
                evs.append(self.q[eng][-1].ev)
                evs[-1].used = True
        return evs

    def do(self, eng, fn, after=(), dma=False, slot=None):
        if dma and slot is not None:
            s = self._engine_sem(eng, ("slot", slot))
        else:
            s = self._engine_sem(eng, dma)
        ev = Ev(s[0], 0, eng, dma)
        ev.slot = slot
        waits = []
        for a in after:
            if a is None:
                continue
            waits.append(a)
            a.used = True
        self.q[eng].append(_Op(fn, waits, ev, dma))
        return ev

    def dma(self, eng, out, in_, after=(), slot=None, **kw):
        return self.do(eng, lambda e: e.dma_start(out=out, in_=in_, **kw),
                       after=after, dma=True, slot=slot)

    def finalize(self):
        totals = {}
        for eng in ENGS:
            for op in self.q[eng]:
                if op.dma:
                    totals[op.ev.sem] = totals.get(op.ev.sem, 0) + 16
                    op.ev.val = totals[op.ev.sem]
                elif op.ev.used:
                    totals[op.ev.sem] = totals.get(op.ev.sem, 0) + 1
                    op.ev.val = totals[op.ev.sem]
        watermark = {e: {} for e in ENGS}
        nc = self.nc
        with nc.Block() as blk:
            def emit_engine(eng):
                def body(engine):
                    wm = watermark[eng]
                    for op in self.q[eng]:
                        for a in op.waits:
                            if a.eng == eng and a.dma and a.slot is None:
                                continue
                            if wm.get(a.sem, -1) >= a.val:
                                continue
                            engine.wait_ge(a.sem, a.val)
                            wm[a.sem] = a.val
                        instr = op.fn(engine)
                        if op.dma:
                            instr.then_inc(op.ev.sem, 16)
                            if op.ev.used and op.ev.slot is None:
                                engine.wait_ge(op.ev.sem, op.ev.val)
                                wm[op.ev.sem] = op.ev.val
                        elif op.ev.used:
                            if instr is None:
                                instr = engine.nop()
                            instr.then_inc(op.ev.sem, 1)
                return body
            for eng in ENGS:
                if self.q[eng]:
                    getattr(blk, eng)(emit_engine(eng))
        self.q = {e: [] for e in ENGS}


from concourse.bass_core_rust import IndirectOffsetOnAxis

F32 = mybir.dt.float32
AF = mybir.ActivationFunctionType
ALU = mybir.AluOpType
AX = mybir.AxisListType

T, D, DI, DS, NH, HD, E, FF = 2048, 1024, 2048, 64, 16, 64, 8, 4096
NTT, NKD, NKI, NFF = T // 128, D // 128, DI // 128, FF // 128
LNEPS = 1e-5


class Chain:
    def __init__(self, P, start=()):
        self.P = P
        self.last = [e for e in start if e is not None]

    def do(self, eng, fn, extra=(), dma=False):
        ev = self.P.do(eng, fn, after=self.last + [e for e in extra if e is not None], dma=dma)
        self.last = [ev]
        return ev

    def dma(self, eng, out, in_, extra=()):
        return self.do(eng, lambda e: e.dma_start(out=out, in_=in_), extra=extra, dma=True)


class L1:
    def __init__(self, upto=99, dbg=(), sub=9, moe_mode="sorted"):
        self.nc = bass.Bass("TRN2", target_bir_lowering=False)
        self.P = Prog(self.nc)
        self.outs = []
        self.upto = upto
        self.sub = sub
        self.moe_mode = moe_mode
        self.dbgnames = dbg
        self.build()

    def din(self, name, shape):
        return self.nc.dram_tensor(name, shape, F32, kind="ExternalInput").ap()

    def dout(self, name, shape):
        return self.nc.dram_tensor(name, shape, F32, kind="ExternalOutput").ap()

    def dint(self, name, shape):
        return self.nc.dram_tensor(name, shape, F32, kind="Internal").ap()

    def phase_end(self):
        b = self.P.barrier()
        self.P.new_epoch()
        return b

    def odma(self, out, in_, after=()):
        ev = self.P.dma("gpsimd", out, in_, after=after)
        self.outs.append(ev)
        return ev

    def dbg(self, name, out_slice_fn, in_, after=()):
        if name not in self.dbt:
            return None
        ev = self.P.dma("gpsimd", out_slice_fn(self.dbt[name]), in_, after=after)
        self.outs.append(ev)
        return ev

    def build(self):
        nc, P = self.nc, self.P
        d = self.din
        self.xw = d("xw", [T, D])
        self.lng, self.lnb = d("lng", [D]), d("lnb", [D])
        self.w_in = d("w_in", [D, 2 * DI])
        self.conv_w, self.conv_b = d("conv_w", [DI, 4]), d("conv_b", [DI])
        self.w_x = d("w_x", [DI, 2 * DS])
        self.w_dt, self.b_dt = d("w_dt", [DI, DI]), d("b_dt", [DI])
        self.a_log, self.dvec = d("a_log", [DS]), d("dvec", [DI])
        self.w_out = d("w_out", [DI, D])
        self.wq, self.wk, self.wv, self.wo = (d(n, [D, D]) for n in ("wq", "wk", "wv", "wo"))
        self.bq, self.bk, self.bv, self.bo = (d(n, [D]) for n in ("bq", "bk", "bv", "bo"))
        self.l1g, self.l1b, self.l2g, self.l2b = (d(n, [D]) for n in ("l1g", "l1b", "l2g", "l2b"))
        self.wr = d("wr", [D, E])
        self.w1 = d("w1", [E, D, FF])
        self.b1 = d("b1", [E, FF])
        self.w2 = d("w2", [E, FF, D])

        self.preout = self.dout("preout", [T, D])
        if self.moe_mode == "dense":
            self.y1o = self.dout("y1o", [T, D])
            self.y2o = self.dout("y2o", [T, D])
        elif self.moe_mode == "sorted":
            self.y12o = self.dout("y12o", [2 * (T + 16) + 1, D])
        self.rowso = self.dout("rowso", [4, T])
        self.oh1o = self.dout("oh1o", [E, T])
        self.oh2o = self.dout("oh2o", [E, T])
        self.cnto = self.dout("cnto", [1, 2 * E])

        self.xn_d = self.dint("xn_d", [T, D])
        self.pre0_d = self.dint("pre0_d", [T, D])
        self.res_d = self.dint("res_d", [NKI, 128, T])
        self.xc_d = self.dint("xc_d", [NKI, 128, T])
        self.mam_d = self.dint("mam_d", [T, D])
        self.h1f_d = self.dint("h1f_d", [NKD, 128, T])
        self.attnf_d = self.dint("attnf_d", [NKD, 128, T])
        self.he_d = self.dint("he_d", [E, NFF, 128, T]) if self.moe_mode == "dense" else None
        self.h2tm_d = (self.dout if self.moe_mode == "expert" else self.dint)(
            "h2tm_d", [T + 1, D])
        self.srctok_d = self.dint("srctok_d", [1, 5248])
        self.dsttok_d = self.dint("dsttok_d", [1, 5248])

        shp = {"xn": [T, D], "pre0": [T, D], "xmf": [NKD, 128, T], "h1f": [NKD, 128, T],
               "xc": [NKI, 128, T], "res": [NKI, 128, T], "bc": [128, T],
               "delta": [1, T], "ad": [DS, T], "ys": [1, T], "mam": [T, D],
               "attnf": [NKD, 128, T], "x2": [T, D], "h2f": [NKD, 128, T],
               "lgf": [E, T], "he0": [NFF, 128, T], "cum1": [E, T], "cum2": [E, T],
               "rws": [4, T]}
        self.dbt = {n: self.dout("dbg_" + n, shp[n]) for n in self.dbgnames}

        A = nc.alloc_sbuf_tensor
        self.ident = A("ident", [128, 128], F32)
        self.iota_r = A("iota_r", [128, 128], F32)
        self.iota_c = A("iota_c", [128, 1], F32)
        self.ones_row = A("ones_row", [1, 128], F32)
        self.ones_col = A("ones_col", [128, 1], F32)
        self.negmean_col = A("negmean_col", [128, 1], F32)
        self.vrow = A("vrow", [1, D], F32)
        for n in ("s_sum", "s_ssq", "s_mu", "s_var", "s_rstd", "s_msq"):
            setattr(self, n, A(n, [128, 1], F32))
        self.cwc = A("cwc", [128, 64], F32)
        self.cbc = A("cbc", [128, 16], F32)
        self.nbdt = A("nbdt", [128, 16], F32)
        self.dcol = A("dcol", [128, 16], F32)
        self.aneg = A("aneg", [1, DS], F32)
        self.bqc = A("bqc", [128, 8], F32)
        self.bkc = A("bkc", [128, 8], F32)
        self.bv_row = A("bv_row", [1, D], F32)
        self.wrc = A("wrc", [128, NKD * E], F32)
        self.b1c = A("b1c", [128, E * NFF], F32)
        self.epsc = A("epsc", [128, 1], F32)
        self.stg = A("stg", [128, 2048], F32)
        self.stg2 = A("stg2", [128, 2048], F32)
        self.oh1tm = A("oh1tm", [128, 128], F32)
        self.eidx_i = A("eidx_i", [128, 40], mybir.dt.int32)
        self.eidx_fs = A("eidx_fs", [128, 40], F32)
        self.basew1 = A("basew1", [128, 8], F32)
        self.basew2 = A("basew2", [128, 32], F32)
        self.oh2tm = A("oh2tm", [128, 128], F32)

        self.psA = nc.alloc_psum_tensor("psA", [128, 2048], F32)
        self.psB = nc.alloc_psum_tensor("psB", [128, 2048], F32)

        self.ph_init()
        with nc.sbuf_tensor("xmf", [128, NKD * T], F32) as xmf:
            self.xmf3 = xmf.ap().rearrange("p (c t) -> p c t", c=NKD)
            self.ph_p0()
            if self.upto >= 1:
                self.ph_m1()
        if self.upto >= 1 and self.sub >= 2:
            with nc.sbuf_tensor("xca", [128, NKI * 2048], F32) as xca, \
                 nc.sbuf_tensor("ysr", [128, T], F32) as ysr:
                self.xca3 = xca.ap().rearrange("p (c t) -> p c t", c=NKI)
                self.ysr = ysr
                self.ph_m34()
            self.bar = self.phase_end()
        if self.upto >= 2:
            self.ph_attn()
        if self.upto >= 3:
            with nc.sbuf_tensor("h2a", [128, NKD * T], F32) as h2a:
                self.h2a = h2a
                self.h2a3 = h2a.ap().rearrange("p (c t) -> p c t", c=NKD)
                self.ph_a3()
                if self.upto >= 4:
                    self.ph_routing()
                if self.moe_mode == "dense" and self.upto >= 5:
                    self.ph_moe1()
            if self.upto >= 5 and self.moe_mode == "sorted":
                self.ph_moes()
            if self.moe_mode == "dense" and self.upto >= 6:
                self.ph_moe2()
        P.do("sync", lambda e: e.nop(), after=self.outs)
        P.finalize()

    def ph_init(self):
        P, nc = self.P, self.nc
        ev_ir = P.do("gpsimd", lambda e: e.iota(self.iota_r.ap(), pattern=[[1, 128]], base=0,
                                                channel_multiplier=0, allow_small_or_imprecise_dtypes=True))
        ev_ic = P.do("gpsimd", lambda e: e.iota(self.iota_c.ap(), pattern=[[0, 1]], base=0,
                                                channel_multiplier=1, allow_small_or_imprecise_dtypes=True))
        ev_m1 = P.do("vector", lambda e: e.memset(self.ones_row[:1, :], 1.0))
        ev_m2 = P.do("vector", lambda e: e.memset(self.ones_col[:, :], 1.0))
        ev_m3 = P.do("vector", lambda e: e.memset(self.negmean_col[:, :], -1.0 / DI))
        ev_m4 = P.do("vector", lambda e: e.memset(self.epsc[:, :], LNEPS))
        self.ev_ident = P.do("vector", lambda e: e.tensor_single_scalar(
            self.ident[:, :], self.iota_r[:, :], self.iota_c[:, :1], ALU.is_equal),
            after=[ev_ir, ev_ic])
        self.ev_const = [ev_m1, ev_m2, ev_m3, ev_m4, self.ev_ident]

        ld = lambda out, in_, after=(): P.dma("sync", out, in_, after=after,
                                              allow_slow_non_contiguous=True)
        e_nb = ld(self.nbdt.ap().rearrange("p (c o) -> p c o", o=1),
                  self.b_dt.rearrange("(c p o) -> p c o", p=128, o=1))
        e_an = ld(self.aneg[:1, :], self.a_log.rearrange("(o d) -> o d", o=1))
        ld(self.cwc.ap().rearrange("p (c k) -> p c k", k=4),
           self.conv_w.rearrange("(c p) k -> p c k", p=128))
        ld(self.cbc.ap().rearrange("p (c o) -> p c o", o=1), self.conv_b.rearrange("(c p o) -> p c o", p=128, o=1))
        ld(self.dcol.ap().rearrange("p (c o) -> p c o", o=1), self.dvec.rearrange("(c p o) -> p c o", p=128, o=1))
        ld(self.bqc.ap().rearrange("p (c o) -> p c o", o=1), self.bq.rearrange("(c p o) -> p c o", p=128, o=1))
        ld(self.bkc.ap().rearrange("p (c o) -> p c o", o=1), self.bk.rearrange("(c p o) -> p c o", p=128, o=1))
        ld(self.bv_row[:1, :], self.bv.rearrange("(o d) -> o d", o=1))
        ld(self.wrc.ap().rearrange("p (c e) -> p c e", e=E),
           self.wr.rearrange("(c p) e -> p c e", p=128))
        ld(self.b1c.ap().rearrange("p (e m) -> p e m", e=E),
           self.b1.rearrange("e (m p) -> p e m", p=128))
        self.ev_small = P.barrier()
        self.ev_nbdt = P.do("vector", lambda e: e.tensor_scalar_mul(self.nbdt[:, :], self.nbdt[:, :], -1.0),
                            after=[e_nb])
        e1 = P.do("scalar", lambda e: e.activation(out=self.aneg[:1, :], in_=self.aneg[:1, :], func=AF.Exp),
                  after=[e_an])
        self.ev_aneg = P.do("scalar", lambda e: e.mul(self.aneg[:1, :], self.aneg[:1, :], -1.0), after=[e1])

    def replicate(self, dst, src_dram, ncols, after=()):
        P = self.P
        c = Chain(P, start=list(after) + self.ev_const)
        c.dma("sync", self.vrow[:1, 0:ncols], src_dram.rearrange("(o d) -> o d", o=1))

        def mm(e):
            r = None
            for s0 in range(0, ncols, 512):
                r = e.matmul(self.psB[:, s0:min(s0 + 512, ncols)], self.ones_row[:1, 0:128],
                             self.vrow[:1, s0:min(s0 + 512, ncols)], start=True, stop=True)
            return r
        c.do("tensor", mm)
        c.do("scalar", lambda e: e.copy(dst[:, 0:ncols], self.psB[:, 0:ncols]))
        return c.last[0]

    def ph_p0(self):
        P, nc = self.P, self.nc
        xmf3 = self.xmf3
        with nc.sbuf_tensor("repA", [128, D], F32) as repA, \
             nc.sbuf_tensor("repB", [128, D], F32) as repB, \
             nc.sbuf_tensor("repC", [128, D], F32) as repC, \
             nc.sbuf_tensor("repD", [128, D], F32) as repD, \
             nc.sbuf_tensor("xt0", [128, D], F32) as xt0, \
             nc.sbuf_tensor("xt1", [128, D], F32) as xt1, \
             nc.sbuf_tensor("t2t", [128, D], F32) as t2t, \
             nc.sbuf_tensor("t3t", [128, D], F32) as t3t, \
             nc.sbuf_tensor("xmt", [128, D], F32) as xmt, \
             nc.sbuf_tensor("h1t", [128, D], F32) as h1t, \
             nc.sbuf_tensor("sqd", [128, D], F32) as sqd:
            rA = self.replicate(repA, self.lng, D)
            rB = self.replicate(repB, self.lnb, D, after=[rA])
            rC = self.replicate(repC, self.l1g, D, after=[rB])
            rD = self.replicate(repD, self.l1b, D, after=[rC])
            c = Chain(P, start=[rD] + self.ev_const)
            h1f_pct = self.h1f_d.rearrange("c p t -> p c t")
            for tt in range(NTT):
                xb = xt0 if tt % 2 == 0 else xt1
                tile = slice(tt * 128, (tt + 1) * 128)
                e_x = P.dma("sync", xb[:, :], self.xw[tile, :],
                            after=c.last if tt >= 2 else [])
                S, V = "scalar", "vector"
                c.do(S, lambda e, xb=xb: e.activation(out=sqd[:, :], in_=xb[:, :], func=AF.Square,
                                                      accum_out=self.s_ssq[:, :1]), extra=[e_x])
                c.do(V, lambda e, xb=xb: e.reduce_sum(self.s_sum[:, :1], xb[:, :], axis=AX.X))
                c.do(V, lambda e: e.tensor_scalar_mul(self.s_mu[:, :1], self.s_sum[:, :1], 1.0 / D))
                c.do(V, lambda e: e.tensor_mul(self.s_msq[:, :1], self.s_mu[:, :1], self.s_mu[:, :1]))
                c.do(V, lambda e: e.scalar_tensor_tensor(self.s_var[:, :1], self.s_ssq[:, :1], 1.0 / D,
                                                         self.s_msq[:, :1], ALU.mult, ALU.subtract))
                c.do(S, lambda e: e.activation(out=self.s_var[:, :1], in_=self.s_var[:, :1],
                                               func=AF.Sqrt, bias=self.epsc[:, :1]))
                c.do(V, lambda e: e.reciprocal(self.s_rstd[:, :1], self.s_var[:, :1]))
                c.do(V, lambda e, xb=xb: e.tensor_scalar(t2t[:, :], xb[:, :], self.s_mu[:, :1],
                                                         self.s_rstd[:, :1], ALU.subtract, ALU.mult))
                c.do(V, lambda e, xb=xb: e.scalar_tensor_tensor(t3t[:, :], t2t[:, :], 2.0, xb[:, :],
                                                                ALU.mult, ALU.add))
                e_xnsp = P.dma("gpsimd", self.xn_d[tile, :], t2t[:, :], after=c.last)
                e_p0sp = P.dma("gpsimd", self.pre0_d[tile, :], t3t[:, :], after=c.last)
                self.dbg("xn", lambda t: t[tile, :], t2t[:, :], after=c.last)
                self.dbg("pre0", lambda t: t[tile, :], t3t[:, :], after=c.last)
                c.do(S, lambda e: e.activation(out=sqd[:, :], in_=t2t[:, :], func=AF.Square,
                                               accum_out=self.s_ssq[:, :1]))
                c.do(V, lambda e: e.reduce_sum(self.s_sum[:, :1], t2t[:, :], axis=AX.X))
                c.do(V, lambda e: e.tensor_scalar_mul(self.s_mu[:, :1], self.s_sum[:, :1], 1.0 / D))
                c.do(V, lambda e: e.tensor_mul(self.s_msq[:, :1], self.s_mu[:, :1], self.s_mu[:, :1]))
                c.do(V, lambda e: e.scalar_tensor_tensor(self.s_var[:, :1], self.s_ssq[:, :1], 1.0 / D,
                                                         self.s_msq[:, :1], ALU.mult, ALU.subtract))
                c.do(S, lambda e: e.activation(out=self.s_var[:, :1], in_=self.s_var[:, :1],
                                               func=AF.Sqrt, bias=self.epsc[:, :1]))
                c.do(V, lambda e: e.reciprocal(self.s_rstd[:, :1], self.s_var[:, :1]))
                c.do(V, lambda e: e.tensor_scalar(xmt[:, :], t2t[:, :], self.s_mu[:, :1],
                                                  self.s_rstd[:, :1], ALU.subtract, ALU.mult),
                     extra=[e_xnsp])
                c.do(V, lambda e: e.tensor_mul(h1t[:, :], xmt[:, :], repC[:, :]))
                c.do(V, lambda e: e.tensor_add(h1t[:, :], h1t[:, :], repD[:, :]))
                c.do(V, lambda e: e.tensor_mul(xmt[:, :], xmt[:, :], repA[:, :]))
                c.do(V, lambda e: e.tensor_add(xmt[:, :], xmt[:, :], repB[:, :]))

                def tps(src, ps):
                    def f(e):
                        r = None
                        for cc in range(NKD):
                            r = e.transpose(ps[:, cc * 128:(cc + 1) * 128],
                                            src[:, cc * 128:(cc + 1) * 128], self.ident[:, :])
                        return r
                    return f
                c.do("tensor", tps(xmt, self.psB))
                c.do(S, lambda e, tt=tt: e.copy(xmf3[:, :, tt * 128:(tt + 1) * 128], self.psB[:, 0:D]))
                self.dbg("xmf", lambda t, tt=tt: t.rearrange("c p t -> p c t")[:, :, tt * 128:(tt + 1) * 128],
                         xmf3[:, :, tt * 128:(tt + 1) * 128], after=c.last)
                c.do("tensor", tps(h1t, self.psA))
                c.do(S, lambda e: e.copy(self.stg2[:, 0:D], self.psA[:, 0:D]))
                c.do("gpsimd", lambda e, tt=tt: e.dma_start(
                    out=h1f_pct[:, :, tt * 128:(tt + 1) * 128],
                    in_=self.stg2[:, 0:1024].rearrange("p (c t) -> p c t", c=NKD)), dma=True)
                self.dbg("h1f", lambda t, tt=tt: t.rearrange("c p t -> p c t")[:, :, tt * 128:(tt + 1) * 128],
                         self.stg2[:, 0:1024].rearrange("p (c t) -> p c t", c=NKD), after=c.last)
                if tt == NTT - 1:
                    self.ev_xn_sp, self.ev_p0_sp = e_xnsp, e_p0sp
            self.ev_p0 = c.last[0]
            self.bar = self.phase_end()

    def ph_m1(self):
        P, nc = self.P, self.nc
        xmf3 = self.xmf3
        ev_xc_sp = self.ev_xc_sp = [None] * NKI
        ev_res_sp = self.ev_res_sp = [None] * NKI
        with nc.sbuf_tensor("winb", [128, NKD * 2048], F32) as winb, \
             nc.sbuf_tensor("xpb", [128, 2051], F32) as xpb, \
             nc.sbuf_tensor("cac", [128, 2048], F32) as cac, \
             nc.sbuf_tensor("xcb", [128, 2048], F32) as xcb, \
             nc.sbuf_tensor("resb", [128, 2048], F32) as resb:
            winb3 = winb.ap().rearrange("p (c m) -> p c m", c=NKD)
            e_pad = P.do("vector", lambda e: e.memset(xpb[:, 0:3], 0.0), after=self.bar)
            ev_psfree = {0: None, 1: None}
            e_evac = e_c4 = e_si = e_sp = e_rsi = e_rsp = None
            e_wl = None
            e_mm = None
            for half in range(2):
                e_wl_parts = []
                for cc in range(NKD):
                    e_wl_parts.append(P.dma(
                        "sync", winb3[:, cc, :],
                        self.w_in[cc * 128:(cc + 1) * 128, half * 2048:(half + 1) * 2048],
                        after=(self.bar if half == 0 else [e_wl, e_mm])))
                e_wl = e_wl_parts[-1]
                for ml in range(16):
                    m = half * 16 + ml
                    psX = self.psA if m % 2 == 0 else self.psB

                    def mmf(e, psX=psX, ml=ml):
                        r = None
                        for s in range(4):
                            for cc in range(NKD):
                                r = e.matmul(psX[:, s * 512:(s + 1) * 512],
                                             winb3[:, cc, ml * 128:(ml + 1) * 128],
                                             xmf3[:, cc, s * 512:(s + 1) * 512],
                                             start=(cc == 0), stop=(cc == NKD - 1))
                        return r
                    e_mm = P.do("tensor", mmf,
                                after=e_wl_parts + [ev_psfree[m % 2]] + (self.bar if m == 0 else []))
                    if half == 0:
                        e_evac = P.do("scalar", lambda e, psX=psX: e.copy(xpb[:, 3:2051], psX[:, :]),
                                      after=[e_mm, e_c4, e_pad])
                        e_c1 = P.do("vector", lambda e, m=m: e.tensor_single_scalar(
                            cac[:, :], xpb[:, 0:2048], self.cwc[:, m * 4:m * 4 + 1], ALU.mult),
                            after=[e_evac, e_si] + self.ev_small)
                        e_ck = e_c1
                        for k in (1, 2, 3):
                            e_ck = P.do("vector", lambda e, m=m, k=k: e.scalar_tensor_tensor(
                                cac[:, :], xpb[:, k:2048 + k], self.cwc[:, m * 4 + k:m * 4 + k + 1],
                                cac[:, :], ALU.mult, ALU.add), after=[e_ck])
                        e_c4 = e_ck
                        e_si = P.do("scalar", lambda e, m=m: e.activation(
                            out=xcb[:, :], in_=cac[:, :], func=AF.Silu,
                            bias=self.cbc[:, m:m + 1]), after=[e_c4, e_sp])
                        e_sp = P.dma("gpsimd", self.xc_d[m, :, :], xcb[:, :], after=[e_si])
                        self.dbg("xc", lambda t, m=m: t[m, :, :], xcb[:, :], after=[e_sp])
                        ev_xc_sp[m] = e_sp
                        ev_psfree[m % 2] = e_evac
                    else:
                        e_rsi = P.do("scalar", lambda e, psX=psX: e.activation(
                            out=resb[:, :], in_=psX[:, :], func=AF.Silu), after=[e_mm, e_rsp])
                        e_rsp = P.dma("gpsimd", self.res_d[m - 16, :, :], resb[:, :], after=[e_rsi])
                        self.dbg("res", lambda t, m=m: t[m - 16, :, :], resb[:, :], after=[e_rsp])
                        ev_res_sp[m - 16] = e_rsp
                        ev_psfree[m % 2] = e_rsi
            self.bar = self.phase_end()

    def ph_m34(self):
        P, nc = self.P, self.nc
        bar1 = self.bar
        ev_xc_sp = self.ev_xc_sp
        xca3 = self.xca3
        with nc.sbuf_tensor("wxb", [128, NKI * 128], F32) as wxb, \
             nc.sbuf_tensor("wdt0", [128, 2048], F32) as wdt0, \
             nc.sbuf_tensor("wdt1", [128, 2048], F32) as wdt1, \
             nc.sbuf_tensor("sgb", [128, 2048], F32) as sgb:
            bcsb = self.stg2
            adb = sgb
            hsb = wdt0
            dlr = wdt1
            ysw = wxb
            e_xca = [P.dma("sync", xca3[:, k, :], self.xc_d[k, :, :],
                           after=[ev_xc_sp[k]] + bar1) for k in range(NKI)]
            e_wx = [P.dma("sync", wxb[:, k * 128:(k + 1) * 128],
                          self.w_x[k * 128:(k + 1) * 128, :], after=bar1) for k in range(NKI)]

            def bcmm(e):
                r = None
                for s in range(4):
                    for k in range(NKI):
                        r = e.matmul(self.psA[:, s * 512:(s + 1) * 512], wxb[:, k * 128:(k + 1) * 128],
                                     xca3[:, k, s * 512:(s + 1) * 512],
                                     start=(k == 0), stop=(k == NKI - 1))
                return r
            e_bcmm = P.do("tensor", bcmm, after=e_xca + e_wx + bar1)
            e_bc = P.do("scalar", lambda e: e.copy(bcsb[:, :], self.psA[:, :]), after=[e_bcmm])
            self.dbg("bc", lambda t: t, bcsb[:, :], after=[e_bc])
            wdt_pct = self.w_dt.rearrange("(kc p) m -> p kc m", p=128)
            e_sg = e_ln = None
            e_dacc = None
            e_wdt = {0: None, 1: None}
            e_mm_prev = e_bc
            for m in range(NKI):
                wb = wdt0 if m % 2 == 0 else wdt1
                wb3 = wb.ap().rearrange("p (kc m) -> p kc m", kc=NKI)
                e_l0 = P.dma("sync", wb3[:, 0:8, :], wdt_pct[:, 0:8, m * 128:(m + 1) * 128],
                             after=[e_wdt[m % 2]])
                e_l1 = P.dma("sync", wb3[:, 8:16, :], wdt_pct[:, 8:16, m * 128:(m + 1) * 128],
                             after=[e_wdt[m % 2]])

                def dtmm(e, wb3=wb3):
                    r = None
                    for s in range(4):
                        for k in range(NKI):
                            r = e.matmul(self.psA[:, s * 512:(s + 1) * 512], wb3[:, k, :],
                                         xca3[:, k, s * 512:(s + 1) * 512],
                                         start=(k == 0), stop=(k == NKI - 1))
                    return r
                e_mm = P.do("tensor", dtmm, after=[e_l0, e_l1, e_sg, e_mm_prev])
                e_wdt[m % 2] = e_mm
                e_mm_prev = e_mm
                e_sg = P.do("scalar", lambda e, m=m: e.activation(
                    out=sgb[:, :], in_=self.psA[:, :], func=AF.Sigmoid, scale=-1.0,
                    bias=self.nbdt[:, m:m + 1]), after=[e_mm, e_ln, self.ev_nbdt])
                e_ln = P.do("scalar", lambda e: e.activation(out=self.stg[:, :], in_=sgb[:, :],
                                                             func=AF.Ln), after=[e_sg, e_dacc])

                def dmm(e, m=m):
                    r = None
                    for s in range(4):
                        r = e.matmul(self.psB[0:1, s * 512:(s + 1) * 512], self.negmean_col[:, :1],
                                     self.stg[:, s * 512:(s + 1) * 512],
                                     start=(m == 0), stop=(m == NKI - 1))
                    return r
                e_dacc = P.do("tensor", dmm, after=[e_ln])
            e_dl = P.do("scalar", lambda e: e.copy(dlr[0:1, :], self.psB[0:1, :]), after=[e_dacc, e_mm_prev])
            self.dbg("delta", lambda t: t, dlr[0:1, :], after=[e_dl])

            def admm(e):
                r = None
                for s in range(4):
                    r = e.matmul(self.psA[0:DS, s * 512:(s + 1) * 512], self.aneg[:1, :],
                                 dlr[0:1, s * 512:(s + 1) * 512], start=True, stop=True)
                return r
            e_admm = P.do("tensor", admm, after=[e_dl, e_sg, self.ev_aneg])
            e_ad = P.do("scalar", lambda e: e.activation(out=adb[0:DS, :], in_=self.psA[0:DS, :],
                                                         func=AF.Exp), after=[e_admm, e_ln])
            self.dbg("ad", lambda t: t, adb[0:DS, :], after=[e_ad])
            e_scan = P.do("vector", lambda e: e.tensor_tensor_scan(
                out=hsb[0:DS, :], data0=adb[0:DS, :], data1=bcsb[0:DS, :], initial=0.0,
                op0=ALU.mult, op1=ALU.add), after=[e_ad, e_bc, e_mm_prev])
            e_bcC = P.dma("gpsimd", wxb[0:DS, :], bcsb[DS:128, :], after=[e_bc, e_bcmm])
            e_hc = P.do("vector", lambda e: e.tensor_mul(adb[0:DS, :], hsb[0:DS, :], wxb[0:DS, :]),
                        after=[e_scan, e_ad, e_bcC])

            def ysmm(e):
                r = None
                for s in range(4):
                    r = e.matmul(self.psB[0:1, s * 512:(s + 1) * 512], self.ones_col[0:DS, :1],
                                 adb[0:DS, s * 512:(s + 1) * 512], start=True, stop=True)
                return r
            e_ysmm = P.do("tensor", ysmm, after=[e_hc, e_dl])
            e_ys = P.do("scalar", lambda e: e.copy(ysw[0:1, :], self.psB[0:1, :]), after=[e_ysmm, e_hc])
            self.dbg("ys", lambda t: t, ysw[0:1, :], after=[e_ys])

            def ysrepmm(e):
                r = None
                for s in range(4):
                    r = e.matmul(self.psA[:, s * 512:(s + 1) * 512], self.ones_row[:1, 0:128],
                                 ysw[0:1, s * 512:(s + 1) * 512], start=True, stop=True)
                return r
            e_ysrmm = P.do("tensor", ysrepmm, after=[e_ys, e_ad])
            e_ysr = P.do("scalar", lambda e: e.copy(self.ysr[:, :], self.psA[:, :]), after=[e_ysrmm])
            bar2 = P.barrier()
            self.e_wdt_done = e_mm_prev
            self.e_ysr = e_ysr
        if self.sub >= 3:
            with nc.sbuf_tensor("woutb", [128, 4 * 1024], F32) as woutb, \
                 nc.sbuf_tensor("resb2", [128, 2048], F32) as resb2:
                woutb3 = woutb.ap().rearrange("p (j n) -> p j n", j=4)
                e_y = [None] * NKI
                e_rld = None
                for m in range(NKI):
                    e_rld = P.dma("sync", resb2[:, :], self.res_d[m, :, :],
                                  after=[self.ev_res_sp[m], e_y[m - 1] if m else self.e_wdt_done] + bar2)
                    e_y1 = P.do("vector", lambda e, m=m: e.scalar_tensor_tensor(
                        xca3[:, m, :], xca3[:, m, :], self.dcol[:, m:m + 1], self.ysr[:, :],
                        ALU.mult, ALU.add), after=[self.e_ysr, self.e_wdt_done] + ([e_y[m - 1]] if m else []))
                    e_y[m] = P.do("vector", lambda e, m=m: e.tensor_mul(xca3[:, m, :], xca3[:, m, :],
                                                                       resb2[:, :]), after=[e_y1, e_rld])
                if self.sub >= 4:
                    ps_slot = [(self.psA, 0), (self.psA, 1024), (self.psB, 0), (self.psB, 1024)]
                    e_wld_last = None
                    e_evac = {}
                    prev_mms = []
                    for ttg in range(4):
                        e_mm_grp = {}
                        for g in range(4):
                            e_wld = []
                            for j in range(4):
                                e_wld.append(P.dma("sync", woutb3[:, j, :],
                                                   self.w_out[(g * 4 + j) * 128:(g * 4 + j + 1) * 128, :],
                                                   after=[e_wld_last] + prev_mms))
                            e_wld_last = e_wld[-1]
                            for tl in range(4):
                                tt = ttg * 4 + tl
                                ps, off = ps_slot[tl]

                                def wmm(e, g=g, tl=tl, tt=tt, ps=ps, off=off):
                                    r = None
                                    for j in range(4):
                                        kk = g * 4 + j
                                        for h2 in range(2):
                                            r = e.matmul(ps[:, off + h2 * 512: off + (h2 + 1) * 512],
                                                         xca3[:, kk, tt * 128:(tt + 1) * 128],
                                                         woutb3[:, j, h2 * 512:(h2 + 1) * 512],
                                                         start=(g == 0 and j == 0), stop=(g == 3 and j == 3))
                                    return r
                                e_mm_grp[tl] = P.do("tensor", wmm,
                                                    after=e_wld[-1:] + e_y + [e_evac.get((ttg - 1, tl))])
                            prev_mms = list(e_mm_grp.values())
                        e_prev_sp = e_evac.get((ttg - 1, 3), None)
                        for tl in range(4):
                            tt = ttg * 4 + tl
                            ps, off = ps_slot[tl]
                            e_ev = P.do("scalar", lambda e, ps=ps, off=off: e.copy(self.stg[:, 0:1024],
                                                                                  ps[:, off:off + 1024]),
                                        after=[e_mm_grp[tl], e_prev_sp])
                            e_spm = P.dma("gpsimd", self.mam_d[tt * 128:(tt + 1) * 128, :], self.stg[:, 0:1024],
                                          after=[e_ev])
                            e_dbgm = self.dbg("mam", lambda t, tt=tt: t[tt * 128:(tt + 1) * 128, :],
                                              self.stg[:, 0:1024], after=[e_spm])
                            e_prev_sp = e_dbgm or e_spm
                            e_evac[(ttg, tl)] = e_prev_sp
                    self.ev_mam_sp = list(e_evac.values())
                else:
                    self.ev_mam_sp = [e_y[-1]]
        self.bar = self.phase_end()
        self.ev_xc_done = None

    def ph_attn(self):
        P, nc = self.P, self.nc
        bar = self.bar
        attnf_pct = self.attnf_d.rearrange("c p t -> p c t")
        with nc.sbuf_tensor("h1a", [128, NKD * T], F32) as h1a, \
             nc.sbuf_tensor("qsb", [128, T], F32) as qsb, \
             nc.sbuf_tensor("ksb", [128, T], F32) as ksb, \
             nc.sbuf_tensor("vsb", [128, 8 * NTT * (HD + 1)], F32) as vsb, \
             nc.sbuf_tensor("pb0", [128, 2048], F32) as pb0, \
             nc.sbuf_tensor("pb1", [128, 2048], F32) as pb1, \
             nc.sbuf_tensor("whb0", [128, 2 * NKD * 128], F32) as whb0, \
             nc.sbuf_tensor("whb1", [128, 2 * NKD * 128], F32) as whb1, \
             nc.sbuf_tensor("wvb", [128, NKD * 512], F32) as wvb, \
             nc.sbuf_tensor("negch", [128, 1], F32) as negch, \
             nc.sbuf_tensor("rrow", [32, 512], F32) as rrow, \
             nc.sbuf_tensor("repsb", [64, 512], F32) as repsb, \
             nc.sbuf_tensor("ob", [64, 512], F32) as ob, \
             nc.sbuf_tensor("mx1", [32, 1], F32) as mx1, \
             nc.sbuf_tensor("mx2", [1, 1], F32) as mx2:
            h1a3 = h1a.ap().rearrange("p (c t) -> p c t", c=NKD)
            vsb4 = vsb.ap().rearrange("p (h tt f) -> p h tt f", h=8, tt=NTT)
            wvb3 = wvb.ap().rearrange("p (c n) -> p c n", c=NKD)
            e_h1l = [P.dma("sync", h1a3[:, c, :], self.h1f_d[c, :, :], after=bar, slot=0)
                     for c in range(NKD)]
            e_h1 = e_h1l[-1]
            e_vones = P.do("vector", lambda e: e.memset(vsb[:, :], 1.0), after=bar)
            P.do("vector", lambda e: e.memset(mx1[:, :], 0.0), after=bar)
            P.do("vector", lambda e: e.memset(rrow[:, :], 0.0), after=bar)
            pbs = (pb0, pb1)
            e_exp = {0: None, 1: None}
            e_ex_prev = None
            e_obv_prev = None
            e_pv_stop = None
            e_rep_ev = None
            e_rpc = None
            e_ob_dma = None
            e_obv_last = None
            e_km_prev = {0: None, 1: None}
            e_wvl_prev = None
            e_vm_last = None
            for g in range(2):
                e_wvl = P.dma("sync", wvb3[:, :, :],
                              self.wv.rearrange("(c p) m -> p c m", p=128)[:, :, g * 512:(g + 1) * 512],
                              after=[e_vm_last] + (bar if g == 0 else []), slot=2)
                e_wvl_prev = e_wvl
                e_ve = {0: None, 1: None}
                e_obv_gate = e_obv_last
                for tt in range(NTT):
                    pbk = (tt % 2) * 512

                    def vmm(e, tt=tt, pbk=pbk, g=g):
                        r = None
                        for cc in range(NKD):
                            r = e.matmul(self.psB[:, pbk:pbk + 512],
                                         h1a3[:, cc, tt * 128:(tt + 1) * 128],
                                         wvb3[:, cc, :], start=(cc == 0), stop=False)
                        return e.matmul(self.psB[:, pbk:pbk + 512], self.ones_row[:1, 0:128],
                                        self.bv_row[:1, g * 512:(g + 1) * 512],
                                        start=False, stop=True)
                    e_vm = P.do("tensor", vmm,
                                after=[e_wvl, e_h1, e_ve[tt % 2], e_obv_gate if tt <= 1 else None])
                    e_ve[tt % 2] = P.do("scalar", lambda e, tt=tt, pbk=pbk: e.copy(
                        vsb4[:, :, tt, 0:HD],
                        self.psB[:, pbk:pbk + 512].rearrange("p (h f) -> p h f", h=8)),
                        after=[e_vm, e_vones, e_pv_stop if tt <= 1 else None])
                e_vm_last = e_vm
                e_ve_last = e_ve[(NTT - 1) % 2]
                for j in range(4):
                    jj = g * 4 + j
                    whb = (whb0, whb1)[jj % 2]
                    whb3 = whb.ap().rearrange("p (w c n) -> p w c n", w=2, c=NKD)
                    e_whl = P.dma("sync", whb3[:, 0, :, :],
                                  self.wq.rearrange("(c p) m -> p c m", p=128)[:, :, jj * 128:(jj + 1) * 128],
                                  after=[e_km_prev[jj % 2]] + (bar if jj == 0 else []),
                                  slot=3 + jj % 2)
                    e_whl2 = P.dma("sync", whb3[:, 1, :, :],
                                   self.wk.rearrange("(c p) m -> p c m", p=128)[:, :, jj * 128:(jj + 1) * 128],
                                   after=[], slot=3 + jj % 2)

                    def qkmm(e, wi, whb3=whb3):
                        def f(ee):
                            r = None
                            for s4 in range(4):
                                for cc in range(NKD):
                                    r = ee.matmul(self.psA[:, s4 * 512:(s4 + 1) * 512],
                                                  whb3[:, wi, cc, :],
                                                  h1a3[:, cc, s4 * 512:(s4 + 1) * 512],
                                                  start=(cc == 0), stop=(cc == NKD - 1))
                            return r
                        return f
                    e_qm = P.do("tensor", qkmm(None, 0),
                                after=[e_whl2, e_h1, e_ex_prev, e_ve_last])
                    e_qe = P.do("vector", lambda e, jj=jj: e.tensor_scalar(
                        qsb[:, :], self.psA[:, :], self.bqc[:, jj:jj + 1], 0.125,
                        ALU.add, ALU.mult), after=[e_qm])
                    e_km = P.do("tensor", qkmm(None, 1), after=[e_qe])
                    e_km_prev[jj % 2] = e_km
                    e_ke = P.do("vector", lambda e, jj=jj: e.tensor_single_scalar(
                        ksb[:, :], self.psA[:, :], self.bkc[:, jj:jj + 1],
                        ALU.add), after=[e_km])
                    for s in range(2):
                        h = g * 8 + 2 * j + s
                        p0 = s * 64
                        e_sq = P.do("scalar", lambda e, p0=p0: e.activation(
                            out=self.stg[p0:p0 + 64, :], in_=qsb[p0:p0 + 64, :],
                            func=AF.Square), after=[e_qe, e_rep_ev])
                        e_sk = P.do("scalar", lambda e, p0=p0: e.activation(
                            out=self.stg2[p0:p0 + 64, :], in_=ksb[p0:p0 + 64, :],
                            func=AF.Square), after=[e_ke])

                        def nrm(e, src, row, p0=p0):
                            def f(ee):
                                r = None
                                for s4 in range(4):
                                    r = ee.matmul(self.psA[row:row + 1, s4 * 512:(s4 + 1) * 512],
                                                  self.ones_col[p0:p0 + 64, :1],
                                                  src[p0:p0 + 64, s4 * 512:(s4 + 1) * 512],
                                                  start=True, stop=True)
                                return r
                            return f
                        e_nq = P.do("tensor", nrm(None, self.stg, 0),
                                    after=[e_sq, e_ke, e_ex_prev])
                        e_nk = P.do("tensor", nrm(None, self.stg2, 32), after=[e_sk])
                        e_mq = P.do("vector", lambda e: e.reduce_max(
                            mx1[:1, :1], self.psA[0:1, :], axis=AX.X), after=[e_nq, e_nk])
                        e_mk = P.do("vector", lambda e: e.reduce_max(
                            mx2[:1, :1], self.psA[32:33, :], axis=AX.X), after=[e_mq])
                        e_s1 = P.do("scalar", lambda e: e.activation(out=mx1[:1, :1], in_=mx1[:1, :1],
                                                                     func=AF.Sqrt), after=[e_mq])
                        e_s2 = P.do("scalar", lambda e: e.activation(out=mx2[:1, :1], in_=mx2[:1, :1],
                                                                     func=AF.Sqrt), after=[e_mk])
                        e_ch = P.do("vector", lambda e: e.tensor_mul(mx1[:1, :1], mx1[:1, :1], mx2[:1, :1]),
                                    after=[e_s1, e_s2])
                        e_chn = P.do("vector", lambda e: e.tensor_scalar_mul(mx1[:1, :1], mx1[:1, :1], -1.0),
                                     after=[e_ch])

                        e_chc = e_chn
                        for qd in range(4):
                            e_chc = P.do("vector", lambda e, qd=qd: e.stream_shuffle(
                                negch[qd * 32:(qd + 1) * 32, 0:1], mx1[0:32, 0:1], [0] * 32),
                                after=[e_chc, e_ex_prev])
                        hh = 2 * j + s
                        for qs in range(4):
                            for sb in range(4):
                                reg = sb % 2

                                def smm4(e, sb=sb, qs=qs, p0=p0):
                                    r = None
                                    for i in range(4):
                                        kt = sb * 4 + i
                                        r = e.matmul(self.psA[:, i * 512:(i + 1) * 512],
                                                     ksb[p0:p0 + 64, kt * 128:(kt + 1) * 128],
                                                     qsb[p0:p0 + 64, qs * 512:(qs + 1) * 512],
                                                     start=True, stop=True)
                                    return r
                                e_sc = P.do("tensor", smm4,
                                            after=[e_chc if (qs == 0 and sb == 0) else None,
                                                   e_ex_prev])
                                e_ex = P.do("scalar", lambda e, reg=reg: e.activation(
                                    out=pbs[reg][:, :], in_=self.psA[:, :],
                                    func=AF.Exp, bias=negch[:, :1]),
                                    after=[e_sc, e_pv_stop if sb <= 1 else e_exp[reg]])
                                e_pv_prev = P.do("tensor", lambda e, sb=sb, qs=qs, reg=reg, hh=hh: [
                                    e.matmul(self.psB[0:65, qs * 512:(qs + 1) * 512],
                                             vsb4[:, hh, sb * 4 + i, 0:HD + 1],
                                             pbs[reg][:, i * 512:(i + 1) * 512],
                                             start=(sb == 0 and i == 0),
                                             stop=(sb == 3 and i == 3)) for i in range(4)][-1],
                                    after=[e_ex, e_ve_last] + ([e_rep_ev] if sb == 0 else []))
                                e_exp[reg] = e_pv_prev
                                e_ex_prev = e_ex
                            e_pv_stop = e_pv_prev
                            e_rc = P.do("vector", lambda e, qs=qs: e.reciprocal(
                                rrow[:1, :], self.psB[64:65, qs * 512:(qs + 1) * 512]), after=[e_pv_stop])
                            e_rp1 = P.do("vector", lambda e: e.stream_shuffle(
                                repsb[0:32, :], rrow[0:32, :], [0] * 32), after=[e_rc, e_obv_prev])
                            e_rpc = P.do("vector", lambda e: e.stream_shuffle(
                                repsb[32:64, :], rrow[0:32, :], [0] * 32), after=[e_rp1])
                            e_obv = P.do("vector", lambda e, qs=qs: e.tensor_mul(
                                ob[:, :], self.psB[0:64, qs * 512:(qs + 1) * 512], repsb[:, :]),
                                after=[e_rpc, e_ob_dma])
                            e_obv_prev = e_obv
                            e_ob_dma = P.dma("gpsimd",
                                             attnf_pct[(h % 2) * 64:(h % 2) * 64 + 64,
                                                       h // 2, qs * 512:(qs + 1) * 512],
                                             ob[:, :], after=[e_obv])
                            self.dbg("attnf", lambda t, h=h, qs=qs: t.rearrange("c p t -> p c t")[
                                (h % 2) * 64:(h % 2) * 64 + 64, h // 2, qs * 512:(qs + 1) * 512],
                                ob[:, :], after=[e_ob_dma])
                            e_rep_ev = e_obv
                        e_obv_last = e_obv
            self.ev_attn_sp = e_ob_dma
        self.bar = self.phase_end()

    def ph_a3(self):
        P, nc = self.P, self.nc
        bar = self.bar
        attnf_pct = self.attnf_d.rearrange("c p t -> p c t")
        h2a3 = self.h2a3
        with nc.sbuf_tensor("wob", [128, NKD * 1024], F32) as wob, \
             nc.sbuf_tensor("at0", [128, NKD * 128], F32) as at0, \
             nc.sbuf_tensor("at1", [128, NKD * 128], F32) as at1, \
             nc.sbuf_tensor("grep", [128, D], F32) as grep, \
             nc.sbuf_tensor("brep", [128, D], F32) as brep, \
             nc.sbuf_tensor("borep", [128, D], F32) as borep, \
             nc.sbuf_tensor("xnt", [128, D], F32) as xnt, \
             nc.sbuf_tensor("pret", [128, D], F32) as pret, \
             nc.sbuf_tensor("mamt", [128, D], F32) as mamt, \
             nc.sbuf_tensor("x2t", [128, D], F32) as x2t, \
             nc.sbuf_tensor("att", [128, D], F32) as att, \
             nc.sbuf_tensor("h2t", [128, D], F32) as h2t, \
             nc.sbuf_tensor("sqd2", [128, D], F32) as sqd2:
            wob3 = wob.ap().rearrange("p (c n) -> p c n", c=NKD)
            if self.moe_mode in ("sorted", "expert"):
                e_zr = P.do("vector", lambda e: e.memset(self.stg[:, 0:D], 0.0), after=bar)
                e_zsp = P.dma("gpsimd", self.h2tm_d[T:T + 1, :], self.stg[0:1, 0:D], after=[e_zr])
                self.ev_h2zero = e_zsp
                if self.moe_mode == "expert":
                    self.outs.append(e_zsp)
            e_g = self.replicate(grep, self.l2g, D, after=bar)
            e_b = self.replicate(brep, self.l2b, D, after=[e_g])
            e_bo = self.replicate(borep, self.bo, D, after=[e_b])
            e_wol = [P.dma("sync", wob3[:, c, :], self.wo[c * 128:(c + 1) * 128, :], after=bar)
                     for c in range(NKD)]
            c = Chain(P, start=[e_bo] + e_wol + [self.ev_attn_sp])
            prev_epo = None
            for tt in range(NTT):
                ab = at0 if tt % 2 == 0 else at1
                ab3 = ab.ap().rearrange("p (c t) -> p c t", c=NKD)
                tile = slice(tt * 128, (tt + 1) * 128)
                e_at = P.dma("sync", ab3[:, :, :], attnf_pct[:, :, tile],
                             after=[self.ev_attn_sp] + (c.last if tt >= 2 else []),
                             slot=2 + tt % 2)
                e_xn = P.dma("sync", xnt[:, :], self.xn_d[tile, :],
                             after=[self.ev_xn_sp] + c.last, slot=1)
                e_pr = P.dma("sync", pret[:, :], self.pre0_d[tile, :],
                             after=[self.ev_p0_sp, prev_epo] + c.last, slot=9)
                e_mm2 = P.dma("sync", mamt[:, :], self.mam_d[tile, :],
                              after=self.ev_mam_sp[-1:] + c.last, slot=10)

                def womm(e, ab3=ab3):
                    r = None
                    for h2 in range(2):
                        for cc in range(NKD):
                            r = e.matmul(self.psA[:, h2 * 512:(h2 + 1) * 512],
                                         ab3[:, cc, :], wob3[:, cc, h2 * 512:(h2 + 1) * 512],
                                         start=(cc == 0), stop=(cc == NKD - 1))
                    return r
                c.do("tensor", womm, extra=[e_at])
                c.do("vector", lambda e: e.tensor_add(att[:, :], self.psA[:, 0:1024], borep[:, :]))
                c.do("vector", lambda e: e.tensor_add(x2t[:, :], att[:, :], xnt[:, :]), extra=[e_xn])
                self.dbg("x2", lambda t, tile=tile: t[tile, :], x2t[:, :], after=c.last)
                c.do("vector", lambda e: e.tensor_add(pret[:, :], pret[:, :], mamt[:, :]),
                     extra=[e_pr, e_mm2])
                c.do("vector", lambda e: e.tensor_add(pret[:, :], pret[:, :], att[:, :]))
                e_po = P.dma("gpsimd", self.preout[tile, :], pret[:, :], after=c.last)
                self.outs.append(e_po)
                prev_epo = e_po
                S, V = "scalar", "vector"
                c.do(S, lambda e: e.activation(out=sqd2[:, :], in_=x2t[:, :], func=AF.Square,
                                               accum_out=self.s_ssq[:, :1]))
                c.do(V, lambda e: e.reduce_sum(self.s_sum[:, :1], x2t[:, :], axis=AX.X))
                c.do(V, lambda e: e.tensor_scalar_mul(self.s_mu[:, :1], self.s_sum[:, :1], 1.0 / D))
                c.do(V, lambda e: e.tensor_mul(self.s_msq[:, :1], self.s_mu[:, :1], self.s_mu[:, :1]))
                c.do(V, lambda e: e.scalar_tensor_tensor(self.s_var[:, :1], self.s_ssq[:, :1], 1.0 / D,
                                                         self.s_msq[:, :1], ALU.mult, ALU.subtract))
                c.do(S, lambda e: e.activation(out=self.s_var[:, :1], in_=self.s_var[:, :1],
                                               func=AF.Sqrt, bias=self.epsc[:, :1]))
                c.do(V, lambda e: e.reciprocal(self.s_rstd[:, :1], self.s_var[:, :1]))
                c.do(V, lambda e: e.tensor_scalar(h2t[:, :], x2t[:, :], self.s_mu[:, :1],
                                                  self.s_rstd[:, :1], ALU.subtract, ALU.mult))
                c.do(V, lambda e: e.tensor_mul(h2t[:, :], h2t[:, :], grep[:, :]))
                c.do(V, lambda e: e.tensor_add(h2t[:, :], h2t[:, :], brep[:, :]))

                if self.moe_mode in ("sorted", "expert"):
                    e_h2sp = P.dma("gpsimd", self.h2tm_d[tile, :], h2t[:, :], after=c.last)
                    if self.moe_mode == "expert":
                        self.outs.append(e_h2sp)
                    c.last = c.last + [e_h2sp]

                def tps(e):
                    r = None
                    for cc in range(NKD):
                        r = e.transpose(self.psB[:, cc * 128:(cc + 1) * 128],
                                        h2t[:, cc * 128:(cc + 1) * 128], self.ident[:, :])
                    return r
                c.do("tensor", tps)
                c.do(S, lambda e, tile=tile: e.copy(h2a3[:, :, tile], self.psB[:, 0:D]))
                self.dbg("h2f", lambda t, tile=tile: t.rearrange("c p t -> p c t")[:, :, tile],
                         h2a3[:, :, tile], after=c.last)
            self.ev_h2 = c.last[0]
        self.bar = self.phase_end()

    def ph_routing(self):
        P, nc = self.P, self.nc
        bar = self.bar
        h2a3 = self.h2a3
        with nc.sbuf_tensor("lgf", [E, T], F32) as lgf, \
             nc.sbuf_tensor("lgt", [128, 128], F32) as lgt, \
             nc.sbuf_tensor("lmx", [128, 8], F32) as lmx, \
             nc.sbuf_tensor("esr", [128, 8], F32) as esr, \
             nc.sbuf_tensor("pk", [128, 4], F32) as pk, \
             nc.sbuf_tensor("negl", [128, 1], F32) as negl, \
             nc.sbuf_tensor("sume", [128, 1], F32) as sume, \
             nc.sbuf_tensor("rws", [4, T], F32) as rws, \
             nc.sbuf_tensor("l1r", [1, T], F32) as l1r, \
             nc.sbuf_tensor("oh1", [E, T], F32) as oh1, \
             nc.sbuf_tensor("oh2", [E, T], F32) as oh2, \
             nc.sbuf_tensor("cum1", [E, T], F32) as cum1, \
             nc.sbuf_tensor("cum2", [E, T], F32) as cum2, \
             nc.sbuf_tensor("reprow", [E, T], F32) as reprow, \
             nc.sbuf_tensor("zro", [E, T], F32) as zro, \
             nc.sbuf_tensor("d1r", [1, T], F32) as d1r, \
             nc.sbuf_tensor("d2r", [1, T], F32) as d2r, \
             nc.sbuf_tensor("cnt", [1, 2 * E], F32) as cnt:
            def lgmm(e):
                r = None
                for s in range(4):
                    for cc in range(NKD):
                        r = e.matmul(self.psA[0:E, s * 512:(s + 1) * 512],
                                     self.wrc[:, cc * E:(cc + 1) * E],
                                     h2a3[:, cc, s * 512:(s + 1) * 512],
                                     start=(cc == 0), stop=(cc == NKD - 1))
                return r
            e_lgmm = P.do("tensor", lgmm, after=bar + [self.ev_h2])
            e_lgf = P.do("scalar", lambda e: e.copy(lgf[:, :], self.psA[0:E, :]), after=[e_lgmm])
            self.dbg("lgf", lambda t: t, lgf[:, :], after=[e_lgf])

            def tptm(e):
                r = None
                for tt in range(NTT):
                    r = e.transpose(self.psB[:, tt * 8:(tt + 1) * 8],
                                    lgf[0:E, tt * 128:(tt + 1) * 128], self.ident[0:E, 0:E])
                return r
            e_tp = P.do("tensor", tptm, after=[e_lgf])
            e_lgt = P.do("scalar", lambda e: e.copy(lgt[:, :], self.psB[:, 0:128]), after=[e_tp])
            c = Chain(P, start=[e_lgt])
            for tt in range(NTT):
                c.do("vector", lambda e, tt=tt: e.max(lmx[:, :], lgt[:, tt * 8:(tt + 1) * 8]))
                c.do("vector", lambda e: e.tensor_scalar_mul(negl[:, :1], lmx[:, 0:1], -1.0))
                c.do("scalar", lambda e: e.activation(out=esr[:, :], in_=lmx[:, :], func=AF.Exp,
                                                      bias=negl[:, :1], accum_out=sume[:, :1]))
                c.do("vector", lambda e: e.reciprocal(sume[:, :1], sume[:, :1]))
                c.do("vector", lambda e: e.tensor_single_scalar(pk[:, 2:4], esr[:, 0:2],
                                                                sume[:, :1], ALU.mult))
                c.do("vector", lambda e: e.tensor_copy(pk[:, 0:2], lmx[:, 0:2]))
                c.do("tensor", lambda e, tt=tt: e.transpose(self.psA[0:4, tt * 128:(tt + 1) * 128],
                                                            pk[:, 0:4], self.ident[:, :]))
            e_rws = P.do("scalar", lambda e: e.copy(rws[:, :], self.psA[0:4, :]), after=c.last)
            self.dbg("rws", lambda t: t, rws[:, :], after=[e_rws])
            e_l1r = P.dma("gpsimd", l1r[0:1, :], rws[1:2, :], after=[e_rws])

            def reprep(rowap):
                def f(e):
                    r = None
                    for s in range(4):
                        r = e.matmul(self.psB[0:E, s * 512:(s + 1) * 512], self.ones_row[:1, 0:E],
                                     rowap[0:1, s * 512:(s + 1) * 512], start=True, stop=True)
                    return r
                return f
            e_r1 = P.do("tensor", reprep(rws), after=[e_rws, e_tp])
            e_r1c = P.do("scalar", lambda e: e.copy(reprow[:, :], self.psB[0:E, :]), after=[e_r1])
            e_o1 = P.do("vector", lambda e: e.tensor_tensor(oh1[:, :], lgf[:, :], reprow[:, :],
                                                            ALU.is_equal), after=[e_r1c])
            e_r2 = P.do("tensor", reprep(l1r), after=[e_o1, e_l1r])
            e_r2c = P.do("scalar", lambda e: e.copy(reprow[:, :], self.psB[0:E, :]), after=[e_r2])
            e_o2 = P.do("vector", lambda e: e.tensor_tensor(oh2[:, :], lgf[:, :], reprow[:, :],
                                                            ALU.is_equal), after=[e_r2c])
            e_z = P.do("vector", lambda e: e.memset(zro[:, :], 0.0), after=bar)
            e_c1 = P.do("vector", lambda e: e.tensor_tensor_scan(
                out=cum1[:, :], data0=oh1[:, :], data1=zro[:, :], initial=0.0,
                op0=ALU.add, op1=ALU.add), after=[e_o1, e_z])
            e_c2 = P.do("vector", lambda e: e.tensor_tensor_scan(
                out=cum2[:, :], data0=oh2[:, :], data1=zro[:, :], initial=0.0,
                op0=ALU.add, op1=ALU.add), after=[e_o2, e_z])
            self.dbg("cum1", lambda t: t, cum1[:, :], after=[e_c1])
            self.dbg("cum2", lambda t: t, cum2[:, :], after=[e_c2])
            e_oh1sp = self.odma(self.oh1o, oh1[:, :], after=[e_o1])
            e_oh2sp = self.odma(self.oh2o, oh2[:, :], after=[e_o2])
            e_m1 = P.do("vector", lambda e: e.tensor_mul(self.stg[0:E, :], oh1[:, :], cum1[:, :]),
                        after=[e_c1])

            def dsum(e):
                r = None
                for s in range(4):
                    r = e.matmul(self.psB[0:1, s * 512:(s + 1) * 512], self.ones_col[0:E, :1],
                                 self.stg[0:E, s * 512:(s + 1) * 512], start=True, stop=True)
                return r
            e_d1m = P.do("tensor", dsum, after=[e_m1, e_r2c])
            e_d1 = P.do("scalar", lambda e: e.copy(d1r[:1, :], self.psB[0:1, :]), after=[e_d1m])
            e_m2 = P.do("vector", lambda e: e.tensor_mul(self.stg[0:E, :], oh2[:, :], cum2[:, :]),
                        after=[e_c2, e_d1m])
            e_d2m = P.do("tensor", dsum, after=[e_m2, e_d1])
            e_d2 = P.do("scalar", lambda e: e.copy(d2r[:1, :], self.psB[0:1, :]), after=[e_d2m])
            self.outs.append(P.dma("gpsimd", self.rowso[0:1, :], d1r[:1, :], after=[e_d1]))
            self.outs.append(P.dma("gpsimd", self.rowso[1:2, :], d2r[:1, :], after=[e_d2]))
            self.outs.append(P.dma("gpsimd", self.rowso[2:3, :], rws[2:3, :], after=[e_rws]))
            self.outs.append(P.dma("gpsimd", self.rowso[3:4, :], rws[3:4, :], after=[e_rws]))
            def cntt(e):
                e.transpose(self.psB[0:1, 0:E], cum1[0:E, T - 1:T], self.ident[0:E, 0:E])
                return e.transpose(self.psB[0:1, E:2 * E], cum2[0:E, T - 1:T], self.ident[0:E, 0:E])
            e_cm = P.do("tensor", cntt, after=[e_c1, e_c2, e_d2])
            e_cc = P.do("scalar", lambda e: e.copy(cnt[:1, :], self.psB[0:1, 0:2 * E]), after=[e_cm])
            self.outs.append(P.dma("gpsimd", self.cnto, cnt[:1, :], after=[e_cc]))
            def ohtm(src, dst):
                def f(e):
                    r = None
                    for tt in range(NTT):
                        r = e.transpose(self.psA[:, tt * 8:(tt + 1) * 8],
                                        src[0:E, tt * 128:(tt + 1) * 128], self.ident[0:E, 0:E])
                    return r
                return f
            e_t1 = P.do("tensor", ohtm(oh1, None), after=[e_o1, e_rws, e_d2m, e_cm])
            e_t1c = P.do("scalar", lambda e: e.copy(self.oh1tm[:, :], self.psA[:, 0:128]), after=[e_t1])
            e_t2 = P.do("tensor", ohtm(oh2, None), after=[e_o2, e_t1c])
            e_t2c = P.do("scalar", lambda e: e.copy(self.oh2tm[:, :], self.psA[:, 0:128]), after=[e_t2])
            self.ev_ohtm = [e_t1c, e_t2c]
            if self.moe_mode == "sorted":
                self._routing_sorted(locals())
        self.bar = self.phase_end()

    def _routing_sorted(self, L):
        P, nc = self.P, self.nc
        cum1, cum2, zro = L["cum1"], L["cum2"], L["zro"]
        oh1, oh2 = L["oh1"], L["oh2"]
        d1r, d2r = L["d1r"], L["d2r"]
        e_c1, e_c2 = L["e_c1"], L["e_c2"]
        e_d1, e_d2 = L["e_d1"], L["e_d2"]
        e_o1, e_o2 = L["e_o1"], L["e_o2"]
        with nc.sbuf_tensor("srt", [128, 96], F32) as srt, \
             nc.sbuf_tensor("vals", [128, 16], F32) as vals, \
             nc.sbuf_tensor("vals2", [128, 16], F32) as vals2, \
             nc.sbuf_tensor("dr1", [1, T], F32) as dr1, \
             nc.sbuf_tensor("dr2", [1, T], F32) as dr2, \
             nc.sbuf_tensor("dest1", [128, 16], F32) as dest1, \
             nc.sbuf_tensor("dest2", [128, 16], F32) as dest2, \
             nc.sbuf_tensor("dest1i", [128, 16], mybir.dt.int32) as dest1i, \
             nc.sbuf_tensor("dest2i", [128, 16], mybir.dt.int32) as dest2i, \
             nc.sbuf_tensor("ixc", [128, 1], mybir.dt.int32) as ixc, \
             nc.sbuf_tensor("row8", [1, 64], F32) as row8:
            c = Chain(P, start=[e_c1, e_c2, e_d1, e_d2, self.ev_h2zero] + self.ev_ohtm)
            V, S, G, Tn = "vector", "scalar", "gpsimd", "tensor"
            c.do(V, lambda e: e.tensor_add(srt[0:E, 0:1], cum1[0:E, T - 1:T], cum2[0:E, T - 1:T]))
            c.do(G, lambda e: e.iota(srt[0:E, 1:33], pattern=[[1, 32]], base=0,
                                     channel_multiplier=0, allow_small_or_imprecise_dtypes=True))
            c.do(V, lambda e: e.tensor_scalar_mul(srt[0:E, 1:33], srt[0:E, 1:33], 128.0))
            c.do(V, lambda e: e.tensor_single_scalar(srt[0:E, 1:33], srt[0:E, 1:33],
                                                     srt[0:E, 0:1], ALU.is_lt))
            c.do(V, lambda e: e.reduce_sum(srt[0:E, 33:34], srt[0:E, 1:33], axis=AX.X))
            c.do(V, lambda e: e.tensor_scalar_mul(srt[0:E, 33:34], srt[0:E, 33:34], 128.0))
            c.do(Tn, lambda e: e.transpose(self.psB[0:1, 0:E], srt[0:E, 33:34],
                                           self.ident[0:E, 0:E]), extra=[e_o2])
            c.do(S, lambda e: e.copy(row8[0:1, 0:E], self.psB[0:1, 0:E]))
            c.do(V, lambda e: e.tensor_tensor_scan(out=row8[0:1, 8:16], data0=row8[0:1, 0:E],
                                                   data1=zro[0:1, 0:E], initial=0.0,
                                                   op0=ALU.add, op1=ALU.add))
            c.do(V, lambda e: e.tensor_sub(row8[0:1, 16:24], row8[0:1, 8:16], row8[0:1, 0:E]))
            c.do(Tn, lambda e: e.transpose(self.psB[0:E, 32:33], row8[0:1, 16:24],
                                           self.ident[0:1, 0:1]))
            c.do(S, lambda e: e.copy(srt[0:E, 34:35], self.psB[0:E, 32:33]))
            c.do(V, lambda e: e.tensor_add(srt[0:E, 35:36], srt[0:E, 34:35], cum1[0:E, T - 1:T]))
            c.do(Tn, lambda e: e.transpose(self.psB[0:E, 33:34], row8[0:1, 8:16],
                                           self.ident[0:1, 0:1]))
            c.do(S, lambda e: e.copy(srt[0:E, 36:37], self.psB[0:E, 33:34]))

            def segmm(col, ohs):
                def f(e):
                    r = None
                    for s in range(4):
                        r = e.matmul(self.psB[0:1, s * 512:(s + 1) * 512], col,
                                     ohs[:, s * 512:(s + 1) * 512], start=True, stop=True)
                    return r
                return f
            c.do(Tn, segmm(srt[0:E, 34:35], oh1), extra=[e_o1])
            c.do(V, lambda e: e.scalar_tensor_tensor(dr1[0:1, :], d1r[0:1, :], -1.0,
                                                     self.psB[0:1, :], ALU.add, ALU.add))
            c.do(Tn, segmm(srt[0:E, 35:36], oh2))
            c.do(V, lambda e: e.scalar_tensor_tensor(dr2[0:1, :], d2r[0:1, :], -1.0,
                                                     self.psB[0:1, :], ALU.add, ALU.add))
            def dtp(row, coloff):
                def f(e):
                    r = None
                    for tt in range(NTT):
                        r = e.transpose(self.psA[:, coloff + tt:coloff + tt + 1],
                                        row[0:1, tt * 128:(tt + 1) * 128], self.ident[0:1, 0:1])
                    return r
                return f
            c.do(Tn, dtp(dr1, 0))
            c.do(S, lambda e: e.copy(dest1[:, :], self.psA[:, 0:16]))
            c.do(Tn, dtp(dr2, 16))
            c.do(S, lambda e: e.copy(dest2[:, :], self.psA[:, 16:32]))
            c.do(V, lambda e: e.tensor_copy(dest1i[:, :], dest1[:, :]))
            c.do(V, lambda e: e.tensor_copy(dest2i[:, :], dest2[:, :]))
            c.do(G, lambda e: e.iota(vals.ap(), pattern=[[128, 16]], base=0,
                                     channel_multiplier=1, allow_small_or_imprecise_dtypes=True))
            c.do(V, lambda e: e.tensor_scalar_add(vals2[:, :], vals[:, :], float(T + 16)))
            reprow = L["reprow"]
            c.do(V, lambda e: e.memset(reprow[0:1, :], float(T)))
            for s0 in (0, 2048, 4096):
                w = min(2048, 5248 - s0)
                c.do(G, lambda e, s0=s0, w=w: e.dma_start(
                    out=self.srctok_d[0:1, s0:s0 + w], in_=reprow[0:1, 0:w]), dma=True)
            c.do(V, lambda e: e.memset(reprow[0:1, :], float(2 * (T + 16))))
            for s0 in (0, 2048, 4096):
                w = min(2048, 5248 - s0)
                c.do(G, lambda e, s0=s0, w=w: e.dma_start(
                    out=self.dsttok_d[0:1, s0:s0 + w], in_=reprow[0:1, 0:w]), dma=True)
            src1d = self.srctok_d.rearrange("o (s u) -> (o s) u", u=1)
            dst1d = self.dsttok_d.rearrange("o (s u) -> (o s) u", u=1)
            for tt in range(NTT):
                for di, dsti in ((0, dest1i), (1, dest2i)):
                    c.do(V, lambda e, tt=tt, dsti=dsti: e.tensor_copy(ixc[:, :], dsti[:, tt:tt + 1]))
                    c.do(G, lambda e, tt=tt: e.indirect_dma_start(
                        src1d, IndirectOffsetOnAxis(ixc[:, 0:1], 0),
                        vals[:, tt:tt + 1], None), dma=True)
                    vsrc = vals if di == 0 else vals2
                    c.do(G, lambda e, vsrc=vsrc, tt=tt: e.indirect_dma_start(
                        dst1d, IndirectOffsetOnAxis(ixc[:, 0:1], 0),
                        vsrc[:, tt:tt + 1], None), dma=True)
            c.do(G, lambda e: e.iota(srt[0:E, 40:80], pattern=[[1, 40]], base=0,
                                     channel_multiplier=0, allow_small_or_imprecise_dtypes=True))
            c.do(V, lambda e: e.tensor_scalar_mul(srt[0:E, 40:80], srt[0:E, 40:80], 128.0))
            c.do(V, lambda e: e.tensor_single_scalar(srt[0:E, 40:80], srt[0:E, 40:80],
                                                     srt[0:E, 36:37], ALU.is_ge))

            def esum(e):
                return e.matmul(self.psB[0:1, 1024:1064], self.ones_col[0:E, 0:1],
                                srt[0:E, 40:80], start=True, stop=True)
            c.do(Tn, esum)
            c.do(S, lambda e: e.copy(row8[0:1, 24:64], self.psB[0:1, 1024:1064]))
            c.do(V, lambda e: e.tensor_scalar_min(row8[0:1, 24:64], row8[0:1, 24:64], 7.0))

            def erep(e):
                return e.matmul(self.psA[:, 32:72], self.ones_row[:1, 0:128],
                                row8[0:1, 24:64], start=True, stop=True)
            c.do(Tn, erep)
            c.do(S, lambda e: e.copy(self.eidx_fs[:, :], self.psA[:, 32:72]))
            c.do(V, lambda e: e.tensor_copy(self.eidx_i[:, :], self.eidx_fs[:, :]))
            c.do(G, lambda e: e.iota(self.basew1.ap(), pattern=[[128, 8]], base=0,
                                     channel_multiplier=1, allow_small_or_imprecise_dtypes=True))
            c.do(G, lambda e: e.iota(self.basew2.ap(), pattern=[[128, 32]], base=0,
                                     channel_multiplier=1, allow_small_or_imprecise_dtypes=True))
            self.ev_sortmeta = c.last[0]

    def ph_moes(self):
        P, nc = self.P, self.nc
        bar = self.bar
        NTILE = 40
        w1rows = self.w1.rearrange("e d f -> (e d) f")
        w2rows = self.w2.rearrange("e f d -> (e f) d")
        src1d = self.srctok_d.rearrange("o (s u) -> (o s) u", u=1)
        dst1d = self.dsttok_d.rearrange("o (s u) -> (o s) u", u=1)
        _st = ExitStack()
        gtm0 = _st.enter_context(nc.sbuf_tensor("gtm0", [128, D], F32))
        gtm1 = _st.enter_context(nc.sbuf_tensor("gtm1", [128, D], F32))
        hfm0 = _st.enter_context(nc.sbuf_tensor("hfm0", [128, NKD * 128], F32))
        hfm1 = _st.enter_context(nc.sbuf_tensor("hfm1", [128, NKD * 128], F32))
        w1g0 = _st.enter_context(nc.sbuf_tensor("w1g0", [128, FF], F32))
        w1g1 = _st.enter_context(nc.sbuf_tensor("w1g1", [128, FF], F32))
        b1g = _st.enter_context(nc.sbuf_tensor("b1g", [1, FF], F32))
        b1g2 = _st.enter_context(nc.sbuf_tensor("b1g2", [128, 32], F32))
        hetm = _st.enter_context(nc.sbuf_tensor("hetm", [128, FF], F32))
        hefm = _st.enter_context(nc.sbuf_tensor("hefm", [128, FF], F32))
        w2g0 = _st.enter_context(nc.sbuf_tensor("w2g0", [128, 4 * D], F32))
        w2g1 = _st.enter_context(nc.sbuf_tensor("w2g1", [128, 4 * D], F32))
        ysb = _st.enter_context(nc.sbuf_tensor("ysb", [128, D], F32))
        si0 = _st.enter_context(nc.sbuf_tensor("si0", [128, 1], mybir.dt.int32))
        si1 = _st.enter_context(nc.sbuf_tensor("si1", [128, 1], mybir.dt.int32))
        di0 = _st.enter_context(nc.sbuf_tensor("di0", [128, 1], mybir.dt.int32))
        di1 = _st.enter_context(nc.sbuf_tensor("di1", [128, 1], mybir.dt.int32))
        iw10 = _st.enter_context(nc.sbuf_tensor("iw10", [128, 8], mybir.dt.int32))
        iw11 = _st.enter_context(nc.sbuf_tensor("iw11", [128, 8], mybir.dt.int32))
        iw20 = _st.enter_context(nc.sbuf_tensor("iw20", [128, 32], mybir.dt.int32))
        iw21 = _st.enter_context(nc.sbuf_tensor("iw21", [128, 32], mybir.dt.int32))
        ib0 = _st.enter_context(nc.sbuf_tensor("ib0", [128, 1], mybir.dt.int32))
        ib1 = _st.enter_context(nc.sbuf_tensor("ib1", [128, 1], mybir.dt.int32))
        scf = _st.enter_context(nc.sbuf_tensor("scf", [128, 8], F32))
        ecol = _st.enter_context(nc.sbuf_tensor("ecol", [128, 1], F32))

        e_sc_prev = None
        e_y_ev = None
        e_b1prev = None
        e_prev_g = None
        e_prev_tp = None
        e_prev_hfuse = {0: None, 1: None}
        e_w1prev = {}
        e_w2prev = {0: None, 1: None}
        e_hetp = None
        e_mm2_last = None
        e_heev = None
        e_mm1_evacs = []
        for i in range(NTILE):
            gb = (gtm0, gtm1)[i % 2]
            hb = (hfm0, hfm1)[i % 2]
            sib = (si0, si1)[i % 2]
            dib = (di0, di1)[i % 2]
            iw1 = (iw10, iw11)[i % 2]
            iw2 = (iw20, iw21)[i % 2]
            ibb = (ib0, ib1)[i % 2]
            par = 4 * (i % 2)
            e_s1 = P.dma("sync", scf[:, par:par + 1], src1d[i * 128:(i + 1) * 128, :],
                         after=[e_sc_prev] + (bar if i == 0 else []))
            e_s2 = P.dma("sync", scf[:, par + 1:par + 2], dst1d[i * 128:(i + 1) * 128, :],
                         after=[e_s1])
            e_ci1 = P.do("vector", lambda e, par=par, sib=sib: e.tensor_copy(
                sib[:, :], scf[:, par:par + 1]),
                after=[e_s2, e_y_ev, self.ev_sortmeta if i == 0 else None])
            e_ci2 = P.do("vector", lambda e, par=par, dib=dib: e.tensor_copy(
                dib[:, :], scf[:, par + 1:par + 2]), after=[e_ci1])
            e_e1 = P.do("vector", lambda e, i=i: e.tensor_scalar_mul(
                ecol[:, :], self.eidx_fs[:, i:i + 1], 1024.0),
                after=[e_ci2, e_w1prev.get(("w", i % 2))])
            e_e2 = P.do("vector", lambda e, iw1=iw1, par=par: e.tensor_single_scalar(
                scf[:, par + 2:par + 3], self.basew1[:, 0:1], ecol[:, 0:1], ALU.add),
                after=[e_e1]) if False else None
            e_i1f = P.do("vector", lambda e, iw1=iw1: e.tensor_single_scalar(
                self.stg2[:, 0:8], self.basew1[:, :], ecol[:, 0:1], ALU.add), after=[e_e1])
            e_i1 = P.do("vector", lambda e, iw1=iw1: e.tensor_copy(iw1[:, :], self.stg2[:, 0:8]),
                        after=[e_i1f])
            e_e3 = P.do("vector", lambda e, i=i: e.tensor_scalar_mul(
                ecol[:, :], self.eidx_fs[:, i:i + 1], 4096.0), after=[e_i1f, e_w2prev[i % 2]])
            e_i2f = P.do("vector", lambda e: e.tensor_single_scalar(
                self.stg2[:, 8:40], self.basew2[:, :], ecol[:, 0:1], ALU.add), after=[e_e3])
            e_i2 = P.do("vector", lambda e, iw2=iw2: e.tensor_copy(iw2[:, :], self.stg2[:, 8:40]),
                        after=[e_i2f])
            e_ibf = P.do("vector", lambda e, i=i: e.tensor_scalar_mul(
                ecol[:, :], self.eidx_fs[:, i:i + 1], 128.0), after=[e_i2f, e_b1prev])
            e_ibf2 = P.do("vector", lambda e: e.tensor_add(ecol[:, :], ecol[:, :],
                                                           self.iota_c[:, 0:1]), after=[e_ibf])
            e_ib = P.do("vector", lambda e, ibb=ibb: e.tensor_copy(ibb[:, :], ecol[:, :]),
                        after=[e_ibf2, e_i2])
            e_ci = e_ib
            e_sc_prev = e_ci
            e_g = P.do("gpsimd", lambda e, gb=gb, sib=sib: e.indirect_dma_start(
                gb[:, :], None, self.h2tm_d, IndirectOffsetOnAxis(sib[:, 0:1], 0)),
                after=[e_ci1, e_prev_g], dma=True, slot=2)

            def gtp(e, gb=gb):
                r = None
                for cc in range(NKD):
                    r = e.transpose(self.psB[:, 1024 + cc * 128:1024 + (cc + 1) * 128],
                                    gb[:, cc * 128:(cc + 1) * 128], self.ident[:, :])
                return r
            e_tp = P.do("tensor", gtp, after=[e_g, e_prev_tp, self.ev_ident, e_mm2_last] +
                        e_mm1_evacs[-4:])
            e_hc = P.do("scalar", lambda e, hb=hb: e.copy(hb[:, :], self.psB[:, 1024:2048]),
                        after=[e_tp, e_prev_hfuse[i % 2]])
            e_prev_g = e_tp
            e_prev_tp = e_hc
            e_b1a = P.do("gpsimd", lambda e, ibb=ibb: e.indirect_dma_start(
                b1g2[:, :], None, self.b1.rearrange("e (p m) -> (e p) m", p=128),
                IndirectOffsetOnAxis(ibb[:, 0:1], 0)),
                after=[e_ci, e_heev, e_b1prev], dma=True, slot=3)
            e_b1 = P.do("gpsimd", lambda e: e.dma_start(out=b1g[0:1, :], in_=b1g2[:, :]),
                        after=[e_b1a, e_heev], dma=True, slot=3)
            e_b1prev = e_b1
            ps8 = [(self.psA, 0), (self.psA, 512), (self.psA, 1024), (self.psA, 1536),
                   (self.psB, 0), (self.psB, 512), (self.psB, 1024), (self.psB, 1536)]
            e_wl = {}
            e_mmk = None
            for cc in range(NKD):
                wb = (w1g0, w1g1)[cc % 2]
                e_w = P.do("gpsimd", lambda e, wb=wb, iw1=iw1, cc=cc: e.indirect_dma_start(
                    wb[:, :], None, w1rows, IndirectOffsetOnAxis(iw1[:, cc:cc + 1], 0)),
                    after=[e_i1, e_wl.get(cc - 2), e_mm2_last if cc <= 1 else None] +
                    (e_mm1_evacs if cc <= 1 else []), dma=True, slot=4 + cc % 2)

                def mm1(e, wb=wb, cc=cc, hb=hb):
                    r = None
                    for ms in range(8):
                        ps, off = ps8[ms]
                        r = e.matmul(ps[:, off:off + 512], hb[:, cc * 128:(cc + 1) * 128],
                                     wb[:, ms * 512:(ms + 1) * 512],
                                     start=(cc == 0), stop=False)
                    return r
                e_mmk = P.do("tensor", mm1, after=[e_w, e_hc, e_mmk] +
                             (e_mm1_evacs if cc <= 1 else []) + ([e_hetp] if cc == 0 else []))
                e_wl[cc] = e_mmk
                e_w_last = e_w

            def mm1b(e):
                r = None
                for ms in range(8):
                    ps, off = ps8[ms]
                    r = e.matmul(ps[:, off:off + 512], self.ones_row[:1, 0:128],
                                 b1g[0:1, ms * 512:(ms + 1) * 512], start=False, stop=True)
                return r
            e_mmb = P.do("tensor", mm1b, after=[e_mmk, e_b1])
            e_mm1_evacs = []
            e_he = {}
            for ms in range(8):
                ps, off = ps8[ms]
                e_h = P.do("scalar", lambda e, ps=ps, off=off, ms=ms: e.activation(
                    out=hetm[:, ms * 512:(ms + 1) * 512], in_=ps[:, off:off + 512],
                    func=AF.Relu), after=[e_mmb, e_hetp])
                e_he[ms] = e_h
                e_mm1_evacs.append(e_h)
            e_heev = e_he[7]
            e_prev_hfuse[i % 2] = e_mmk
            e_w1prev[("w", i % 2)] = e_w_last
            e_hetp = None
            for g8 in range(4):
                def htp(e, g8=g8):
                    r = None
                    for j in range(8):
                        mc = g8 * 8 + j
                        r = e.transpose(self.psA[:, 1024 + j * 128:1024 + (j + 1) * 128],
                                        hetm[:, mc * 128:(mc + 1) * 128], self.ident[:, :])
                    return r
                e_t = P.do("tensor", htp,
                           after=[e_he[2 * g8], e_he[2 * g8 + 1], e_he[2], e_he[3],
                                  e_hetp, e_mm2_last])
                e_hetp = P.do("scalar", lambda e, g8=g8: e.copy(
                    hefm[:, g8 * 1024:(g8 + 1) * 1024], self.psA[:, 1024:2048]),
                    after=[e_t, e_mm2_last])
            e_mm2_last = None
            for g in range(8):
                wb = (w2g0, w2g1)[g % 2]
                e_w2 = None
                for j in range(4):
                    e_w2 = P.do("gpsimd", lambda e, wb=wb, iw2=iw2, g=g, j=j: e.indirect_dma_start(
                        wb[:, j * D:(j + 1) * D], None, w2rows,
                        IndirectOffsetOnAxis(iw2[:, g * 4 + j:g * 4 + j + 1], 0)),
                        after=([e_i2, e_w2prev[g % 2]] if j == 0 else []),
                        dma=True, slot=6 + g % 2)

                def mm2(e, wb=wb, g=g):
                    r = None
                    for j in range(4):
                        mc = g * 4 + j
                        for dh in range(2):
                            r = e.matmul(self.psB[:, dh * 512:(dh + 1) * 512],
                                         hefm[:, mc * 128:(mc + 1) * 128],
                                         wb[:, j * D + dh * 512:j * D + (dh + 1) * 512],
                                         start=(g == 0 and j == 0), stop=(g == 7 and j == 3))
                    return r
                e_m2 = P.do("tensor", mm2,
                            after=[e_w2, e_hetp if g >= 1 else None, e_mm2_last] +
                            ([e_he[4], e_he[5], e_y_ev] if g == 0 else []) +
                            ([e_hetp] if g == 0 else []))
                e_w2prev[g % 2] = e_m2
                e_mm2_last = e_m2
            e_ye = P.do("scalar", lambda e: e.copy(ysb[:, :], self.psB[:, 0:1024]),
                        after=[e_mm2_last, e_y_ev])
            e_ysc = P.do("gpsimd", lambda e, dib=dib: e.indirect_dma_start(
                self.y12o, IndirectOffsetOnAxis(dib[:, 0:1], 0),
                ysb[:, :], None), after=[e_ye], dma=True, slot=8)
            self.outs.append(e_ysc)
            e_y_ev = e_ysc
        _st.close()
        self.bar = self.phase_end()


    def ph_moe1(self):
        P, nc = self.P, self.nc
        bar = self.bar
        h2a3 = self.h2a3
        with nc.sbuf_tensor("w1b0", [128, NKD * 128], F32) as w1b0, \
             nc.sbuf_tensor("w1b1", [128, NKD * 128], F32) as w1b1, \
             nc.sbuf_tensor("heb0", [128, 2048], F32) as heb0, \
             nc.sbuf_tensor("heb1", [128, 2048], F32) as heb1:
            w1bufs = (w1b0, w1b1)
            hebufs = (heb0, heb1)
            e_mm = {0: None, 1: None}
            e_sp = {0: None, 1: None}
            e_ld = {0: None, 1: None}
            for e8 in range(E):
                for m in range(NFF):
                    i = m % 2
                    wb = w1bufs[i]
                    wb3 = wb.ap().rearrange("p (c m) -> p c m", c=NKD)
                    e_l = []
                    for cc in range(NKD):
                        e_l.append(P.dma("sync", wb3[:, cc, :],
                                         self.w1[e8, cc * 128:(cc + 1) * 128, m * 128:(m + 1) * 128],
                                         after=[e_mm[i]] + (bar if e8 == 0 and m == 0 else [])))
                    e_ld[i] = e_l

                    def mmf(e, wb3=wb3, i=i):
                        ps = self.psA if i == 0 else self.psB
                        r = None
                        for s in range(4):
                            for cc in range(NKD):
                                r = e.matmul(ps[:, s * 512:(s + 1) * 512],
                                             wb3[:, cc, :], h2a3[:, cc, s * 512:(s + 1) * 512],
                                             start=(cc == 0), stop=(cc == NKD - 1))
                        return r
                    e_m = P.do("tensor", mmf, after=e_l + [e_sp[i]])
                    e_mm[i] = e_m
                    e_rl = P.do("scalar", lambda e, i=i, e8=e8, m=m: e.activation(
                        out=hebufs[i][:, :], in_=(self.psA if i == 0 else self.psB)[:, :],
                        func=AF.Relu, bias=self.b1c[:, e8 * NFF + m:e8 * NFF + m + 1]),
                        after=[e_m, e_sp[i]])
                    e_s = P.dma("gpsimd", self.he_d[e8, m, :, :], hebufs[i][:, :], after=[e_rl])
                    if e8 == 0:
                        e_s = self.dbg("he0", lambda t, m=m: t[m, :, :], hebufs[i][:, :],
                                       after=[e_s]) or e_s
                    e_sp[i] = e_s
            self.ev_he = [e_sp[0], e_sp[1]]
        self.bar = self.phase_end()

    def ph_moe2(self):
        P, nc = self.P, self.nc
        bar = self.bar
        with nc.sbuf_tensor("y1a", [128, NTT * 1024], F32) as y1a, \
             nc.sbuf_tensor("y2a", [128, NTT * 1024], F32) as y2a, \
             nc.sbuf_tensor("hec0", [128, 512], F32) as heb0, \
             nc.sbuf_tensor("hec1", [128, 512], F32) as heb1, \
             nc.sbuf_tensor("w2b0", [128, 1024], F32) as w2b0, \
             nc.sbuf_tensor("w2b1", [128, 1024], F32) as w2b1:
            y1a3 = y1a.ap().rearrange("p (tt n) -> p tt n", tt=NTT)
            y2a3 = y2a.ap().rearrange("p (tt n) -> p tt n", tt=NTT)
            hebufs, w2bufs = (heb0, heb1), (w2b0, w2b1)
            e_z1 = P.do("vector", lambda e: e.memset(y1a[:, :], 0.0), after=bar)
            e_z2 = P.do("vector", lambda e: e.memset(y2a[:, :], 0.0), after=bar)
            ps_slot = [(self.psA, 0), (self.psA, 1024), (self.psB, 0), (self.psB, 1024)]
            e_ev2 = {}
            e_mmk = {0: None, 1: None}
            for e8 in range(E):
                for tb in range(4):
                    e_mm_last = None
                    for k in range(NFF):
                        i = k % 2
                        e_hl = P.dma("sync", hebufs[i][:, :],
                                     self.he_d[e8, k, :, tb * 512:(tb + 1) * 512],
                                     after=[e_mmk[i], self.ev_he[i] if (e8 == 0 and tb == 0) else None])
                        e_wl = P.dma("sync", w2bufs[i][:, :],
                                     self.w2[e8, k * 128:(k + 1) * 128, :],
                                     after=[e_mmk[i]])

                        def mmf(e, i=i, k=k, tb=tb):
                            r = None
                            for tl in range(4):
                                ps, off = ps_slot[tl]
                                for h2 in range(2):
                                    r = e.matmul(ps[:, off + h2 * 512:off + (h2 + 1) * 512],
                                                 hebufs[i][:, tl * 128:(tl + 1) * 128],
                                                 w2bufs[i][:, h2 * 512:(h2 + 1) * 512],
                                                 start=(k == 0), stop=(k == NFF - 1))
                            return r
                        prev_tb = (e8, tb - 1) if tb else (e8 - 1, 3)
                        e_mm_last = P.do("tensor", mmf,
                                         after=[e_hl, e_wl] +
                                         ([e_ev2.get(prev_tb + (tl,)) for tl in range(4)]
                                          if k == 0 else []) +
                                         ([e_z1, e_z2] if (e8 == 0 and k == 0) else []))
                        e_mmk[i] = e_mm_last
                    for tl in range(4):
                        tt = tb * 4 + tl
                        ps, off = ps_slot[tl]
                        e_a1 = P.do("vector", lambda e, ps=ps, off=off, tt=tt, e8=e8:
                                    e.scalar_tensor_tensor(
                                        y1a3[:, tt, :], ps[:, off:off + 1024],
                                        self.oh1tm[:, tt * 8 + e8:tt * 8 + e8 + 1],
                                        y1a3[:, tt, :], ALU.mult, ALU.add),
                                    after=[e_mm_last, e_ev2.get((e8 - 1, tb, tl))] +
                                    (self.ev_ohtm if e8 == 0 else []) +
                                    ([e_z1] if e8 == 0 else []))
                        e_a2 = P.do("vector", lambda e, ps=ps, off=off, tt=tt, e8=e8:
                                    e.scalar_tensor_tensor(
                                        y2a3[:, tt, :], ps[:, off:off + 1024],
                                        self.oh2tm[:, tt * 8 + e8:tt * 8 + e8 + 1],
                                        y2a3[:, tt, :], ALU.mult, ALU.add),
                                    after=[e_a1] + ([e_z2] if e8 == 0 else []))
                        e_ev2[(e8, tb, tl)] = e_a2
            for tt in range(NTT):
                self.outs.append(P.dma(
                    "gpsimd", self.y1o[tt * 128:(tt + 1) * 128, :], y1a3[:, tt, :],
                    after=[e_ev2[(E - 1, tt // 4, tt % 4)]]))
                self.outs.append(P.dma(
                    "gpsimd", self.y2o[tt * 128:(tt + 1) * 128, :], y2a3[:, tt, :],
                    after=[e_ev2[(E - 1, tt // 4, tt % 4)]]))
        self.bar = self.phase_end()



F32 = mybir.dt.float32
I32 = mybir.dt.int32
AF = mybir.ActivationFunctionType
ALU = mybir.AluOpType
AX = mybir.AxisListType
IndirectOffsetOnAxis = bass.IndirectOffsetOnAxis

T, D, FF, E, C = 2048, 1024, 4096, 8, 4096
NTT = T // 128
SL = 1024
NCH = C // SL
NTL = SL // 128
NFB = FF // 512


class L2E:
    def __init__(self):
        self.nc = bass.Bass("TRN2", target_bir_lowering=False)
        self.P = Prog(self.nc)
        self.outs = []
        self.build()

    def build(self):
        nc, P = self.nc, self.P

        def din(name, shape, dt=F32):
            return nc.dram_tensor(name, shape, dt, kind="ExternalInput").ap()
        w1e = din("w1e", [D, FF])
        b1e = din("b1e", [1, FF])
        w2e = din("w2e", [FF, D])
        h2cat = din("h2cat", [E * (T + 1), D])
        oh1r = din("oh1r", [E, T])
        oh2r = din("oh2r", [E, T])
        d1a = din("d1a", [E, T])
        d2a = din("d2a", [E, T])
        cnt2 = din("cnt2", [E, 2])
        yeo = nc.dram_tensor("yeo", [C, D], F32, kind="ExternalOutput").ap()
        slmk = [nc.dram_tensor(f"slm{q}", [C + 260, 1], F32, kind="Internal").ap()
                for q in range(16)]

        A = nc.alloc_sbuf_tensor
        ident = A("ident", [128, 128], F32)
        iota_r = A("iota_r", [128, 128], F32)
        iota8 = A("iota8", [128, 1], F32)
        ones_row = A("ones_row", [1, 128], F32)
        ones_col = A("ones_col", [128, 1], F32)
        sli = A("sli", [128, 32], I32)
        psA = nc.alloc_psum_tensor("psA", [128, 2048], F32)
        psB = nc.alloc_psum_tensor("psB", [128, 2048], F32)

        e_ir = P.do("gpsimd", lambda e: e.iota(iota8.ap(), pattern=[[0, 1]], base=0,
                                               channel_multiplier=1,
                                               allow_small_or_imprecise_dtypes=True))
        e_ir2 = P.do("gpsimd", lambda e: e.iota(iota_r.ap(), pattern=[[1, 128]], base=0,
                                                channel_multiplier=0,
                                                allow_small_or_imprecise_dtypes=True))
        e_c0 = P.do("vector", lambda e: e.memset(ones_row[:1, :], 1.0))
        e_c1 = P.do("vector", lambda e: e.memset(ones_col[:, :], 1.0))
        e_id = P.do("vector", lambda e: e.tensor_single_scalar(
            ident[:, :], iota_r[:, :], iota8[:, :1], ALU.is_equal), after=[e_ir, e_ir2])

        _st1 = ExitStack()
        ltm = _st1.enter_context(nc.sbuf_tensor("ltm", [8, 8], F32))
        cnt2s = _st1.enter_context(nc.sbuf_tensor("cnt2s", [8, 2], F32))
        offm1 = _st1.enter_context(nc.sbuf_tensor("offm1", [8, 2], F32))
        totc = _st1.enter_context(nc.sbuf_tensor("totc", [32, 1], F32))
        totc8 = _st1.enter_context(nc.sbuf_tensor("totc8", [32, 1], F32))
        dro = _st1.enter_context(nc.sbuf_tensor("dro", [8, 2048], F32))
        idx1 = _st1.enter_context(nc.sbuf_tensor("idx1", [8, 2048], F32))
        idx2 = _st1.enter_context(nc.sbuf_tensor("idx2", [8, 2048], F32))
        vals = _st1.enter_context(nc.sbuf_tensor("vals", [8, 2048], F32))
        ohb = _st1.enter_context(nc.sbuf_tensor("ohb", [8, 2048], F32))
        ohb2 = _st1.enter_context(nc.sbuf_tensor("ohb2", [8, 2048], F32))
        ixc1 = _st1.enter_context(nc.sbuf_tensor("ixc1", [128, 128], F32))
        ixc2 = _st1.enter_context(nc.sbuf_tensor("ixc2", [128, 128], F32))
        vc = _st1.enter_context(nc.sbuf_tensor("vc", [128, 128], F32))
        ixi1 = _st1.enter_context(nc.sbuf_tensor("ixi1", [128, 128], I32))
        ixi2 = _st1.enter_context(nc.sbuf_tensor("ixi2", [128, 128], I32))
        vci = _st1.enter_context(nc.sbuf_tensor("vci", [128, 128], I32))
        zrow = _st1.enter_context(nc.sbuf_tensor("zrow", [1, 4356], F32))
        zrowi = _st1.enter_context(nc.sbuf_tensor("zrowi", [1, 4356], I32))
        srows = _st1.enter_context(nc.sbuf_tensor("srows", [16, C], F32))
        srow = _st1.enter_context(nc.sbuf_tensor("srow", [1, C], F32))
        mrow = _st1.enter_context(nc.sbuf_tensor("mrow", [1, C], F32))
        slc = _st1.enter_context(nc.sbuf_tensor("slc", [128, 32], F32))
        trw0 = _st1.enter_context(nc.sbuf_tensor("trw0", [8, 2048], F32))
        trw1 = _st1.enter_context(nc.sbuf_tensor("trw1", [8, 2048], F32))
        msk = _st1.enter_context(nc.sbuf_tensor("msk", [8, 2048], F32))
        if True:
            e_ld = [P.dma("sync", cnt2s[:, :], cnt2),
                    P.dma("sync", dro[:, :], d1a),
                    P.dma("sync", ohb[:, :], oh1r),
                    P.dma("sync", ohb2[:, :], oh2r)]
            e_z0 = P.do("vector", lambda e: e.memset(zrow[:1, :], 0.0))
            e_zd_last = None
            for q in range(16):
                e_zd_last = P.dma("sync", slmk[q].rearrange("r o -> o (r)"), zrow[:1, :],
                                  after=[e_z0] if q == 0 else [], slot=8)
            e_tr = [P.do("gpsimd", lambda e, ch=ch2, b=(trw0, trw1)[ch2]: e.iota(
                b.ap(), pattern=[[1, 16], [0, 128]], base=C + ch * 16,
                channel_multiplier=32, allow_small_or_imprecise_dtypes=True))
                for ch2 in range(2)]
            e_vi = P.do("gpsimd", lambda e: e.iota(vals.ap(), pattern=[[1, T]], base=1,
                                                   channel_multiplier=T + 1,
                                                   allow_small_or_imprecise_dtypes=True))
            e_lt = P.do("vector", lambda e: e.tensor_single_scalar(
                ltm[:, :], iota_r[0:8, 0:8], iota8[0:8, :1], ALU.is_gt),
                after=[e_ir, e_ir2])
            e_om = P.do("tensor", lambda e: e.matmul(psA[0:8, 0:2], ltm[:, :],
                                                     cnt2s[:, :], start=True, stop=True),
                        after=[e_lt, e_ld[0]])
            e_tm = P.do("tensor", lambda e: e.matmul(psA[0:1, 4:5], ones_col[0:8, :1],
                                                     cnt2s[:, 0:1], start=True, stop=True),
                        after=[e_om, e_c1])
            e_tz0 = P.do("vector", lambda e: e.memset(totc[:, :1], 0.0))
            e_oc = P.do("scalar", lambda e: e.copy(offm1[:, :], psA[0:8, 0:2]), after=[e_tm])
            e_tc = P.do("scalar", lambda e: e.copy(totc[0:1, :1], psA[0:1, 4:5]),
                        after=[e_tm, e_tz0])
            e_o1 = P.do("vector", lambda e: e.tensor_scalar_add(offm1[:, :], offm1[:, :], -1.0),
                        after=[e_oc])
            e_t8 = P.do("vector", lambda e: e.stream_shuffle(totc8[0:32, :1], totc[0:32, :1],
                                                             [0] * 32), after=[e_tc, e_o1])
            e_o2 = P.do("vector", lambda e: e.tensor_tensor(offm1[:, 1:2], offm1[:, 1:2],
                                                            totc8[0:8, :1], ALU.add),
                        after=[e_t8])
            ev_sc = {}
            e_d2 = None
            for ch in range(2):
                ohs = ohb if ch == 0 else ohb2
                idx = idx1 if ch == 0 else idx2
                trw = (trw0, trw1)[ch]
                if ch == 1:
                    e_d2 = P.dma("sync", dro[:, :], d2a, after=[ev_sc[0]])
                e_r1 = P.do("vector", lambda e, ch=ch, idx=idx: e.tensor_scalar(
                    idx[:, :], dro[:, :], offm1[:, ch:ch + 1], 0.0,
                    ALU.add, ALU.add),
                    after=[e_o2, e_ld[1] if ch == 0 else e_d2,
                           ev_sc.get(0) if ch == 1 else None])
                e_q1 = P.do("vector", lambda e, idx=idx: e.tensor_single_scalar(
                    msk[:, :], idx[:, :], float(C), ALU.is_lt),
                    after=[e_r1, e_tr[0], e_tr[1]])
                e_q2 = P.do("vector", lambda e, ohs=ohs: e.tensor_mul(
                    msk[:, :], msk[:, :], ohs[:, :]), after=[e_q1, e_ld[2], e_ld[3]])
                e_q3 = P.do("vector", lambda e, idx=idx, trw=trw: e.tensor_sub(
                    idx[:, :], idx[:, :], trw[:, :]), after=[e_q2])
                e_q4 = P.do("vector", lambda e, idx=idx: e.tensor_mul(
                    idx[:, :], idx[:, :], msk[:, :]), after=[e_q3])
                e_r3 = P.do("vector", lambda e, idx=idx, trw=trw: e.tensor_add(
                    idx[:, :], idx[:, :], trw[:, :]), after=[e_q4])
                ev_sc[ch] = e_r3
            e_prev_evac = None
            e_prev_cast = None
            sct_q = {}
            for half in range(2):
                hs = slice(half * 64, half * 64 + 64)

                def tp8(e, src, c0, half=half):
                    r = None
                    for t in range(8):
                        tt = half * 8 + t
                        r = e.transpose(psB[:, c0 + t * 8:c0 + (t + 1) * 8],
                                        src[0:8, tt * 128:(tt + 1) * 128], ident[0:8, 0:8])
                    return r
                e_t1 = P.do("tensor", lambda e, f=tp8: f(e, idx1[0:8, :], 0),
                            after=[ev_sc[0], e_id, e_prev_evac])
                e_e1 = P.do("scalar", lambda e, hs=hs: e.copy(ixc1[:, hs], psB[:, 0:64]),
                            after=[e_t1])
                e_t2 = P.do("tensor", lambda e, f=tp8: f(e, idx2[0:8, :], 64),
                            after=[ev_sc[1], e_e1])
                e_e2 = P.do("scalar", lambda e, hs=hs: e.copy(ixc2[:, hs], psB[:, 64:128]),
                            after=[e_t2])
                e_t3 = P.do("tensor", lambda e, f=tp8: f(e, vals[0:8, :], 128),
                            after=[e_vi, e_e2])
                e_e3 = P.do("scalar", lambda e, hs=hs: e.copy(vc[:, hs], psB[:, 128:192]),
                            after=[e_t3])
                e_prev_evac = e_e3
                e_x1 = P.do("vector", lambda e, hs=hs: e.tensor_copy(ixi1[:, hs], ixc1[:, hs]),
                            after=[e_e1, e_prev_cast])
                e_x2 = P.do("vector", lambda e, hs=hs: e.tensor_copy(ixi2[:, hs], ixc2[:, hs]),
                            after=[e_e2, e_x1])
                e_x3 = e_x2
                e_prev_cast = e_x3
                for tloc in range(8):
                    for k in range(8):
                        col = tloc * 8 + k + half * 64
                        for ci, ixi in enumerate((ixi1, ixi2)):
                            q = k * 2 + ci
                            sct_q[q] = P.do(
                                "gpsimd", lambda e, ixi=ixi, col=col, q=q: e.indirect_dma_start(
                                    slmk[q], IndirectOffsetOnAxis(ixi[:, col:col + 1], 0),
                                    vc[:, col:col + 1], None),
                                after=[sct_q.get(q), e_x3 if (tloc == 0 and k == 0 and ci == 0)
                                       else None, e_zd_last if (half == 0 and tloc == 0) else None,
                                       e_e3 if (tloc == 0 and k == 0 and ci == 0) else None],
                                dma=True, slot=10 + q)
            e_sll = None
            for q in range(16):
                e_sll = P.dma("sync", srows[q:q + 1, :],
                              slmk[q][0:C, :].rearrange("r o -> o (r)"),
                              after=[sct_q[q]] + ([e_sll] if q else []), slot=12)

            def mergemm2(e):
                r = None
                for s8 in range(8):
                    ps = psA if s8 < 4 else psB
                    r = e.matmul(ps[0:1, (s8 % 4) * 512:(s8 % 4 + 1) * 512],
                                 ones_col[0:16, 0:1], srows[0:16, s8 * 512:(s8 + 1) * 512],
                                 start=True, stop=True)
                return r
            e_mg = P.do("tensor", mergemm2, after=[e_sll, e_c1])
            e_mge = P.do("scalar", lambda e: e.copy(srow[0:1, 0:2048], psA[0:1, 0:2048]),
                         after=[e_mg])
            e_mge2 = P.do("scalar", lambda e: e.copy(srow[0:1, 2048:4096], psB[0:1, 0:2048]),
                          after=[e_mge])
            e_d1 = P.do("vector", lambda e: e.tensor_single_scalar(
                mrow[:1, :], srow[:1, :], 0.5, ALU.is_gt), after=[e_mge2])
            e_d2m = P.do("vector", lambda e: e.tensor_scalar_add(
                srow[:1, :], srow[:1, :], float(-1 - T)), after=[e_d1])
            e_d3 = P.do("vector", lambda e: e.tensor_mul(srow[:1, :], srow[:1, :], mrow[:1, :]),
                        after=[e_d2m])
            e_d4 = P.do("vector", lambda e: e.tensor_scalar_add(
                srow[:1, :], srow[:1, :], float(T)), after=[e_d3])
            def sltp(e):
                r = None
                for tt in range(32):
                    r = e.transpose(psA[:, tt:tt + 1],
                                    srow[0:1, tt * 128:(tt + 1) * 128], ident[0:1, 0:1])
                return r
            e_st = P.do("tensor", sltp, after=[e_d4, e_mge2, e_id])
            e_se = P.do("scalar", lambda e: e.copy(slc[:, :], psA[:, 0:32]), after=[e_st])
            e_slm_done = P.do("vector", lambda e: e.tensor_copy(sli[:, :], slc[:, :]),
                              after=[e_se])
        _st1.close()
        bar1 = P.barrier()
        P.new_epoch()

        w1v = w1e.rearrange("(kc p) f -> p kc f", p=128)
        w2v = w2e.rearrange("(mc p) d -> p mc d", p=128)
        _st = ExitStack()
        xg0 = _st.enter_context(nc.sbuf_tensor("xg0", [128, 8 * SL], F32))
        xg1 = _st.enter_context(nc.sbuf_tensor("xg1", [128, 8 * SL], F32))
        y_sb = _st.enter_context(nc.sbuf_tensor("y_sb", [128, NTL * D], F32))
        w1b0 = _st.enter_context(nc.sbuf_tensor("w1b0", [128, 8 * 512], F32))
        w1b1 = _st.enter_context(nc.sbuf_tensor("w1b1", [128, 8 * 512], F32))
        w2b0 = _st.enter_context(nc.sbuf_tensor("w2b0", [128, 4 * D], F32))
        w2b1 = _st.enter_context(nc.sbuf_tensor("w2b1", [128, 4 * D], F32))
        b1s = _st.enter_context(nc.sbuf_tensor("b1s", [1, FF], F32))
        gb0 = _st.enter_context(nc.sbuf_tensor("gb0", [128, D], F32))
        gb1 = _st.enter_context(nc.sbuf_tensor("gb1", [128, D], F32))
        sl0 = _st.enter_context(nc.sbuf_tensor("sl0", [128, 1], I32))
        sl1 = _st.enter_context(nc.sbuf_tensor("sl1", [128, 1], I32))
        het0 = _st.enter_context(nc.sbuf_tensor("het0", [128, 512], F32))
        het1 = _st.enter_context(nc.sbuf_tensor("het1", [128, 512], F32))
        hef0 = _st.enter_context(nc.sbuf_tensor("hef0", [128, 512], F32))
        hef1 = _st.enter_context(nc.sbuf_tensor("hef1", [128, 512], F32))

        e_b1 = P.dma("sync", b1s[0:1, :], b1e, after=bar1)
        xgs = (xg0, xg1)
        e_pro_done = {}
        e_gb = {0: None, 1: None}
        e_sl = {0: None, 1: None}
        e_tpp = [None]

        def prologue(ch):
            xg = xgs[ch % 2]
            xg3 = xg.ap().rearrange("p (kc t) -> p kc t", kc=8)
            evs = []
            for tl in range(NTL):
                gb = (gb0, gb1)[tl % 2]
                ic = ch * NTL + tl
                e_g = P.do("gpsimd", lambda e, gb=gb, ic=ic: e.indirect_dma_start(
                    gb[:, :], None, h2cat, IndirectOffsetOnAxis(sli[:, ic:ic + 1], 0)),
                    after=[e_slm_done if (ch == 0 and tl <= 1) else None,
                           e_gb[tl % 2]], dma=True, slot=2 + tl % 2)
                e_last = None
                for g4 in range(2):
                    e_t = P.do("tensor", lambda e, gb=gb, g4=g4: [
                        e.transpose(psA[:, 1536 + cc4 * 128:1536 + (cc4 + 1) * 128],
                                    gb[:, (g4 * 4 + cc4) * 128:(g4 * 4 + cc4 + 1) * 128],
                                    ident[:, :]) for cc4 in range(4)][-1],
                        after=[e_g, e_id, e_tpp[0]])
                    e_ev = P.do("scalar", lambda e, xg3=xg3, tl=tl, g4=g4: e.copy(
                        xg3[:, g4 * 4:(g4 + 1) * 4, tl * 128:(tl + 1) * 128],
                        psA[:, 1536:2048].rearrange("p (c t) -> p c t", c=4)),
                        after=[e_t])
                    e_tpp[0] = e_ev
                    e_last = e_ev
                e_gb[tl % 2] = e_last
                e_sl[tl % 2] = e_g
                evs.append(e_last)
            e_pro_done[ch] = evs

        prologue(0)
        e_w1 = {0: None, 1: None}
        e_w2 = {0: None, 1: None}
        e_he_pv = {0: None, 1: None}
        e_hf_pv = {0: None, 1: None}
        e_mm2_pv = {0: None, 1: None}
        e_ya = None
        e_yd = {}
        for ch in range(NCH):
            xg3 = xgs[ch % 2].ap().rearrange("p (kc t) -> p kc t", kc=8)
            for fb in range(NFB):
                w1b = (w1b0, w1b1)[fb % 2]
                w2b = (w2b0, w2b1)[fb % 2]
                e_wl = None
                for kc in range(8):
                    e_wl = P.dma("sync",
                                 w1b.ap().rearrange("p (kc f) -> p kc f", kc=8)[:, kc, :],
                                 w1v[:, kc, fb * 512:(fb + 1) * 512],
                                 after=(([e_w1[fb % 2]] +
                                         (bar1 if (ch == 0 and fb <= 1) else []))
                                        if kc == 0 else []), slot=4 + fb % 2)
                e_w2l = None
                for mc in range(4):
                    e_w2l = P.dma("sync",
                                  w2b.ap().rearrange("p (mc d) -> p mc d", mc=4)[:, mc, :],
                                  w2v[:, fb * 4 + mc, :],
                                  after=([e_w2[fb % 2]] if mc == 0 else []), slot=6 + fb % 2)
                for tl in range(NTL):
                    pg = tl % 2
                    het = (het0, het1)[pg]
                    hef = (hef0, hef1)[pg]
                    m1ps = psA[:, pg * 512:(pg + 1) * 512]

                    def mm1(e, tl=tl, fb=fb, xg3=xg3, w1b=w1b, m1ps=m1ps):
                        for kc in range(8):
                            e.matmul(m1ps, xg3[:, kc, tl * 128:(tl + 1) * 128],
                                     w1b[:, kc * 512:(kc + 1) * 512],
                                     start=(kc == 0), stop=False)
                        return e.matmul(m1ps, ones_row[:1, 0:128],
                                        b1s[0:1, fb * 512:(fb + 1) * 512],
                                        start=False, stop=True)
                    e_m1 = P.do("tensor", mm1,
                                after=[e_wl, e_b1, e_he_pv[pg]] +
                                (e_pro_done[ch] if (fb == 0 and tl <= 1) else []))
                    e_he = P.do("scalar", lambda e, het=het, m1ps=m1ps: e.activation(
                        out=het[:, :], in_=m1ps, func=AF.Relu),
                        after=[e_m1, e_hf_pv[pg]])
                    e_he_pv[pg] = e_he

                    def htp(e, het=het):
                        r = None
                        for j in range(4):
                            r = e.transpose(psA[:, 1024 + j * 128:1024 + (j + 1) * 128],
                                            het[:, j * 128:(j + 1) * 128], ident[:, :])
                        return r
                    e_ht = P.do("tensor", htp, after=[e_he, e_hf_pv[0], e_hf_pv[1]])
                    e_hf = P.do("scalar", lambda e, hef=hef: e.copy(hef[:, :],
                                                                    psA[:, 1024:1536]),
                                after=[e_ht])
                    e_hf_pv[pg] = e_hf

                    def mm2(e, hef=hef, w2b=w2b, pg=pg):
                        r = None
                        for mc in range(4):
                            for dh in range(2):
                                r = e.matmul(
                                    psB[:, pg * 1024 + dh * 512:pg * 1024 + (dh + 1) * 512],
                                    hef[:, mc * 128:(mc + 1) * 128],
                                    w2b[:, mc * D + dh * 512:mc * D + (dh + 1) * 512],
                                    start=(mc == 0), stop=(mc == 3))
                        return r
                    e_m2 = P.do("tensor", mm2, after=[e_hf, e_w2l, e_ya])
                    e_mm2_pv[pg] = e_m2
                    if fb == 0:
                        e_y = P.do("vector", lambda e, tl=tl, pg=pg: e.tensor_copy(
                            y_sb[:, tl * D:(tl + 1) * D],
                            psB[:, pg * 1024:(pg + 1) * 1024]),
                            after=[e_m2, e_yd.get((ch - 1, NTL - 1))])
                    else:
                        e_y = P.do("vector", lambda e, tl=tl, pg=pg: e.tensor_tensor(
                            y_sb[:, tl * D:(tl + 1) * D], y_sb[:, tl * D:(tl + 1) * D],
                            psB[:, pg * 1024:(pg + 1) * 1024], ALU.add),
                            after=[e_m2])
                    e_ya = e_y
                e_w1[fb % 2] = e_m1
                e_w2[fb % 2] = e_m2
                if fb == 0 and ch + 1 < NCH:
                    prologue(ch + 1)
            for tl in range(NTL):
                e_o = P.dma("gpsimd", yeo[ch * SL + tl * 128:ch * SL + (tl + 1) * 128, :],
                            y_sb[:, tl * D:(tl + 1) * D],
                            after=[e_ya], slot=8)
                e_yd[(ch, tl)] = e_o
        self.outs = [e_yd[(NCH - 1, NTL - 1)]]
        _st.close()
        P.do("sync", lambda e: e.nop(), after=self.outs)
        P.finalize()



F32 = mybir.dt.float32
AF = mybir.ActivationFunctionType
ALU = mybir.AluOpType
AX = mybir.AxisListType

T, D, E, C = 2048, 1024, 8, 4096
IndirectOffsetOnAxis = bass.IndirectOffsetOnAxis
NTT = T // 128
EPS32 = float(np.finfo(np.float32).eps)


class L2:
    def __init__(self, gather=False):
        self.nc = bass.Bass("TRN2", target_bir_lowering=False)
        self.P = Prog(self.nc)
        self.outs = []
        self.gather = gather
        self.build()

    def build(self):
        nc, P = self.nc, self.P

        def din(name, shape):
            return nc.dram_tensor(name, shape, F32, kind="ExternalInput").ap()
        pre = din("pre", [T, D])
        if self.gather:
            ycat = din("ycat", [E * C, D])
        else:
            y1 = din("y1", [T, D])
            y2 = din("y2", [T, D])
        rows = din("rows", [4, T])
        oh1 = din("oh1", [E, T])
        oh2 = din("oh2", [E, T])
        cnts = din("cnts", [E, 2 * E])
        rid = din("rid", [1, 1])
        b2 = din("b2", [E, D])
        outp = nc.dram_tensor("outp", [T, D], F32, kind="ExternalOutput").ap()

        A = nc.alloc_sbuf_tensor
        oh1s = A("oh1s", [E, T], F32)
        oh2s = A("oh2s", [E, T], F32)
        d1s = A("d1s", [1, T], F32)
        d2s = A("d2s", [1, T], F32)
        t1s = A("t1s", [1, T], F32)
        t2s = A("t2s", [1, T], F32)
        cnt = A("cnt", [E, 2 * E], F32)
        rid_s = A("rid_s", [1, 1], F32)
        b2s = A("b2s", [E, D], F32)
        iota8 = A("iota8", [128, 1], F32)
        ones_row = A("ones_row", [1, 128], F32)
        ones_col = A("ones_col", [128, 1], F32)
        ident = A("ident", [128, 128], F32)
        iota_r = A("iota_r", [128, 128], F32)
        prefm = A("prefm", [E, 1], F32)
        offc1 = A("offc1", [E, 1], F32)
        offc2 = A("offc2", [E, 1], F32)
        totc1 = A("totc1", [E, 1], F32)
        loc = A("loc", [1, T], F32)
        idx1r = A("idx1r", [1, T], F32)
        idx2r = A("idx2r", [1, T], F32)
        itm = A("itm", [128, 32], F32)
        ii = A("ii", [128, 32], mybir.dt.int32)
        v1r = A("v1r", [1, T], F32)
        v2r = A("v2r", [1, T], F32)
        g1r = A("g1r", [1, T], F32)
        g2r = A("g2r", [1, T], F32)
        den = A("den", [1, T], F32)
        G1 = A("G1", [E, T], F32)
        GG = A("GG", [E, T], F32)
        gtm = A("gtm", [128, 2 * NTT], F32)
        pret = A("pret", [128, D], F32)
        y1t = A("y1t", [128, D], F32)
        y2t = A("y2t", [128, D], F32)
        ot = A("ot", [128, D], F32)

        psA = nc.alloc_psum_tensor("psA", [128, 2048], F32)
        psB = nc.alloc_psum_tensor("psB", [128, 2048], F32)

        ld = lambda out, in_, after=(): P.dma("sync", out, in_, after=after)
        e_in = [ld(oh1s[:, :], oh1), ld(oh2s[:, :], oh2), ld(d1s[0:1, :], rows[0:1, :]),
                ld(cnt[:, :], cnts), ld(rid_s[:1, :1], rid), ld(b2s[:, :], b2),
                ld(d2s[0:1, :], rows[1:2, :]), ld(t1s[0:1, :], rows[2:3, :]),
                ld(t2s[0:1, :], rows[3:4, :])]
        e_ir = P.do("gpsimd", lambda e: e.iota(iota8.ap(), pattern=[[0, 1]], base=0,
                                               channel_multiplier=1,
                                               allow_small_or_imprecise_dtypes=True))
        e_ir2 = P.do("gpsimd", lambda e: e.iota(iota_r.ap(), pattern=[[1, 128]], base=0,
                                                channel_multiplier=0,
                                                allow_small_or_imprecise_dtypes=True))
        e_c0 = P.do("vector", lambda e: e.memset(ones_row[:1, :], 1.0))
        e_c1 = P.do("vector", lambda e: e.memset(ones_col[:, :], 1.0))
        e_id = P.do("vector", lambda e: e.tensor_single_scalar(
            ident[:, :], iota_r[:, :], iota8[:, :1], ALU.is_equal), after=[e_ir, e_ir2])

        e_rb = P.do("tensor", lambda e: e.matmul(psB[0:E, 0:1], ones_row[:1, 0:E],
                                                 rid_s[:1, 0:1], start=True, stop=True),
                    after=[e_in[4], e_c0])
        e_pm = P.do("vector", lambda e: e.tensor_tensor(prefm[:, :1], iota8[0:E, :1],
                                                        psB[0:E, 0:1], ALU.is_lt),
                    after=[e_rb, e_ir])
        e_o1 = P.do("tensor", lambda e: e.matmul(psA[0:E, 0:1], cnt[:, 0:E], prefm[:, 0:1],
                                                 start=True, stop=True),
                    after=[e_pm, e_in[3]])
        e_o1c = P.do("scalar", lambda e: e.copy(offc1[:, :1], psA[0:E, 0:1]), after=[e_o1])
        e_o2 = P.do("tensor", lambda e: e.matmul(psA[0:E, 1:2], cnt[:, E:2 * E], prefm[:, 0:1],
                                                 start=True, stop=True), after=[e_o1c, e_pm])
        e_o2c = P.do("scalar", lambda e: e.copy(offc2[:, :1], psA[0:E, 1:2]), after=[e_o2])
        e_t1 = P.do("tensor", lambda e: e.matmul(psA[0:E, 2:3], cnt[:, 0:E], ones_col[0:E, 0:1],
                                                 start=True, stop=True), after=[e_o2c, e_c1])
        e_t1c = P.do("scalar", lambda e: e.copy(totc1[:, :1], psA[0:E, 2:3]), after=[e_t1])

        def rowmm(col_evs, cols, ohs, oh_ev):
            def f(e):
                r = None
                for s in range(4):
                    for ci, colap in enumerate(cols):
                        r = e.matmul(psB[0:1, s * 512:(s + 1) * 512], colap,
                                     ohs[:, s * 512:(s + 1) * 512],
                                     start=(ci == 0), stop=(ci == len(cols) - 1))
                return r
            return f
        e_m1 = P.do("tensor", rowmm(None, [offc1[:, 0:1]], oh1s, None),
                    after=[e_o1c, e_in[0], e_rb])
        e_l1 = P.do("vector", lambda e: e.scalar_tensor_tensor(
            loc[:1, :], d1s[0:1, :], -1.0, psB[0:1, :], ALU.add, ALU.add),
            after=[e_m1, e_in[2]])
        e_v1 = P.do("vector", lambda e: e.tensor_single_scalar(v1r[:1, :], loc[:1, :],
                                                               float(C), ALU.is_lt),
                    after=[e_l1])
        e_em1 = P.do("tensor", rowmm(None, [iota8[0:E, 0:1]], oh1s, None),
                     after=[e_v1, e_ir, e_in[0]])
        e_i1a = P.do("vector", lambda e: e.scalar_tensor_tensor(
            idx1r[:1, :], psB[0:1, :], float(C), loc[:1, :], ALU.mult, ALU.add),
            after=[e_em1])
        e_i1b = P.do("vector", lambda e: e.tensor_mul(idx1r[:1, :], idx1r[:1, :], v1r[:1, :]),
                     after=[e_i1a])
        e_m2 = P.do("tensor", rowmm(None, [offc2[:, 0:1], totc1[:, 0:1]], oh2s, None),
                    after=[e_o2c, e_t1c, e_in[1], e_v1, e_i1a])
        e_l2 = P.do("vector", lambda e: e.scalar_tensor_tensor(
            loc[:1, :], d2s[0:1, :], -1.0, psB[0:1, :], ALU.add, ALU.add), after=[e_m2, e_in[6]])
        e_v2 = P.do("vector", lambda e: e.tensor_single_scalar(v2r[:1, :], loc[:1, :],
                                                               float(C), ALU.is_lt),
                    after=[e_l2])
        e_em2 = P.do("tensor", rowmm(None, [iota8[0:E, 0:1]], oh2s, None),
                     after=[e_v2, e_ir, e_in[1]])
        e_i2a = P.do("vector", lambda e: e.scalar_tensor_tensor(
            idx2r[:1, :], psB[0:1, :], float(C), loc[:1, :], ALU.mult, ALU.add),
            after=[e_em2])
        e_i2b = P.do("vector", lambda e: e.tensor_mul(idx2r[:1, :], idx2r[:1, :], v2r[:1, :]),
                     after=[e_i2a])
        e_k1 = P.do("vector", lambda e: e.tensor_mul(v1r[:1, :], v1r[:1, :], t1s[0:1, :]),
                    after=[e_v1, e_in[7], e_i1b])
        e_k2 = P.do("vector", lambda e: e.tensor_mul(v2r[:1, :], v2r[:1, :], t2s[0:1, :]),
                    after=[e_v2, e_in[8], e_i2b])
        e_dn = P.do("vector", lambda e: e.tensor_add(den[:1, :], v1r[:1, :], v2r[:1, :]),
                    after=[e_k1, e_k2])
        e_dn2 = P.do("vector", lambda e: e.tensor_scalar_max(den[:1, :], den[:1, :], EPS32),
                     after=[e_dn])
        e_rc = P.do("vector", lambda e: e.reciprocal(den[:1, :], den[:1, :]), after=[e_dn2])
        e_g1 = P.do("vector", lambda e: e.tensor_mul(g1r[:1, :], v1r[:1, :], den[:1, :]),
                    after=[e_rc])
        e_g1b = P.do("vector", lambda e: e.tensor_scalar_mul(g1r[:1, :], g1r[:1, :], 0.8),
                     after=[e_g1])
        e_g2 = P.do("vector", lambda e: e.tensor_mul(g2r[:1, :], v2r[:1, :], den[:1, :]),
                    after=[e_g1b])
        e_g2b = P.do("vector", lambda e: e.tensor_scalar_mul(g2r[:1, :], g2r[:1, :], 0.8),
                     after=[e_g2])
        def grep(rowap):
            def f(e):
                r = None
                for s in range(4):
                    r = e.matmul(psB[0:E, s * 512:(s + 1) * 512], ones_row[:1, 0:E],
                                 rowap[0:1, s * 512:(s + 1) * 512], start=True, stop=True)
                return r
            return f
        e_gr1 = P.do("tensor", grep(g1r), after=[e_g1b, e_m2])
        e_G1 = P.do("vector", lambda e: e.tensor_mul(G1[:, :], oh1s[:, :], psB[0:E, :]),
                    after=[e_gr1])
        e_gr2 = P.do("tensor", grep(g2r), after=[e_g2b, e_G1])
        e_G2 = P.do("vector", lambda e: e.tensor_mul(GG[:, :], oh2s[:, :], psB[0:E, :]),
                    after=[e_gr2])
        e_G = P.do("vector", lambda e: e.tensor_add(GG[:, :], GG[:, :], G1[:, :]), after=[e_G2])
        def gtp(e):
            r = None
            for tt in range(NTT):
                r = e.transpose(psA[:, 512 + tt * 2:512 + tt * 2 + 1],
                                g1r[0:1, tt * 128:(tt + 1) * 128], ident[0:1, 0:1])
                r = e.transpose(psA[:, 512 + tt * 2 + 1:512 + tt * 2 + 2],
                                g2r[0:1, tt * 128:(tt + 1) * 128], ident[0:1, 0:1])
            return r
        e_gt = P.do("tensor", gtp, after=[e_g2b, e_id, e_t1c])
        e_gtc = P.do("scalar", lambda e: e.copy(gtm[:, :], psA[:, 512:512 + 2 * NTT]),
                     after=[e_gt])
        if self.gather:
            def itp(e):
                r = None
                for tt in range(NTT):
                    r = e.transpose(psA[:, 576 + tt * 2:576 + tt * 2 + 1],
                                    idx1r[0:1, tt * 128:(tt + 1) * 128], ident[0:1, 0:1])
                    r = e.transpose(psA[:, 576 + tt * 2 + 1:576 + tt * 2 + 2],
                                    idx2r[0:1, tt * 128:(tt + 1) * 128], ident[0:1, 0:1])
                return r
            e_it = P.do("tensor", itp, after=[e_i1b, e_i2b, e_id, e_gt])
            e_itc = P.do("scalar", lambda e: e.copy(itm[:, :], psA[:, 576:576 + 2 * NTT]),
                         after=[e_it])
            e_ii = P.do("vector", lambda e: e.tensor_copy(ii[:, :], itm[:, :]),
                        after=[e_itc])
            e_gtc = e_ii
        e_prev = {}
        for tt in range(NTT):
            tile = slice(tt * 128, (tt + 1) * 128)
            e_p = ld(pret[:, :], pre[tile, :], after=[e_prev.get("o")])
            if self.gather:
                e_y1 = P.do("gpsimd", lambda e, tt=tt: e.indirect_dma_start(
                    y1t[:, :], None, ycat, IndirectOffsetOnAxis(ii[:, 2 * tt:2 * tt + 1], 0)),
                    after=[e_ii, e_prev.get("o")], dma=True, slot=2)
                e_y2 = P.do("gpsimd", lambda e, tt=tt: e.indirect_dma_start(
                    y2t[:, :], None, ycat, IndirectOffsetOnAxis(ii[:, 2 * tt + 1:2 * tt + 2], 0)),
                    after=[e_ii, e_prev.get("o")], dma=True, slot=3)
            else:
                e_y1 = ld(y1t[:, :], y1[tile, :], after=[e_prev.get("o")])
                e_y2 = ld(y2t[:, :], y2[tile, :], after=[e_prev.get("o")])

            def b2mm(e, tt=tt):
                r = None
                for h2 in range(2):
                    r = e.matmul(psA[:, h2 * 512:(h2 + 1) * 512],
                                 GG[0:E, tt * 128:(tt + 1) * 128],
                                 b2s[:, h2 * 512:(h2 + 1) * 512], start=True, stop=True)
                return r
            e_bm = P.do("tensor", b2mm, after=[e_G, e_in[5], e_gtc, e_prev.get("a3")])
            e_a1 = P.do("vector", lambda e, tt=tt: e.scalar_tensor_tensor(
                ot[:, :], y1t[:, :], gtm[:, tt * 2:tt * 2 + 1], pret[:, :],
                ALU.mult, ALU.add), after=[e_p, e_y1, e_gtc, e_prev.get("o")])
            e_a2 = P.do("vector", lambda e, tt=tt: e.scalar_tensor_tensor(
                ot[:, :], y2t[:, :], gtm[:, tt * 2 + 1:tt * 2 + 2], ot[:, :],
                ALU.mult, ALU.add), after=[e_a1, e_y2])
            e_a3 = P.do("vector", lambda e: e.tensor_add(ot[:, :], ot[:, :], psA[:, 0:1024]),
                        after=[e_a2, e_bm])
            e_o = P.dma("gpsimd", outp[tile, :], ot[:, :], after=[e_a3])
            self.outs.append(e_o)
            e_prev = {"o": e_o, "a3": e_a3}
        P.do("sync", lambda e: e.nop(), after=self.outs)
        P.finalize()


_CACHE = {}


def _get_progs():
    if "p" not in _CACHE:
        _CACHE["p"] = (L1(upto=6, dbg=(), moe_mode="expert"), L2E(), L2(gather=True))
    return _CACHE["p"]


def kernel(**inputs):
    b1p, b2e, b3p = _get_progs()
    KEYMAP = {
        "lng": "L0_attn_ln_g", "lnb": "L0_attn_ln_b", "w_in": "L0_attn_W_in",
        "conv_w": "L0_attn_conv_w", "conv_b": "L0_attn_conv_b", "w_x": "L0_attn_W_x",
        "w_dt": "L0_attn_W_dt", "b_dt": "L0_attn_b_dt", "a_log": "L0_attn_A_log",
        "dvec": "L0_attn_D", "w_out": "L0_attn_W_out",
        "wq": "L0_ffn_wq", "wk": "L0_ffn_wk", "wv": "L0_ffn_wv", "wo": "L0_ffn_wo",
        "bq": "L0_ffn_bq", "bk": "L0_ffn_bk", "bv": "L0_ffn_bv", "bo": "L0_ffn_bo",
        "l1g": "L0_ffn_ln1_g", "l1b": "L0_ffn_ln1_b",
        "l2g": "L0_ffn_ln2_g", "l2b": "L0_ffn_ln2_b",
        "wr": "L0_ffn_wr", "w1": "L0_ffn_w1", "b1": "L0_ffn_b1", "w2": "L0_ffn_w2",
    }
    base = {k: np.ascontiguousarray(np.asarray(inputs[v], np.float32))
            for k, v in KEYMAP.items()}
    x = np.asarray(inputs["L0_attn_x"], np.float32)
    in_maps1 = []
    for c in range(8):
        m = dict(base)
        m["xw"] = np.ascontiguousarray(x[c])
        in_maps1.append(m)
    r1 = run_bass_kernel_spmd(b1p.nc, in_maps1, core_ids=list(range(8)))
    R1 = r1.results
    cnts = np.stack([np.asarray(r["cnto"])[0] for r in R1])
    h2cat = np.ascontiguousarray(
        np.concatenate([np.asarray(r["h2tm_d"]) for r in R1], axis=0))
    d1a = np.ascontiguousarray(np.stack([np.asarray(R1[k]["rowso"])[0] for k in range(8)]))
    d2a = np.ascontiguousarray(np.stack([np.asarray(R1[k]["rowso"])[1] for k in range(8)]))
    in2 = []
    for c in range(8):
        in2.append({
            "w1e": np.ascontiguousarray(base["w1"][c]),
            "b1e": np.ascontiguousarray(base["b1"][c][None, :]),
            "w2e": np.ascontiguousarray(base["w2"][c]),
            "h2cat": h2cat,
            "oh1r": np.ascontiguousarray(np.stack([np.asarray(R1[k]["oh1o"])[c]
                                                   for k in range(8)])),
            "oh2r": np.ascontiguousarray(np.stack([np.asarray(R1[k]["oh2o"])[c]
                                                   for k in range(8)])),
            "d1a": d1a, "d2a": d2a,
            "cnt2": np.ascontiguousarray(cnts[:, [c, 8 + c]].astype(np.float32)),
        })
    r2 = run_bass_kernel_spmd(b2e.nc, in2, core_ids=list(range(8)))
    ycat = np.ascontiguousarray(
        np.concatenate([np.asarray(r["yeo"]) for r in r2.results], axis=0))
    b2w = np.ascontiguousarray(np.asarray(inputs["L0_ffn_b2"], np.float32))
    in3 = []
    for c in range(8):
        r = R1[c]
        in3.append({
            "pre": np.asarray(r["preout"]), "ycat": ycat,
            "rows": np.asarray(r["rowso"]),
            "oh1": np.asarray(r["oh1o"]), "oh2": np.asarray(r["oh2o"]),
            "cnts": cnts, "rid": np.full((1, 1), float(c), np.float32),
            "b2": b2w,
        })
    r3 = run_bass_kernel_spmd(b3p.nc, in3, core_ids=list(range(8)))
    return np.stack([np.asarray(r["outp"]) for r in r3.results])
